# version 29
# baseline (speedup 1.0000x reference)
import numpy as np
import ml_dtypes

import concourse.bass as bass
import concourse.mybir as mybir
from concourse.bass_utils import run_bass_kernel_spmd

F32 = mybir.dt.float32
BF16 = mybir.dt.bfloat16
AF = mybir.ActivationFunctionType
ALU = mybir.AluOpType
AX = mybir.AxisListType

EPS = 1e-5

ENGS = ("sync", "scalar", "vector", "gpsimd", "tensor")
STREAMS = ("con", "cast0", "cast1", "cast2", "cast3", "cast4", "cast5",
           "slab0", "slab1", "xch", "x32", "pin0", "pin1",
           "qw0", "qw1", "qw2", "pin2", "xrst", "xqa", "xqb",
           "y0", "y1", "y2", "y3")
SEMS = ("sV", "sS", "sG", "sP") + STREAMS
SELF = {"vector": "sV", "scalar": "sS", "gpsimd": "sG"}


def tmax(*xs):
    best = {}
    stack = list(xs)
    while stack:
        t = stack.pop()
        if t is None or t == ():
            continue
        if isinstance(t, list):
            stack.extend(t)
            continue
        s, v = t
        if v is not None and v > best.get(s, -1):
            best[s] = v
    return [(s, v) for s, v in best.items()]


class Sched:
    def __init__(self):
        self.q = {e: [] for e in ENGS}
        self.cnt = {s: 0 for s in SEMS}
        self.floor = {e: {s: 0 for s in SEMS} for e in ENGS}

    def _emit(self, eng, fn, deps, inc, by):
        waits = []
        fl = self.floor[eng]
        for s, v in tmax(deps):
            if v > fl[s]:
                waits.append((s, v))
                fl[s] = v
        tok = None
        if inc is not None:
            self.cnt[inc] += by
            tok = (inc, self.cnt[inc])
        self.q[eng].append((waits, fn, inc, by))
        return tok

    def op(self, eng, fn, deps=()):
        s = SELF[eng]
        return self._emit(eng, fn, tmax(deps, (s, self.cnt[s])), s, 1)

    def pe(self, fn, deps=(), mark=False):
        return self._emit("tensor", fn, deps, "sP" if mark else None, 1)

    def pemark(self):
        return ("sP", self.cnt["sP"])

    def dma(self, eng, stream, fn, deps=()):
        return self._emit(eng, fn, deps, stream, 16)

    def wait(self, eng, deps):
        self._emit(eng, None, deps, None, 0)

    def run(self, name, sems, e):
        for waits, fn, inc, by in self.q[name]:
            for s, v in waits:
                e.wait_ge(sems[s], v)
            if fn is None:
                continue
            inst = fn(e)
            if inc is not None:
                inst.then_inc(sems[inc], by)


def build(D=2048, T=4096, F=8192, TC=512, TKV=1024, TFF=1024, GF=4,
          debug_stop=None, no_prep=False, full_upto=None, wo_level=4):
    NT = D // 128
    NF = F // 128
    M = T // 2
    n_kv = M // TKV
    n_full = M // TC
    n_out = M // TFF
    NFG = NF // GF
    NH = TFF // 512
    NQ = TFF // 256

    nc = bass.Bass("TRN2", target_bir_lowering=False)

    xw = nc.dram_tensor("xw", [D, T + 1], F32, kind="ExternalInput").ap()
    gate_in = nc.dram_tensor("gate", [128, 1], F32, kind="ExternalInput").ap()
    wkv_in = nc.dram_tensor("wkv", [NT, D, 256], F32,
                            kind="ExternalInput").ap()
    wrg_in = nc.dram_tensor("wrg", [NT, D, 256], F32,
                            kind="ExternalInput").ap()
    wo_in = nc.dram_tensor("wo", [D, D], F32, kind="ExternalInput").ap()
    w1t = nc.dram_tensor("w1t", [D, F], F32, kind="ExternalInput").ap()
    w2t = nc.dram_tensor("w2t", [F, D], F32, kind="ExternalInput").ap()
    VN = ("tmk", "tmv", "tmr", "tmg", "dec", "tf", "lng", "lnb", "b2")
    vec_in = {n: nc.dram_tensor(n, [128, NT], F32, kind="ExternalInput").ap()
              for n in VN}
    b1_in = nc.dram_tensor("b1", [128, NF], F32, kind="ExternalInput").ap()
    lngrow_in = nc.dram_tensor("lngrow", [1, D], F32,
                               kind="ExternalInput").ap()
    y_out = nc.dram_tensor("y", [D, M], F32, kind="ExternalOutput").ap()

    qkv = nc.dram_tensor("qkv", [NT, D, 256], BF16).ap()
    qrg = nc.dram_tensor("qrg", [NT, D, 256], BF16).ap()
    qwo = nc.dram_tensor("qwo", [D, D], BF16).ap()
    wq1 = nc.dram_tensor("wq1", [D, F], BF16).ap()
    wq2 = nc.dram_tensor("wq2", [NT, F, 128], BF16).ap()
    xres_d = nc.dram_tensor("xres", [D, M], F32).ap()
    xwbf_d = nc.dram_tensor("xwbf", [D, T + 1], BF16).ap()

    sch = Sched()
    from contextlib import ExitStack
    with ExitStack() as ctx:
        e = ctx.enter_context

        A_XBF = e(nc.sbuf_tensor("A_XBF", [128, NT * (TC + 1) // 2 + 8],
                                 F32))
        A_MIX = e(nc.sbuf_tensor("A_MIX", [128, NT * TC], F32))
        A_WKV = e(nc.sbuf_tensor("A_WKV", [128, NT * TC], F32))
        A_O32 = e(nc.sbuf_tensor("A_O32", [128, NT * TC + NT // 2], F32))
        A_AO = e(nc.sbuf_tensor("A_AO", [128, NT * TC // 2], F32))
        A_SCR = e(nc.sbuf_tensor("A_SCR", [128, 16 * TC], F32))
        A_SLAB = e(nc.sbuf_tensor("A_SLAB", [128, NT * 256], F32))
        sqq_extra = 512 if NT * 64 < 512 else 0
        A_PREP = e(nc.sbuf_tensor("A_PREP", [128, 2 * NT * 64 + NT * 96
                                             + sqq_extra], F32))
        A_CON = e(nc.sbuf_tensor("A_CON", [128, 16 * NT + NF + 210], F32))
        CARRY = e(nc.sbuf_tensor("CARRY", [128, 2 * NT], F32))
        LNROW = e(nc.sbuf_tensor("LNROW", [1, 5 * TC], F32))
        PS = [e(nc.psum_tensor(f"PS{i}", [128, 512], F32)) for i in range(8)]
        sems = {s: e(nc.semaphore(s)) for s in SEMS}
        block = e(nc.Block())

        con = {}
        off = 0
        for n in VN + ("w", "eu"):
            con[n] = A_CON[:, off:off + NT]
            off += NT
        b1s = A_CON[:, off:off + NF]
        off += NF
        ones_bf = A_CON[:, off:off + 1].bitcast(BF16)[:, 0:1]
        off += 1
        gcol = A_CON[:, off:off + 1]
        off += 1
        thrs = A_CON[:, off:off + 4]
        off += 4
        racc = A_CON[:, off:off + 4]
        off += 4
        ones_f = A_CON[:, off:off + 1]
        off += 1
        ones_b = A_CON[0:1, off:off + 128]
        off += 128
        ones_bb = A_CON[0:1, off:off + 64].bitcast(BF16)[0:1, 0:128]
        off += 64

        car_a = CARRY[:, 0:NT]
        car_b = CARRY[:, NT:2 * NT]

        sY = TC + 1
        xbf = A_XBF[:].bitcast(BF16)
        mixA = A_MIX[:, 0:NT * TC // 2].bitcast(BF16)
        mixB = A_MIX[:, NT * TC // 2:NT * TC].bitcast(BF16)
        x32 = A_MIX[:, 0:NT * TC]
        wkvr = A_WKV[:, 0:NT * TC]
        resf = A_WKV[:, 0:NT * TC]
        o32 = A_O32[:, 0:NT * TC]
        aobf = A_AO[:].bitcast(BF16)
        resbf = A_AO[:].bitcast(BF16)
        slabs = [A_SLAB[:, 0:NT * 128].bitcast(BF16),
                 A_SLAB[:, NT * 128:NT * 256].bitcast(BF16)]
        pin = [A_PREP[:, 0:NT * 64], A_PREP[:, NT * 64:2 * NT * 64],
               A_SCR[:, 14 * TC:14 * TC + NT * 64]]
        pouts = [A_PREP[:, 2 * NT * 64 + j * NT * 32:
                        2 * NT * 64 + (j + 1) * NT * 32].bitcast(BF16)
                 for j in range(3)]

        sK = TKV + 1
        xbf_kv = A_O32[:].bitcast(BF16)
        mixKA = A_MIX[:].bitcast(BF16)
        mixKB = A_WKV[:].bitcast(BF16)
        xx_kv = A_AO[:].bitcast(BF16)

        def scr(i, w=TC):
            return A_SCR[:, i * TC:i * TC + w]

        def scrk(i, w=TKV):
            return A_SCR[:, i * TKV:i * TKV + w]

        row_mu = LNROW[0:1, 0:TC]
        row_rs = LNROW[0:1, TC:2 * TC]
        row_mr = LNROW[0:1, 2 * TC:3 * TC]
        row_e2 = row_mr

        lts = []
        for n in VN:
            lts.append(sch.dma("sync", "con",
                               lambda e_, d=con[n], s_=vec_in[n]:
                               e_.dma_start(out=d, in_=s_)))
        lts.append(sch.dma("sync", "con",
                           lambda e_: e_.dma_start(out=b1s, in_=b1_in)))
        lts.append(sch.dma("sync", "con", lambda e_: e_.dma_start(
            out=gcol, in_=gate_in)))
        tok_con = tmax(*lts)

        t_ = sch.op("scalar", lambda e_: e_.activation(con["w"], con["dec"],
                                                       AF.Exp), tok_con)
        tok_w = sch.op("scalar", lambda e_: e_.activation(con["w"], con["w"],
                                                          AF.Exp,
                                                          scale=-1.0), t_)
        tok_eu = sch.op("scalar", lambda e_: e_.activation(
            con["eu"], con["tf"], AF.Exp), tok_con)
        tok_ones = sch.op("vector", lambda e_: e_.memset(ones_bf, 1.0))
        tok_onesb = sch.op("vector", lambda e_: e_.memset(ones_b, 1.0))
        tok_onesbb = sch.op("vector", lambda e_: e_.memset(ones_bb, 1.0))
        tok_onesf = sch.op("vector", lambda e_: e_.memset(ones_f, 1.0))
        tok_car0 = sch.op("vector", lambda e_: e_.memset(CARRY[:], 0.0))

        tok_qkv = [None] * NT
        tok_qrg = [None] * NT
        tok_qwo = [None] * (D // 256)

        cast_chain = [None] * 6
        cast_rr = [0]
        tok_xwbf = {}

        def cast_dma(fn):
            j = cast_rr[0] % 6
            cast_rr[0] += 1
            tok = sch.dma("gpsimd", f"cast{j}", fn, cast_chain[j])
            cast_chain[j] = tok
            return tok

        def emit_cast_x(pos0, w_):
            a0 = 0 if pos0 == 0 else pos0 + 1
            tok_xwbf[pos0] = cast_dma(
                lambda e_, a=a0, b_=pos0 + w_ + 1:
                e_.dma_start(out=xwbf_d[:, a:b_], in_=xw[:, a:b_]))

        def emit_casts_kv():
            for ct in range(NT):
                tok_qkv[ct] = cast_dma(lambda e_, c=ct: e_.dma_start(
                    out=qkv[c], in_=wkv_in[c]))

        def emit_casts_rg():
            for ct in range(NT):
                tok_qrg[ct] = cast_dma(lambda e_, c=ct: e_.dma_start(
                    out=qrg[c], in_=wrg_in[c]))

        def emit_casts_wo():
            for j in range(D // 256):
                tok_qwo[j] = cast_dma(
                    lambda e_, a=j * 256, b_=(j + 1) * 256:
                    e_.dma_start(out=qwo[:, a:b_], in_=wo_in[:, a:b_]))

        n_p1 = F // 64
        n_prep = n_p1 + 2 * NF
        DH = D // 2
        pb_free = [tmax(), tmax(), tmax()]
        st = {"thr": None, "ps4": tmax(), "ps5": tmax(),
              "bc": [tmax(), tmax()], "q1": [], "q2": [], "rows": tmax(),
              "red": None, "lhl": [tmax(), tmax()]}

        def bc_pair(bank, row, dep_row, dep_bank, slot):
            hlv = LNROW[0:1, (3 + slot) * TC:(4 + slot) * TC].bitcast(BF16)
            hi = hlv[0:1, 0:TC]
            lo = hlv[0:1, TC:2 * TC]
            t_h = sch.op("scalar", lambda e_, o=hi, a=row:
                         e_.copy(o, a),
                         tmax(dep_row, st["lhl"][slot]))
            t_b = sch.op("scalar", lambda e_, o=row_mu, a=hi:
                         e_.copy(o, a), tmax(t_h, st["lhl"][slot ^ 1]))
            t_l = sch.op("vector", lambda e_, o=lo, a=row:
                         e_.tensor_tensor(o, a, row_mu, ALU.subtract), t_b)
            sch.pe(lambda e_, p=PS[bank], r_=hi:
                   e_.matmul(p[:], ones_bb, r_, start=True, stop=False),
                   tmax(t_h, tok_onesbb, dep_bank))
            sch.pe(lambda e_, p=PS[bank], r_=lo:
                   e_.matmul(p[:], ones_bb, r_, start=False, stop=True),
                   tmax(t_l), mark=True)
            tok = sch.pemark()
            st["lhl"][slot] = tok
            return tok

        def prep_views(i, dst):
            if i < n_p1:
                return (dst.rearrange("p (c f) -> p c f", f=64),
                        w1t[:, i * 64:(i + 1) * 64].rearrange(
                            "(c p) f -> p c f", p=128))
            j, h = divmod(i - n_p1, 2)
            return (dst.rearrange("p (c f) -> p c f", f=128),
                    w2t[j * 128:(j + 1) * 128, h * DH:(h + 1) * DH]
                    .rearrange("p (c f) -> p c f", f=128))

        def emit_thr(tok_red):
            tok_rmm = sch.pe(lambda e_: e_.matmul(
                PS[4][0:1, 0:2], ones_f, racc[:, 0:2], start=True,
                stop=True), tmax(tok_red, tok_onesf, st["ps4"]), mark=True)
            trow = LNROW[0:1, 0:2]
            tok_tr = sch.op("vector", lambda e_: e_.tensor_scalar(
                trow, PS[4][0:1, 0:2], 0.7 / (D * F), None, ALU.mult),
                tok_rmm)
            tok_bc = sch.pe(lambda e_: e_.matmul(
                PS[4][:, 2:4], ones_b, trow, start=True, stop=True),
                tmax(tok_tr, tok_onesb), mark=True)
            tok_thr = sch.op("vector", lambda e_: e_.tensor_copy(
                thrs[:, 0:2], PS[4][:, 2:4]), tmax(tok_bc, tok_con))
            tok_nthr = sch.op("vector", lambda e_: e_.tensor_scalar(
                thrs[:, 2:4], thrs[:, 0:2], -1.0, None, ALU.mult), tok_thr)
            st["thr"] = tok_nthr
            st["ps4"] = tmax(tok_thr)
            st["rows"] = tmax(tok_tr)

        UNITS = [("r", i) for i in range(n_prep)] + [("t", 0)]
        QOF = len(UNITS)
        UNITS = UNITS + [("q", i) for i in range(n_prep)]
        NPU = len(UNITS)
        p_load = {}
        p_done = {}
        cur = {"l": 0, "c": 0}

        def pin_of(s):
            kind, i = UNITS[s]
            if kind == "q" and i >= n_p1:
                return (s - QOF) % 2
            return s % 3

        def emit_load(s):
            kind, i = UNITS[s]
            if kind == "t":
                p_load[s] = tmax()
                return
            b = pin_of(s)
            dv, sv = prep_views(i, pin[b])
            if kind == "q":
                dep = tmax(p_done.get(s - 2), p_done.get(s - 3),
                           p_done.get(s - 4),
                           p_done.get(s - 1)
                           if (s >= 1 and UNITS[s - 1][0] != "t"
                               and pin_of(s - 1) == pin_of(s)) else None)
            else:
                dep = tmax(p_done.get(s - 3))
            p_load[s] = sch.dma("gpsimd", f"pin{b}",
                                lambda e_, d=dv, s_=sv:
                                e_.dma_start(out=d, in_=s_), dep)

        def emit_compute(s):
            kind, i = UNITS[s]
            b = pin_of(s)
            if kind == "t":
                emit_thr(st["red"])
                p_done[s] = tmax()
                return
            if kind == "r":
                m = 0 if i < n_p1 else 1
                rdep = tmax(p_load[s], st.get("racc"))
                if m == 0:
                    tok_r = sch.op("scalar", lambda e_, s_=pin[b]:
                                   e_.activation(s_, s_, AF.Abs,
                                                 accum_out=racc[:, 2:3]),
                                   rdep)
                else:
                    tok_r = sch.op("vector", lambda e_, s_=pin[b]:
                                   e_.tensor_reduce(
                                       racc[:, 2:3], s_, AX.X, ALU.add,
                                       apply_absolute_value=True), rdep)
                if i == 0 or i == n_p1:
                    tok_a = sch.op("vector", lambda e_, m_=m:
                                   e_.tensor_copy(racc[:, m_:m_ + 1],
                                                  racc[:, 2:3]), tok_r)
                else:
                    tok_a = sch.op("vector", lambda e_, m_=m:
                                   e_.tensor_tensor(
                                       racc[:, m_:m_ + 1],
                                       racc[:, m_:m_ + 1], racc[:, 2:3],
                                       ALU.add), tok_r)
                st["racc"] = tok_a
                st["red"] = tok_a
                p_done[s] = tmax(tok_r)
                return
            m = 0 if i < n_p1 else 1
            po = pouts[i % 3]
            tok_m = sch.op("vector", lambda e_, s_=pin[b], m_=m, o=po:
                           e_.tensor_scalar(o, s_,
                                            thrs[:, 2 + m_:3 + m_], None,
                                            ALU.is_le),
                           tmax(p_load[s], st["thr"],
                                st.get(("pout", i % 3))))
            tok_q = sch.op("vector", lambda e_, s_=pin[b], m_=m, o=po:
                           e_.scalar_tensor_tensor(
                               o, s_, thrs[:, 0 + m_:1 + m_], o,
                               ALU.is_ge, ALU.subtract), tok_m)
            p_done[s] = tmax(tok_q)
            if i < n_p1:
                qdst = wq1[:, i * 64:(i + 1) * 64].rearrange(
                    "(c p) f -> p c f", p=128)
                psrc = po.rearrange("p (c f) -> p c f", f=64)
            else:
                j, h = divmod(i - n_p1, 2)
                nct = DH // 128
                qdst = wq2[h * nct:(h + 1) * nct,
                           j * 128:(j + 1) * 128, :].rearrange(
                    "c p n -> p c n")
                psrc = po.rearrange("p (c f) -> p c f", f=128)
            tok_wq = sch.dma("gpsimd", f"qw{i % 3}",
                             lambda e_, d=qdst, s_=psrc:
                             e_.dma_start(out=d, in_=s_), tok_q)
            st[("pout", i % 3)] = tok_wq
            (st["q1"] if i < n_p1 else st["q2"]).append(tok_wq)

        n_ticks = (n_kv + 2 * n_full) * NT + n_full * (NT // 2 + NT)
        ATTN_LIM = QOF + n_p1
        per_tick = -(-(ATTN_LIM + 3) // max(1, n_ticks - 2 * NT))

        def prep_tick(k=None, limit=None, thr_ok=True):
            if no_prep:
                return
            lim = min(NPU, ATTN_LIM if limit is None else limit)
            k = per_tick if k is None else k
            for _ in range(k):
                ahead = 2 if cur["c"] >= QOF + n_p1 else 3
                while cur["l"] < min(lim, cur["c"] + ahead):
                    emit_load(cur["l"])
                    cur["l"] += 1
                if cur["c"] < lim:
                    if not thr_ok and UNITS[cur["c"]][0] == "t":
                        return
                    emit_compute(cur["c"])
                    cur["c"] += 1

        def q_run_to(n_q):
            lim = min(NPU, QOF + n_q)
            prep_tick(k=max(0, lim - cur["c"]), limit=lim)

        def q_front():
            return tmax(st.get(("pout", 0)), st.get(("pout", 1)),
                        st.get(("pout", 2)))

        bank_free = [tmax(), tmax(), tmax(), tmax()]
        bank_i = [0]

        def next_bank():
            b = bank_i[0] % 4
            bank_i[0] += 1
            return b

        slab_free = [tmax(), tmax()]
        slab_i = [0]

        def next_slab():
            b = slab_i[0] % 2
            slab_i[0] += 1
            return b

        def proj_group(lhs_fn, rhs_fn, bank, dep0, cj_dep=None):
            for cj in range(NT):
                d = dep0 if cj == 0 else ()
                if cj_dep is not None:
                    d = tmax(d, cj_dep(cj))
                sch.pe(lambda e_, p=PS[bank], l_=lhs_fn(cj), r_=rhs_fn(cj),
                       s0=(cj == 0), s1=(cj == NT - 1):
                       e_.matmul(p[:], l_, r_, start=s0, stop=s1),
                       d, mark=(cj == NT - 1))
            return sch.pemark()

        scrf = {}
        tok_carry = tmax(tok_car0)
        xbf_free = tmax()
        mix_free = tmax()
        wkv_free = tmax()
        o32_free = tmax()
        ao_free = tmax()
        xres_st = []

        for ci in range(n_kv):
            pos0 = ci * TKV

            xsrc = xw[:, pos0:pos0 + TKV + 1].rearrange("(c p) t -> p c t",
                                                        p=128)
            xdst = xbf_kv[:, 0:NT * sK].rearrange("p (c t) -> p c t", t=sK)
            if ci == 0:
                emit_cast_x(0, TKV)
                emit_casts_kv()
                for cc in range(1, n_kv):
                    emit_cast_x(cc * TKV, TKV)
                for cc in range(n_full):
                    emit_cast_x(n_kv * TKV + cc * TC, TC)
            if ci == max(0, n_kv - 2) and ci > 0:
                emit_casts_rg()
            xsrcb = xwbf_d[:, pos0:pos0 + TKV + 1].rearrange(
                "(c p) t -> p c t", p=128)
            tok_x = sch.dma("sync", "xch", lambda e_, d=xdst, s_=xsrcb:
                            e_.dma_start(out=d, in_=s_),
                            tmax(o32_free, st.get("xch"), tok_xwbf[pos0]))
            st["xch"] = tok_x

            mk_toks, mv_toks, xx_toks = [], [], []
            for cj in range(NT):
                xxs = xx_kv[:, (cj % 2) * TKV:(cj % 2) * TKV + TKV]
                prev_v = xbf_kv[:, cj * sK:cj * sK + TKV]
                x_v = xbf_kv[:, cj * sK + 1:cj * sK + 1 + TKV]
                t_xx = sch.op("vector", lambda e_, o=xxs, a=prev_v, b_=x_v:
                              e_.tensor_tensor(o, a, b_, ALU.subtract),
                              tmax(tok_x, scrf.get(("xxk", cj % 2)),
                                   ao_free if cj < 2 else ()))
                mk = mixKA[:, cj * TKV:(cj + 1) * TKV]
                mv = mixKB[:, cj * TKV:(cj + 1) * TKV]
                t_mk = sch.op("vector", lambda e_, o=mk, a=xxs, c=cj, b_=x_v:
                              e_.scalar_tensor_tensor(
                                  o, a, con["tmk"][:, c:c + 1], b_,
                                  ALU.mult, ALU.add),
                              tmax(t_xx, tok_con,
                                   mix_free if cj == 0 else ()))
                t_mv = sch.op("vector", lambda e_, o=mv, a=xxs, c=cj, b_=x_v:
                              e_.scalar_tensor_tensor(
                                  o, a, con["tmv"][:, c:c + 1], b_,
                                  ALU.mult, ALU.add),
                              tmax(t_xx, tok_con,
                                   wkv_free if cj == 0 else ()))
                scrf[("xxk", cj % 2)] = tmax(t_mk, t_mv)
                mk_toks.append(t_mk)
                mv_toks.append(t_mv)
                xx_toks.append(t_xx)
            tok_mixk = tmax(*mk_toks)
            tok_mixv = tmax(*mv_toks)
            mix_use = []
            carry_next = []

            for ct in range(NT):
                prep_tick(2, thr_ok=False)
                sb = next_slab()
                sdst = slabs[sb].rearrange("p (c n) -> p c n", n=256)
                tok_sl = sch.dma("sync" if sb == 0 else "scalar", f"slab{sb}",
                                 lambda e_, d=sdst,
                                 s_=qkv[ct].rearrange("(c p) n -> p c n",
                                                      p=128):
                                 e_.dma_start(out=d, in_=s_),
                                 tmax(slab_free[sb], tok_qkv[ct]))
                slab = slabs[sb]
                ek = scrk(0 + ct % 2)
                z = scrk(2 + ct % 2)
                sa = scrk(4)
                sb_ = scrk(5)
                ek_toks, z_toks = [], []
                for hf in range(TKV // 512):
                    bk = next_bank()
                    g1 = proj_group(
                        lambda cj: slab[:, cj * 256:cj * 256 + 128],
                        lambda cj, h=hf: mixKA[:, cj * TKV + h * 512:
                                               cj * TKV + h * 512 + 512],
                        bk, tmax(tok_sl, bank_free[bk]),
                        cj_dep=lambda cj: mk_toks[cj])
                    t_ek = sch.op("scalar",
                                  lambda e_, o=ek, h=hf, p=PS[bk]:
                                  e_.activation(o[:, h * 512:h * 512 + 512],
                                                p[:], AF.Exp),
                                  tmax(g1, scrf.get(("ekk", ct % 2))
                                       if hf == 0 else ()))
                    bank_free[bk] = tmax(t_ek)
                    ek_toks.append(t_ek)
                    bv = next_bank()
                    g2 = proj_group(
                        lambda cj: slab[:, cj * 256 + 128:cj * 256 + 256],
                        lambda cj, h=hf: mixKB[:, cj * TKV + h * 512:
                                               cj * TKV + h * 512 + 512],
                        bv, tmax(tok_sl, bank_free[bv]),
                        cj_dep=lambda cj: mv_toks[cj])
                    t_zc = sch.op("scalar",
                                  lambda e_, o=z, h=hf, p=PS[bv]:
                                  e_.copy(o[:, h * 512:h * 512 + 512],
                                          p[:]),
                                  tmax(g2, scrf.get(("zk", ct % 2))
                                       if hf == 0 else ()))
                    bank_free[bv] = tmax(t_zc)
                    t_z = sch.op("vector",
                                 lambda e_, o=z, h=hf, a=ek:
                                 e_.tensor_tensor(
                                     o[:, h * 512:h * 512 + 512],
                                     o[:, h * 512:h * 512 + 512],
                                     a[:, h * 512:h * 512 + 512],
                                     ALU.mult),
                                 tmax(t_zc, ek_toks[hf]))
                    z_toks.append(t_z)
                    mix_use.append(tmax(g1, g2))
                slab_free[sb] = mix_use[-1]
                tok_sa = sch.op("vector",
                                lambda e_, o=sa, c=ct, d1=z:
                                e_.tensor_tensor_scan(
                                    o, con["w"][:, c:c + 1].broadcast_to(
                                        [128, TKV]), d1, car_a[:, c:c + 1],
                                    ALU.mult, ALU.add),
                                tmax(z_toks, tok_w, tok_carry,
                                     scrf.get("sak")))
                tok_sb = sch.op("vector",
                                lambda e_, o=sb_, c=ct, d1=ek:
                                e_.tensor_tensor_scan(
                                    o, con["w"][:, c:c + 1].broadcast_to(
                                        [128, TKV]), d1, car_b[:, c:c + 1],
                                    ALU.mult, ALU.add),
                                tmax(ek_toks, tok_w, tok_carry,
                                     scrf.get("sbk")))
                gated = False
                t_ca = sch.op("vector", lambda e_, c=ct, s_=sa:
                              e_.tensor_copy(car_a[:, c:c + 1],
                                             s_[:, TKV - 1:TKV]), tok_sa)
                t_cb = sch.op("vector", lambda e_, c=ct, s_=sb_:
                              e_.tensor_copy(car_b[:, c:c + 1],
                                             s_[:, TKV - 1:TKV]), tok_sb)
                carry_next.append(tmax(t_ca, t_cb))
                scrf["sak"] = tmax(t_ca)
                scrf["sbk"] = tmax(t_cb)
                scrf[("ekk", ct % 2)] = tmax(tok_sb, *z_toks)
                scrf[("zk", ct % 2)] = tmax(tok_sa)
            tok_carry = tmax(*carry_next)
            if ci == n_kv - 1:
                t_ga = sch.op("vector", lambda e_: e_.tensor_scalar(
                    car_a[:], car_a[:], gcol, None, ALU.mult),
                    tmax(tok_carry, tok_con))
                t_gb = sch.op("vector", lambda e_: e_.tensor_scalar(
                    car_b[:], car_b[:], gcol, None, ALU.mult),
                    tmax(tok_carry, tok_con))
                tok_carry = tmax(t_ga, t_gb)
            mix_free = tmax(*mix_use)
            wkv_free = tmax(*mix_use)
            o32_free = tmax(*(xx_toks + mk_toks + mv_toks))
            ao_free = tmax(*(mk_toks + mv_toks))

        if debug_stop == "kv":
            t_dbg = sch.dma("sync", "y", lambda e_: e_.dma_start(
                out=y_out[:, 0:M], in_=xw[:, 0:M]),
                tmax(tok_con, tok_carry))
            sch.wait("sync", t_dbg)
            emit_all(block, sch, sems)
            return nc

        for fci in range(n_full):
            pos0 = n_kv * TKV + fci * TC

            xsrc = xw[:, pos0:pos0 + TC + 1].rearrange("(c p) t -> p c t",
                                                       p=128)
            xdst = xbf[:, 0:NT * sY].rearrange("p (c t) -> p c t", t=sY)
            if tok_qrg[0] is None:
                emit_casts_rg()
            if fci == 0:
                emit_casts_wo()
            xsrcb = xwbf_d[:, pos0:pos0 + TC + 1].rearrange(
                "(c p) t -> p c t", p=128)
            tok_x = sch.dma("sync", "xch", lambda e_, d=xdst, s_=xsrcb:
                            e_.dma_start(out=d, in_=s_),
                            tmax(xbf_free, st.get("xch"), tok_xwbf[pos0]))
            st["xch"] = tok_x

            mk_toks, mv_toks, xx_toks = [], [], []
            for cj in range(NT):
                xxs = scr(10).bitcast(BF16)[:, (cj % 2) * TC:
                                            (cj % 2) * TC + TC]
                prev_v = xbf[:, cj * sY:cj * sY + TC]
                x_v = xbf[:, cj * sY + 1:cj * sY + 1 + TC]
                t_xx = sch.op("vector", lambda e_, o=xxs, a=prev_v, b_=x_v:
                              e_.tensor_tensor(o, a, b_, ALU.subtract),
                              tmax(tok_x, scrf.get(("xx", cj % 2)),
                                   scrf.get(("osq", cj % 2))))
                mk = mixA[:, cj * TC:(cj + 1) * TC]
                mv = mixB[:, cj * TC:(cj + 1) * TC]
                t_mk = sch.op("vector", lambda e_, o=mk, a=xxs, c=cj, b_=x_v:
                              e_.scalar_tensor_tensor(
                                  o, a, con["tmk"][:, c:c + 1], b_,
                                  ALU.mult, ALU.add),
                              tmax(t_xx, tok_con,
                                   mix_free if cj == 0 else ()))
                t_mv = sch.op("vector", lambda e_, o=mv, a=xxs, c=cj, b_=x_v:
                              e_.scalar_tensor_tensor(
                                  o, a, con["tmv"][:, c:c + 1], b_,
                                  ALU.mult, ALU.add),
                              tmax(t_xx, tok_con,
                                   mix_free if cj == 0 else ()))
                scrf[("xx", cj % 2)] = tmax(t_mk, t_mv)
                mk_toks.append(t_mk)
                mv_toks.append(t_mv)
                xx_toks.append(t_xx)
            tok_mixk = tmax(*mk_toks)
            tok_mixv = tmax(*mv_toks)

            wkvr_toks = [None] * NT
            kv_use = []
            carry_next = []
            for ct in range(NT):
                prep_tick(2)
                sb = next_slab()
                sdst = slabs[sb].rearrange("p (c n) -> p c n", n=256)
                tok_sl = sch.dma("sync" if sb == 0 else "scalar", f"slab{sb}",
                                 lambda e_, d=sdst,
                                 s_=qkv[ct].rearrange("(c p) n -> p c n",
                                                      p=128):
                                 e_.dma_start(out=d, in_=s_),
                                 tmax(slab_free[sb], tok_qkv[ct]))
                slab = slabs[sb]

                bk = next_bank()
                gk = proj_group(lambda cj: slab[:, cj * 256:cj * 256 + 128],
                                lambda cj: mixA[:, cj * TC:(cj + 1) * TC],
                                bk, tmax(tok_sl, bank_free[bk]),
                                cj_dep=lambda cj: mk_toks[cj])
                ek = scr(0 + ct % 2)
                tok_ek = sch.op("scalar", lambda e_, o=ek, p=PS[bk]:
                                e_.activation(o, p[:], AF.Exp),
                                tmax(gk, scrf.get(("ek", ct % 2))))
                bank_free[bk] = tmax(tok_ek)

                bv = next_bank()
                gv = proj_group(lambda cj: slab[:, cj * 256 + 128:
                                                cj * 256 + 256],
                                lambda cj: mixB[:, cj * TC:(cj + 1) * TC],
                                bv, tmax(tok_sl, bank_free[bv]),
                                cj_dep=lambda cj: mv_toks[cj])
                slab_free[sb] = tmax(gk, gv)
                kv_use.append(tmax(gk, gv))

                z = scr(2 + ct % 2)
                tok_zc = sch.op("scalar", lambda e_, o=z, p=PS[bv]:
                                e_.copy(o, p[:]),
                                tmax(gv, scrf.get(("z", ct % 2))))
                bank_free[bv] = tmax(tok_zc)
                tok_z = sch.op("vector", lambda e_, o=z, a=ek:
                               e_.tensor_tensor(o, o, a, ALU.mult),
                               tmax(tok_zc, tok_ek))

                sa = scr(4 + ct % 2)
                sb_ = scr(6 + ct % 2)
                tok_sa = sch.op("vector", lambda e_, o=sa, c=ct, d1=z:
                                e_.tensor_tensor_scan(
                                    o, con["w"][:, c:c + 1].broadcast_to(
                                        [128, TC]), d1, car_a[:, c:c + 1],
                                    ALU.mult, ALU.add),
                                tmax(tok_z, tok_w, tok_carry,
                                     scrf.get(("sa", ct % 2))))
                tok_sb = sch.op("vector", lambda e_, o=sb_, c=ct, d1=ek:
                                e_.tensor_tensor_scan(
                                    o, con["w"][:, c:c + 1].broadcast_to(
                                        [128, TC]), d1, car_b[:, c:c + 1],
                                    ALU.mult, ALU.add),
                                tmax(tok_ek, tok_w, tok_carry,
                                     scrf.get(("sb", ct % 2))))
                num = scr(8)
                den = scr(9)
                t_nb = sch.op("vector",
                              lambda e_, o=num, zz=z, s_=sa, c=ct:
                              e_.scalar_tensor_tensor(
                                  o[:, 1:TC], zz[:, 1:TC],
                                  con["eu"][:, c:c + 1], s_[:, 0:TC - 1],
                                  ALU.mult, ALU.add),
                              tmax(tok_sa, tok_eu,
                                   scrf.get("num")))
                t_nt = sch.op("vector",
                              lambda e_, o=num, zz=z, c=ct:
                              e_.scalar_tensor_tensor(
                                  o[:, 0:1], zz[:, 0:1],
                                  con["eu"][:, c:c + 1], car_a[:, c:c + 1],
                                  ALU.mult, ALU.add),
                              tmax(tok_z, tok_eu, tok_carry,
                                   scrf.get("num")))
                t_d1 = sch.op("scalar",
                              lambda e_, o=den, ee=ek, c=ct:
                              e_.mul(o[:, 1:TC], ee[:, 1:TC],
                                     con["eu"][:, c:c + 1]),
                              tmax(tok_ek, tok_eu,
                                   scrf.get("den")))
                t_d2 = sch.op("vector",
                              lambda e_, o=den, s_=sb_:
                              e_.tensor_tensor(o[:, 1:TC], o[:, 1:TC],
                                               s_[:, 0:TC - 1], ALU.add),
                              tmax(t_d1, tok_sb))
                t_dt = sch.op("vector",
                              lambda e_, o=den, ee=ek, c=ct:
                              e_.scalar_tensor_tensor(
                                  o[:, 0:1], ee[:, 0:1],
                                  con["eu"][:, c:c + 1], car_b[:, c:c + 1],
                                  ALU.mult, ALU.add),
                              tmax(tok_ek, tok_eu, tok_carry,
                                   scrf.get("den")))
                t_ca = sch.op("vector", lambda e_, c=ct, s_=sa:
                              e_.tensor_copy(car_a[:, c:c + 1],
                                             s_[:, TC - 1:TC]),
                              tmax(tok_sa, t_nt))
                t_cb = sch.op("vector", lambda e_, c=ct, s_=sb_:
                              e_.tensor_copy(car_b[:, c:c + 1],
                                             s_[:, TC - 1:TC]),
                              tmax(tok_sb, t_dt))
                carry_next.append(tmax(t_ca, t_cb))
                t_l = sch.op("scalar", lambda e_, o=den:
                             e_.activation(o, o, AF.Ln), tmax(t_d2, t_dt))
                t_e = sch.op("scalar", lambda e_, o=den:
                             e_.activation(o, o, AF.Exp, scale=-1.0), t_l)
                wv_ = wkvr[:, ct * TC:(ct + 1) * TC]
                t_wk = sch.op("vector", lambda e_, o=wv_, a=num, b_=den:
                              e_.tensor_tensor(o, a, b_, ALU.mult),
                              tmax(t_nb, t_nt, t_e,
                                   wkv_free if ct == 0 else ()))
                wkvr_toks[ct] = t_wk
                scrf["num"] = tmax(t_wk)
                scrf["den"] = tmax(t_wk)
                scrf[("sa", ct % 2)] = tmax(t_nb, t_ca)
                scrf[("sb", ct % 2)] = tmax(t_d2, t_cb)
                scrf[("ek", ct % 2)] = tmax(t_d1, t_dt, tok_sb, tok_z)
                scrf[("z", ct % 2)] = tmax(t_nb, t_nt, tok_sa)
            tok_carry = tmax(*carry_next)
            kv_mix_done = tmax(*kv_use)
            if full_upto == "kv":
                mix_free = kv_mix_done
                wkv_free = tmax(*[t for t in wkvr_toks if t])
                xbf_free = tmax(*(xx_toks + mk_toks + mv_toks))
                o32_free = tmax()
                ao_free = tmax()
                continue

            mr_toks, mg_toks = [], []
            for cj in range(NT):
                xxs = scr(10).bitcast(BF16)[:, (cj % 2) * TC:
                                            (cj % 2) * TC + TC]
                prev_v = xbf[:, cj * sY:cj * sY + TC]
                x_v = xbf[:, cj * sY + 1:cj * sY + 1 + TC]
                t_xx = sch.op("vector", lambda e_, o=xxs, a=prev_v, b_=x_v:
                              e_.tensor_tensor(o, a, b_, ALU.subtract),
                              tmax(tok_x, scrf.get(("xx", cj % 2)),
                                   scrf.get(("osq", cj % 2))))
                mr = mixA[:, cj * TC:(cj + 1) * TC]
                mg = mixB[:, cj * TC:(cj + 1) * TC]
                t_mr = sch.op("vector", lambda e_, o=mr, a=xxs, c=cj, b_=x_v:
                              e_.scalar_tensor_tensor(
                                  o, a, con["tmr"][:, c:c + 1], b_,
                                  ALU.mult, ALU.add),
                              tmax(t_xx, tok_con,
                                   kv_mix_done if cj == 0 else ()))
                t_mg = sch.op("vector", lambda e_, o=mg, a=xxs, c=cj, b_=x_v:
                              e_.scalar_tensor_tensor(
                                  o, a, con["tmg"][:, c:c + 1], b_,
                                  ALU.mult, ALU.add),
                              tmax(t_xx, tok_con,
                                   kv_mix_done if cj == 0 else ()))
                scrf[("xx", cj % 2)] = tmax(t_mr, t_mg)
                mr_toks.append(t_mr)
                mg_toks.append(t_mg)
            tok_mixr = tmax(*mr_toks)
            tok_mixg = tmax(*mg_toks)
            xbf_free = tmax(*(xx_toks + mr_toks + mg_toks))

            ao_toks = [None] * NT
            rg_use = []
            for ct in range(NT):
                prep_tick()
                sb = next_slab()
                sdst = slabs[sb].rearrange("p (c n) -> p c n", n=256)
                tok_sl = sch.dma("sync" if sb == 0 else "scalar", f"slab{sb}",
                                 lambda e_, d=sdst,
                                 s_=qrg[ct].rearrange("(c p) n -> p c n",
                                                      p=128):
                                 e_.dma_start(out=d, in_=s_),
                                 tmax(slab_free[sb], tok_qrg[ct]))
                slab = slabs[sb]
                br = next_bank()
                gr = proj_group(lambda cj: slab[:, cj * 256:cj * 256 + 128],
                                lambda cj: mixA[:, cj * TC:(cj + 1) * TC],
                                br, tmax(tok_sl, bank_free[br]),
                                cj_dep=lambda cj: mr_toks[cj])
                sigr = scr(11)
                t_sr = sch.op("scalar", lambda e_, o=sigr, p=PS[br]:
                              e_.activation(o, p[:], AF.Sigmoid),
                              tmax(gr, scrf.get("sigr")))
                bank_free[br] = tmax(t_sr)
                bg = next_bank()
                gg = proj_group(lambda cj: slab[:, cj * 256 + 128:
                                                cj * 256 + 256],
                                lambda cj: mixB[:, cj * TC:(cj + 1) * TC],
                                bg, tmax(tok_sl, bank_free[bg]),
                                cj_dep=lambda cj: mg_toks[cj])
                slab_free[sb] = tmax(gr, gg)
                rg_use.append(tmax(gr, gg))
                sigg = scr(12)
                t_sg = sch.op("scalar", lambda e_, o=sigg, p=PS[bg]:
                              e_.activation(o, p[:], AF.Sigmoid),
                              tmax(gg, scrf.get("sigg")))
                bank_free[bg] = tmax(t_sg)
                t_p = sch.op("vector", lambda e_, a=sigr, b_=sigg:
                             e_.tensor_tensor(a, a, b_, ALU.mult),
                             tmax(t_sr, t_sg))
                scrf["sigg"] = tmax(t_p)
                aov = aobf[:, ct * TC:(ct + 1) * TC]
                t_ao = sch.op("vector", lambda e_, o=aov, a=sigr, c=ct:
                              e_.tensor_tensor(
                                  o, wkvr[:, c * TC:(c + 1) * TC], a,
                                  ALU.mult),
                              tmax(t_p, wkvr_toks[ct],
                                   ao_free if ct == 0 else ()))
                scrf["sigr"] = tmax(t_ao)
                ao_toks[ct] = t_ao
            tok_ao_all = tmax(*ao_toks)
            rg_mm_done = tmax(*rg_use)

            if full_upto == "rg":
                mix_free = rg_mm_done
                wkv_free = tmax(*[t for t in wkvr_toks if t])
                ao_free = tok_ao_all
                o32_free = tmax()
                continue

            x32src = xw[:, pos0 + 1:pos0 + TC + 1].rearrange(
                "(c p) t -> p c t", p=128)
            tok_x32 = sch.dma("scalar", "x32", lambda e_, s_=x32src:
                              e_.dma_start(out=x32.rearrange(
                                  "p (c t) -> p c t", t=TC), in_=s_),
                              tmax(rg_mm_done, st.get("x32")))
            st["x32"] = tok_x32

            o32_toks = [None] * NT
            tok_statmm = tok_statmq = None
            for ct2 in range(0, NT, 2):
                prep_tick()
                sb = next_slab()
                sdst = slabs[sb].rearrange("p (c n) -> p c n", n=256)
                tok_sl = sch.dma("sync" if sb == 0 else "scalar", f"slab{sb}",
                                 lambda e_, d=sdst,
                                 s_=qwo[:, ct2 * 128:(ct2 + 2) * 128]
                                 .rearrange("(c p) n -> p c n", p=128):
                                 e_.dma_start(out=d, in_=s_),
                                 tmax(slab_free[sb], tok_qwo[ct2 // 2]))
                slab = slabs[sb]
                for ctl in range(2):
                    ct = ct2 + ctl
                    bo = next_bank()
                    go = proj_group(
                        lambda cj, l_=ctl: slab[:, cj * 256 + l_ * 128:
                                                cj * 256 + l_ * 128 + 128],
                        lambda cj: aobf[:, cj * TC:(cj + 1) * TC],
                        bo, tmax(tok_sl, bank_free[bo]),
                        cj_dep=lambda cj: ao_toks[cj])
                    ov = o32[:, ct * TC:(ct + 1) * TC]
                    t_o = sch.op("scalar", lambda e_, o=ov, p=PS[bo]:
                                 e_.copy(o, p[:]),
                                 tmax(go, o32_free if ct == 0 else ()))
                    obv = scr(13).bitcast(BF16)[:, (ct % 2) * TC:
                                                (ct % 2) * TC + TC]
                    if wo_level >= 2:
                        t_ob = sch.op("vector", lambda e_, o=obv, s_=ov:
                                      e_.tensor_copy(o, s_),
                                      tmax(t_o, scrf.get(("obf", ct % 2))))
                    else:
                        t_ob = t_o
                    bank_free[bo] = tmax(t_o)
                    o32_toks[ct] = t_o
                    sqv = scr(10).bitcast(BF16)[:, (ct % 2) * TC:
                                                (ct % 2) * TC + TC]
                    if wo_level >= 3:
                        t_sq = sch.op("scalar", lambda e_, o=sqv, a=obv:
                                      e_.activation(o, a, AF.Square),
                                      tmax(t_ob, scrf.get(("xx", ct % 2)),
                                           scrf.get(("osq", ct % 2))))
                    else:
                        t_sq = t_ob
                    if wo_level >= 4:
                        t_ms = sch.pe(lambda e_, r_=obv, s0=(ct == 0),
                                      s1=(ct == NT - 1):
                                      e_.matmul(PS[4][0:1, 0:TC], ones_bf,
                                                r_, start=s0, stop=s1),
                                      tmax(t_ob, tok_ones,
                                           st["ps4"] if ct == 0 else ()),
                                      mark=True)
                        t_mq = sch.pe(lambda e_, r_=sqv, s0=(ct == 0),
                                      s1=(ct == NT - 1):
                                      e_.matmul(PS[5][0:1, 0:TC], ones_bf,
                                                r_, start=s0, stop=s1),
                                      tmax(t_sq, tok_ones,
                                           st["ps5"] if ct == 0 else ()),
                                      mark=True)
                        scrf[("obf", ct % 2)] = tmax(t_ms)
                        scrf[("osq", ct % 2)] = tmax(t_mq)
                        if ct == NT - 1:
                            tok_statmm, tok_statmq = t_ms, t_mq
                    else:
                        scrf[("obf", ct % 2)] = tmax(t_ob)
                        scrf[("osq", ct % 2)] = tmax(t_sq)
                        tok_statmm = tok_statmq = None
                slab_free[sb] = sch.pemark()
            ao_free = sch.pemark()

            if full_upto == "wo":
                mix_free = tmax(rg_mm_done, tok_x32)
                wkv_free = tmax(*[t for t in wkvr_toks if t])
                o32_free = tmax(*[t for t in o32_toks if t])
                if wo_level >= 4:
                    t_mu = sch.op("vector", lambda e_: e_.tensor_scalar(
                        row_mu, PS[4][0:1, 0:TC], 1.0 / D, None, ALU.mult),
                        tmax(tok_statmm, st["rows"]))
                    t_e2 = sch.op("vector", lambda e_: e_.tensor_scalar(
                        row_e2, PS[5][0:1, 0:TC], 1.0 / D, None, ALU.mult),
                        tmax(tok_statmq, st["rows"]))
                    st["ps4"] = tmax(t_mu)
                    st["ps5"] = tmax(t_e2)
                    st["rows"] = tmax(t_mu, t_e2)
                continue

            def ln_rows(tok_s, tok_q):
                t_mu = sch.op("vector", lambda e_: e_.tensor_scalar(
                    row_mu, PS[4][0:1, 0:TC], 1.0 / D, None, ALU.mult),
                    tmax(tok_s, st["rows"]))
                t_e2 = sch.op("vector", lambda e_: e_.tensor_scalar(
                    row_e2, PS[5][0:1, 0:TC], 1.0 / D, None, ALU.mult),
                    tmax(tok_q, st["rows"]))
                st["ps4"] = tmax(t_mu)
                st["ps5"] = tmax(t_e2)
                t_m2 = sch.op("vector", lambda e_: e_.tensor_tensor(
                    row_rs, row_mu, row_mu, ALU.mult), t_mu)
                t_var = sch.op("vector", lambda e_: e_.scalar_tensor_tensor(
                    row_rs, row_e2, EPS, row_rs, ALU.add, ALU.subtract),
                    tmax(t_e2, t_m2))
                t_ln = sch.op("scalar", lambda e_: e_.activation(
                    row_rs, row_rs, AF.Ln), t_var)
                t_rs = sch.op("scalar", lambda e_: e_.activation(
                    row_rs, row_rs, AF.Exp, scale=-0.5), t_ln)
                t_mr = sch.op("vector", lambda e_: e_.tensor_tensor(
                    row_mr, row_mu, row_rs, ALU.mult), tmax(t_rs, t_mu))
                return t_rs, t_mr, t_mu

            t_rs, t_mr, t_mu0 = ln_rows(tok_statmm, tok_statmq)
            t_p6 = bc_pair(6, row_rs, t_rs, st["bc"][0], 0)
            t_p7 = bc_pair(7, row_mr, t_mr, st["bc"][1], 1)

            bc_toks = [t_p6, t_p7]
            res_toks = [None] * NT
            rb_toks = []
            tts, tqs = [], []
            stat2s = stat2q = None
            tvs = []
            for ct in range(NT):
                prep_tick(1)
                uv = scr(4 + ct % 2)
                t_u = sch.op("scalar", lambda e_, o=uv, c=ct:
                             e_.mul(o, o32[:, c * TC:(c + 1) * TC],
                                    con["lng"][:, c:c + 1]),
                             tmax(o32_toks[ct], tok_con,
                                  scrf.get(("sa", ct % 2))))
                tv = scr(0 + ct % 2)
                t_t = sch.op("vector", lambda e_, o=tv, a=uv:
                             e_.tensor_tensor(o, a, PS[6][:], ALU.mult),
                             tmax(t_p6, t_u, scrf.get(("ek", ct % 2))))
                vv = scr(6 + ct % 2)
                t_v = sch.op("scalar", lambda e_, o=vv, c=ct:
                             e_.mul(o, PS[7][:], con["lng"][:, c:c + 1]),
                             tmax(t_p7, tok_con,
                                  scrf.get(("sb", ct % 2))))
                tvs.append(t_v)
                qv = scr(2 + ct % 2)
                t_q = sch.op("vector", lambda e_, o=qv, a=vv, c=ct:
                             e_.tensor_tensor(
                                 o, a, x32[:, c * TC:(c + 1) * TC],
                                 ALU.subtract),
                             tmax(t_v, tok_x32,
                                  scrf.get(("z", ct % 2))))
                rv = resf[:, ct * TC:(ct + 1) * TC]
                t_res = sch.op("vector", lambda e_, o=rv, a=tv, b_=qv, c=ct:
                               e_.scalar_tensor_tensor(
                                   o, a, con["lnb"][:, c:c + 1], b_,
                                   ALU.add, ALU.subtract),
                               tmax(t_t, t_q, tok_con))
                tts.append(t_t)
                tqs.append(t_q)
                scrf[("ek", ct % 2)] = tmax(t_res)
                scrf[("z", ct % 2)] = tmax(t_res)
                scrf[("sa", ct % 2)] = tmax(t_t)
                scrf[("sb", ct % 2)] = tmax(t_q)
                res_toks[ct] = t_res
                rbv = resbf[:, ct * TC:(ct + 1) * TC]
                t_rb = sch.op("vector", lambda e_, o=rbv, a=rv:
                              e_.tensor_copy(o, a),
                              tmax(t_res, ao_free if ct == 0 else ()))
                sqv = scr(10).bitcast(BF16)[:, (ct % 2) * TC:
                                            (ct % 2) * TC + TC]
                t_sq = sch.op("scalar", lambda e_, o=sqv, a=rbv:
                              e_.activation(o, a, AF.Square),
                              tmax(t_rb, scrf.get(("osq", ct % 2))))
                t_ms = sch.pe(lambda e_, r_=rbv, s0=(ct == 0),
                              s1=(ct == NT - 1):
                              e_.matmul(PS[4][0:1, 0:TC], ones_bf, r_,
                                        start=s0, stop=s1),
                              tmax(t_rb, tok_ones,
                                   st["ps4"] if ct == 0 else ()),
                              mark=True)
                t_mq = sch.pe(lambda e_, r_=sqv, s0=(ct == 0),
                              s1=(ct == NT - 1):
                              e_.matmul(PS[5][0:1, 0:TC], ones_bf, r_,
                                        start=s0, stop=s1),
                              tmax(t_sq, tok_ones,
                                   st["ps5"] if ct == 0 else ()),
                              mark=True)
                scrf[("osq", ct % 2)] = tmax(t_mq)
                rb_toks.append(t_ms)
                if ct == NT - 1:
                    stat2s, stat2q = t_ms, t_mq

            st["bc"][0] = tmax(*tts)
            st["bc"][1] = tmax(*tvs)

            if full_upto == "innerln":
                mix_free = tmax(rg_mm_done, *tqs)
                wkv_free = tmax(*res_toks)
                o32_free = tmax(*[t for t in o32_toks if t])
                ao_free = tmax(*rb_toks)
                st["rows"] = tmax(st["rows"], bc_toks)
                continue

            st["rows"] = tmax(st["rows"], bc_toks)
            t_rs, t_mr, _ = ln_rows(stat2s, stat2q)
            t_b6 = bc_pair(6, row_rs, t_rs, st["bc"][0], 0)
            t_b7 = bc_pair(7, row_mr, t_mr, st["bc"][1], 1)
            st["rows"] = tmax(st["rows"], t_b6, t_b7)
            res3 = resf.rearrange("p (c t) -> p c t", t=TC)
            o323 = o32.rearrange("p (c t) -> p c t", t=TC)
            ps6b = PS[6][:].rearrange("p (a t) -> p a t", a=1).broadcast_to(
                [128, NT, TC])
            ps7b = PS[7][:].rearrange("p (a t) -> p a t", a=1).broadcast_to(
                [128, NT, TC])
            t_t2 = sch.op("vector", lambda e_: e_.tensor_tensor(
                o323, res3, ps6b, ALU.mult),
                tmax(t_b6, *res_toks))
            t_xr = sch.op("vector", lambda e_: e_.tensor_tensor(
                res3, o323, ps7b, ALU.subtract), tmax(t_b7, t_t2))
            st["bc"][0] = tmax(t_t2)
            st["bc"][1] = tmax(t_xr)
            o32_free = tmax(t_xr)
            t_st = sch.dma("scalar", "xrst", lambda e_, o=fci: e_.dma_start(
                out=xres_d[:, o * TC:(o + 1) * TC].rearrange(
                    "(c p) t -> p c t", p=128), in_=res3),
                tmax(t_xr, st.get("xrst")))
            st["xrst"] = t_st
            xres_st.append(t_st)
            wkv_free = tmax(t_st, t_xr)
            ao_free = tmax(*rb_toks)
            mix_free = tmax(rg_mm_done, *tqs)

        tok_attn = tmax(wkv_free, o32_free, ao_free, mix_free, xbf_free,
                        tok_carry, *xres_st)


        if debug_stop == "attn":
            t_dbg = sch.dma("sync", "y", lambda e_: e_.dma_start(
                out=y_out[:, 0:M], in_=xres_d[:, 0:M]),
                tmax(tok_attn, st.get("rows"), st.get("ps4"), st.get("ps5")))
            sch.wait("sync", t_dbg)
            emit_all(block, sch, sems)
            return nc


        xresbf = A_MIX[:, 0:NT * TFF // 2].bitcast(BF16)
        h_ap = [A_WKV[:].bitcast(BF16), A_O32[:, 0:NT * TC].bitcast(BF16)]
        accq_ap = [A_XBF[:, 0:NT * 256], A_AO[:, 0:NT * 256],
                   A_SCR[:, 0:NT * 256], A_SCR[:, NT * 256:2 * NT * 256]]
        xq32s = [A_O32[:, 0:NT * 256], A_O32[:, NT * 256:2 * NT * 256]]
        resbf2 = A_WKV[:].bitcast(BF16)
        if sqq_extra:
            sqq = A_PREP[:, 2 * NT * 64 + NT * 96:].bitcast(BF16)
        else:
            sqq = A_PREP[:, 2 * NT * 64 + NT * 32:
                         2 * NT * 64 + NT * 96].bitcast(BF16)

        xrb_free = tmax(tok_attn)
        h_free = tmax(tok_attn)
        acc_free = [tmax(tok_attn), tmax(tok_attn),
                    tmax(tok_attn), tmax(tok_attn)]
        xq_free = [tmax(pb_free[0], pb_free[1], tok_attn),
                   tmax(pb_free[0], pb_free[1], tok_attn)]
        sq_free = {}
        y_toks = []

        for oc in range(n_out):
            t0 = oc * TFF
            xrb_toks = []
            for qi in range(NQ):
                qb = qi % 2
                src = xres_d[:, t0 + qi * 256:t0 + (qi + 1) * 256].rearrange(
                    "(c p) t -> p c t", p=128)
                tok_l = sch.dma("scalar", "xqa" if qb == 0 else "xqb",
                                lambda e_, s_=src, q=qb:
                                e_.dma_start(out=xq32s[q].rearrange(
                                    "p (c t) -> p c t", t=256), in_=s_),
                                tmax(xq_free[qb],
                                     xres_st[(t0 + qi * 256) // TC]))
                dstv = xresbf.rearrange("p (c t) -> p c t", t=TFF)[
                    :, :, qi * 256:(qi + 1) * 256]
                t_c = sch.op("vector", lambda e_, d=dstv, q=qb:
                             e_.tensor_copy(d, xq32s[q].rearrange(
                                 "p (c t) -> p c t", t=256)),
                             tmax(tok_l, xrb_free if qi == 0 else ()))
                xq_free[qb] = tmax(t_c)
                xrb_toks.append(t_c)
            tok_xrb = tmax(*xrb_toks)

            acc_toks = [None] * 4
            h_tok = {}
            HGOF = (NFG // 2) * TFF

            def mm1_group(g):
                for fts in range(0, NFG, 2):
                    ft0 = g * NFG + fts
                    q_run_to(n_p1 + g * NFG * 2 + (fts + 2) * 2 + 4)
                    sb = next_slab()
                    sdst = slabs[sb].rearrange("p (c n) -> p c n", n=256)
                    tok_sl = sch.dma("sync" if sb == 0 else "scalar", f"slab{sb}",
                                 lambda e_, d=sdst,
                                     s_=wq1[:, ft0 * 128:(ft0 + 2) * 128]
                                     .rearrange("(c p) n -> p c n", p=128):
                                     e_.dma_start(out=d, in_=s_),
                                     tmax(slab_free[sb],
                                          *st["q1"][ft0 * 2:ft0 * 2 + 4]))
                    slab = slabs[sb]
                    for ftl in range(2):
                        ft = fts + ftl
                        ftg = ft0 + ftl
                        hxs = NFG // 2
                        for hf in range(NH):
                            bb = next_bank()
                            gm = proj_group(
                                lambda cj, l_=ftl: slab[
                                    :, cj * 256 + l_ * 128:
                                    cj * 256 + l_ * 128 + 128],
                                lambda cj, h=hf: xresbf[
                                    :, cj * TFF + h * 512:
                                    cj * TFF + h * 512 + 512],
                                bb, tmax(tok_sl, xrb_toks[2 * hf],
                                         xrb_toks[2 * hf + 1],
                                         bank_free[bb],
                                         h_free if (fts == 0 and ftl == 0
                                                    and hf == 0 and g == 0)
                                         else ()))
                            hx, hcol = divmod(ft, hxs)
                            hv = h_ap[hx][:, (g % 2) * HGOF + hcol * TFF
                                          + hf * 512:
                                          (g % 2) * HGOF + hcol * TFF
                                          + hf * 512 + 512]
                            t_h = sch.op("scalar",
                                         lambda e_, o=hv, p=PS[bb], f=ftg:
                                         e_.activation(o, p[:], AF.Gelu,
                                                       bias=b1s[:, f:f + 1]),
                                         tmax(gm, tok_con))
                            bank_free[bb] = tmax(t_h)
                            h_tok[(g, ft, hf)] = t_h
                    slab_free[sb] = sch.pemark()

            def mm2_group(g):
                for ct in range(NT):
                    if ct == 0:
                        q_run_to(n_p1 + (g + 1) * NFG * 2)
                    sb = next_slab()
                    sdst = slabs[sb][:, 0:NFG * 128].rearrange(
                        "p (a n) -> p a n", n=128)
                    tok_sl = sch.dma("sync" if sb == 0 else "scalar", f"slab{sb}",
                                 lambda e_, d=sdst,
                                     s_=wq2[ct, g * NFG * 128:
                                            (g + 1) * NFG * 128, :]
                                     .rearrange("(a p) n -> p a n", p=128):
                                     e_.dma_start(out=d, in_=s_),
                                     tmax(slab_free[sb], q_front()))
                    slab = slabs[sb]
                    for hf in range(NH):
                        bb = next_bank()
                        for ft in range(NFG):
                            hx, hcol = divmod(ft, NFG // 2)
                            lt = slab[:, ft * 128:(ft + 1) * 128]
                            rh = h_ap[hx][:, (g % 2) * HGOF + hcol * TFF
                                          + hf * 512:
                                          (g % 2) * HGOF + hcol * TFF
                                          + hf * 512 + 512]
                            sch.pe(lambda e_, p=PS[bb], l_=lt, r_=rh,
                                   s0=(ft == 0), s1=(ft == NFG - 1):
                                   e_.matmul(p[:], l_, r_, start=s0,
                                             stop=s1),
                                   tmax(h_tok[(g, ft, hf)],
                                        tmax(tok_sl, bank_free[bb])
                                        if ft == 0 else ()),
                                   mark=(ft == NFG - 1))
                        gmm = sch.pemark()
                        evs = []
                        for h2 in range(2):
                            qi = hf * 2 + h2
                            av = accq_ap[qi][:, ct * 256:(ct + 1) * 256]
                            pv = PS[bb][:, h2 * 256:h2 * 256 + 256]
                            if g == 0:
                                t_a = sch.op("vector",
                                             lambda e_, o=av, p=pv:
                                             e_.tensor_copy(o, p),
                                             tmax(gmm, acc_free[qi]
                                                  if ct == 0 else ()))
                            else:
                                t_a = sch.op("vector",
                                             lambda e_, o=av, p=pv:
                                             e_.tensor_tensor(o, o, p,
                                                              ALU.add),
                                             tmax(gmm, acc_toks[qi]))
                            acc_toks[qi] = tmax(t_a, acc_toks[qi])
                            evs.append(t_a)
                        bank_free[bb] = tmax(*evs)
                    slab_free[sb] = sch.pemark()

            mm1_group(0)
            for g in range(GF):
                if g + 1 < GF:
                    mm1_group(g + 1)
                mm2_group(g)
            h_free = sch.pemark()

            ep_toks = [[None] * NT for _ in range(4)]
            rb2_toks = [None] * 4
            for qi in range(NQ):
                qb = qi % 2
                src = xres_d[:, t0 + qi * 256:t0 + (qi + 1) * 256].rearrange(
                    "(c p) t -> p c t", p=128)
                tok_l = sch.dma("scalar", "xqa" if qb == 0 else "xqb",
                                lambda e_, s_=src, q=qb:
                                e_.dma_start(out=xq32s[q].rearrange(
                                    "p (c t) -> p c t", t=256), in_=s_),
                                tmax(xq_free[qb], h_free))
                qtoks = []
                for ct in range(NT):
                    av = accq_ap[qi][:, ct * 256:(ct + 1) * 256]
                    xv = xq32s[qb][:, ct * 256:(ct + 1) * 256]
                    t_ep = sch.op("vector",
                                  lambda e_, o=av, x_=xv, c=ct:
                                  e_.scalar_tensor_tensor(
                                      o, o, con["b2"][:, c:c + 1], x_,
                                      ALU.add, ALU.add),
                                  tmax(tok_l, acc_toks[qi], tok_con))
                    ep_toks[qi][ct] = t_ep
                    qtoks.append(t_ep)
                xq_free[qb] = tmax(*qtoks)
                dstv = resbf2[:, 0:NT * TFF].rearrange(
                    "p (c t) -> p c t", t=TFF)[:, :, qi * 256:(qi + 1) * 256]
                rb2_toks[qi] = sch.op("vector",
                                      lambda e_, d=dstv, s_=accq_ap[qi]
                                      .rearrange("p (c t) -> p c t", t=256):
                                      e_.tensor_copy(d, s_),
                                      tmax(qtoks, h_free))

            for hf in range(NH):
                s2s = s2q = None
                for ct in range(NT):
                    rbv = resbf2[:, ct * TFF + hf * 512:
                                 ct * TFF + hf * 512 + 512]
                    sqv = sqq[:, (ct % 2) * 512:(ct % 2) * 512 + 512]
                    t_sq = sch.op("scalar", lambda e_, o=sqv, a=rbv:
                                  e_.activation(o, a, AF.Square),
                                  tmax(rb2_toks[hf * 2],
                                       rb2_toks[hf * 2 + 1],
                                       sq_free.get(ct % 2), q_front()))
                    t_ms = sch.pe(lambda e_, r_=rbv, s0=(ct == 0),
                                  s1=(ct == NT - 1):
                                  e_.matmul(PS[4][0:1, 0:TC], ones_bf, r_,
                                            start=s0, stop=s1),
                                  tmax(rb2_toks[hf * 2],
                                       rb2_toks[hf * 2 + 1], tok_ones,
                                       st["ps4"] if ct == 0 else ()),
                                  mark=True)
                    t_mq = sch.pe(lambda e_, r_=sqv, s0=(ct == 0),
                                  s1=(ct == NT - 1):
                                  e_.matmul(PS[5][0:1, 0:TC], ones_bf, r_,
                                            start=s0, stop=s1),
                                  tmax(t_sq, tok_ones,
                                       st["ps5"] if ct == 0 else ()),
                                  mark=True)
                    sq_free[ct % 2] = tmax(t_mq)
                    if ct == NT - 1:
                        s2s, s2q = t_ms, t_mq
                t_rs, t_mr, _ = ln_rows(s2s, s2q)
                t_b6 = bc_pair(6, row_rs, t_rs, st["bc"][0], 0)
                t_b7 = bc_pair(7, row_mr, t_mr, st["bc"][1], 1)
                st["rows"] = tmax(st["rows"], t_b6, t_b7)
                for h2 in range(2):
                    qi = hf * 2 + h2
                    a3 = accq_ap[qi].rearrange("p (c t) -> p c t", t=256)
                    p6v = PS[6][:, h2 * 256:h2 * 256 + 256].rearrange(
                        "p (a t) -> p a t", a=1).broadcast_to([128, NT, 256])
                    p7v = PS[7][:, h2 * 256:h2 * 256 + 256].rearrange(
                        "p (a t) -> p a t", a=1).broadcast_to([128, NT, 256])
                    t_a1 = sch.op("vector", lambda e_, o=a3, p=p6v:
                                  e_.tensor_tensor(o, o, p, ALU.mult),
                                  tmax(t_b6, *ep_toks[qi], rb2_toks[qi]))
                    t_a2 = sch.op("vector", lambda e_, o=a3, p=p7v:
                                  e_.tensor_tensor(o, o, p, ALU.subtract),
                                  tmax(t_b7, t_a1))
                    st["bc"][0] = tmax(st["bc"][0], t_a1)
                    st["bc"][1] = tmax(st["bc"][1], t_a2)
                    t_y = sch.dma("scalar", f"y{qi}",
                                  lambda e_, q=qi, t_=t0:
                                  e_.dma_start(
                        out=y_out[:, t_ + q * 256:t_ + (q + 1) * 256]
                        .rearrange("(c p) t -> p c t", p=128),
                        in_=accq_ap[q].rearrange("p (c t) -> p c t", t=256)),
                        t_a2)
                    y_toks.append(t_y)
                    acc_free[qi] = tmax(t_y)
            h_free = tmax(h_free, sch.pemark())

        sch.wait("sync", tmax(*y_toks))
        emit_all(block, sch, sems)

    return nc


def emit_all(block, sch, sems):
    @block.sync
    def _(e_):
        sch.run("sync", sems, e_)

    @block.scalar
    def _(e_):
        sch.run("scalar", sems, e_)

    @block.vector
    def _(e_):
        sch.run("vector", sems, e_)

    @block.gpsimd
    def _(e_):
        sch.run("gpsimd", sems, e_)

    @block.tensor
    def _(e_):
        sch.run("tensor", sems, e_)



def _prep_inputs(x, td, tf, tmk, tmv, tmr, tmg, Wr, Wk, Wv, Wg, Wo,
                 ln_g, ln_b, w1, b1, w2, b2, D, T, F):
    NT = D // 128
    NF = F // 128
    B = x.shape[0]

    def vec128(v):
        return np.ascontiguousarray(
            np.asarray(v, np.float32).reshape(NT, 128).T)

    WkT = np.ascontiguousarray(np.asarray(Wk, np.float32).T)
    WvT = np.ascontiguousarray(np.asarray(Wv, np.float32).T)
    WrT = np.ascontiguousarray(np.asarray(Wr, np.float32).T)
    WgT = np.ascontiguousarray(np.asarray(Wg, np.float32).T)
    WoT = np.ascontiguousarray(np.asarray(Wo, np.float32).T)
    wkv_cat = np.empty((NT, D, 256), np.float32)
    wrg_cat = np.empty((NT, D, 256), np.float32)
    for ct in range(NT):
        wkv_cat[ct, :, 0:128] = WkT[:, ct * 128:(ct + 1) * 128]
        wkv_cat[ct, :, 128:256] = WvT[:, ct * 128:(ct + 1) * 128]
        wrg_cat[ct, :, 0:128] = WrT[:, ct * 128:(ct + 1) * 128]
        wrg_cat[ct, :, 128:256] = WgT[:, ct * 128:(ct + 1) * 128]

    common = {
        "wkv": wkv_cat, "wrg": wrg_cat, "wo": WoT,
        "w1t": np.ascontiguousarray(np.asarray(w1, np.float32).T),
        "w2t": np.ascontiguousarray(np.asarray(w2, np.float32).T),
        "tmk": vec128(tmk), "tmv": vec128(tmv), "tmr": vec128(tmr),
        "tmg": vec128(tmg), "dec": vec128(td), "tf": vec128(tf),
        "lng": vec128(ln_g), "lnb": vec128(ln_b), "b2": vec128(b2),
        "b1": np.ascontiguousarray(
            np.asarray(b1, np.float32).reshape(NF, 128).T),
        "lngrow": np.ascontiguousarray(
            np.asarray(ln_g, np.float32).reshape(1, D)),
    }

    xT = np.transpose(np.asarray(x, np.float32), (0, 2, 1))
    in_maps = []
    for core in range(2 * B):
        b = core // 2
        half = core % 2
        xb = xT[b]
        if half == 1:
            win = xb
        else:
            win = np.concatenate([xb[:, T // 2:], xb[:, :T // 2]], axis=1)
            win = np.ascontiguousarray(win)
            win[:, T // 2 - 1] = xb[:, 1]
        wext = np.concatenate([win[:, 1:2], win], axis=1)
        m = {"xw": np.ascontiguousarray(wext),
             "gate": np.full((128, 1), float(half), np.float32)}
        m.update(common)
        in_maps.append(m)
    return in_maps


def kernel(**inp):
    D, T, F = 2048, 4096, 8192
    B = 4
    x = np.asarray(inp["L0_attn_x"], np.float32)
    in_maps = _prep_inputs(
        x, inp["L0_attn_time_decay"], inp["L0_attn_time_first"],
        inp["L0_attn_tm_k"], inp["L0_attn_tm_v"], inp["L0_attn_tm_r"],
        inp["L0_attn_tm_g"], inp["L0_attn_Wr"], inp["L0_attn_Wk"],
        inp["L0_attn_Wv"], inp["L0_attn_Wg"], inp["L0_attn_Wo"],
        inp["L0_attn_ln_g"], inp["L0_attn_ln_b"], inp["L0_ffn_w1"],
        inp["L0_ffn_b1"], inp["L0_ffn_w2"], inp["L0_ffn_b2"], D, T, F)

    nc = build(D=D, T=T, F=F)
    res = run_bass_kernel_spmd(nc, in_maps, core_ids=list(range(8)))

    out = np.empty((B, T, D), np.float32)
    for core in range(8):
        b = core // 2
        half = core % 2
        yc = res.results[core]["y"]
        if half == 0:
            out[b, 0:T // 2, :] = yc.T
        else:
            out[b, T // 2:T, :] = yc.T
    return out


# version 31
# speedup vs baseline: 1.2563x; 1.2563x over previous
import numpy as np
import ml_dtypes

import concourse.bass as bass
import concourse.mybir as mybir
from concourse.bass_utils import run_bass_kernel_spmd

F32 = mybir.dt.float32
BF16 = mybir.dt.bfloat16
AF = mybir.ActivationFunctionType
ALU = mybir.AluOpType
AX = mybir.AxisListType

EPS = 1e-5

ENGS = ("sync", "scalar", "vector", "gpsimd", "tensor")
STREAMS = ("con", "cast0", "cast1", "cast2", "cast3", "cast4", "cast5",
           "slab0", "slab1", "xch", "x32", "pin0", "pin1",
           "qw0", "qw1", "qw2", "pin2", "xrst", "xqa", "xqb",
           "y0", "y1", "y2", "y3")
SEMS = ("sV", "sS", "sG", "sP") + STREAMS
SELF = {"vector": "sV", "scalar": "sS", "gpsimd": "sG"}


def tmax(*xs):
    best = {}
    stack = list(xs)
    while stack:
        t = stack.pop()
        if t is None or t == ():
            continue
        if isinstance(t, list):
            stack.extend(t)
            continue
        s, v = t
        if v is not None and v > best.get(s, -1):
            best[s] = v
    return [(s, v) for s, v in best.items()]


class Sched:
    def __init__(self):
        self.q = {e: [] for e in ENGS}
        self.cnt = {s: 0 for s in SEMS}
        self.floor = {e: {s: 0 for s in SEMS} for e in ENGS}

    def _emit(self, eng, fn, deps, inc, by):
        waits = []
        fl = self.floor[eng]
        for s, v in tmax(deps):
            if v > fl[s]:
                waits.append((s, v))
                fl[s] = v
        tok = None
        if inc is not None:
            self.cnt[inc] += by
            tok = (inc, self.cnt[inc])
        self.q[eng].append((waits, fn, inc, by))
        return tok

    def op(self, eng, fn, deps=()):
        s = SELF[eng]
        return self._emit(eng, fn, tmax(deps, (s, self.cnt[s])), s, 1)

    def pe(self, fn, deps=(), mark=False):
        return self._emit("tensor", fn, deps, "sP" if mark else None, 1)

    def pemark(self):
        return ("sP", self.cnt["sP"])

    def dma(self, eng, stream, fn, deps=()):
        return self._emit(eng, fn, deps, stream, 16)

    def wait(self, eng, deps):
        self._emit(eng, None, deps, None, 0)

    def run(self, name, sems, e):
        for waits, fn, inc, by in self.q[name]:
            for s, v in waits:
                e.wait_ge(sems[s], v)
            if fn is None:
                continue
            inst = fn(e)
            if inc is not None:
                inst.then_inc(sems[inc], by)


def build(D=2048, T=4096, F=8192, TC=512, TKV=1024, TFF=1024, GF=4,
          debug_stop=None, no_prep=False, full_upto=None, wo_level=4):
    NT = D // 128
    NF = F // 128
    M = T // 2
    n_kv = M // TKV
    n_full = M // TC
    n_out = M // TFF
    NFG = NF // GF
    NH = TFF // 512
    NQ = TFF // 256

    nc = bass.Bass("TRN2", target_bir_lowering=False)

    xw = nc.dram_tensor("xw", [D, T + 1], F32, kind="ExternalInput").ap()
    gate_in = nc.dram_tensor("gate", [128, 1], F32, kind="ExternalInput").ap()
    wkv_in = nc.dram_tensor("wkv", [NT, D, 256], F32,
                            kind="ExternalInput").ap()
    wrg_in = nc.dram_tensor("wrg", [NT, D, 256], F32,
                            kind="ExternalInput").ap()
    wo_in = nc.dram_tensor("wo", [D, D], F32, kind="ExternalInput").ap()
    w1t = nc.dram_tensor("w1t", [D, F], F32, kind="ExternalInput").ap()
    w2t = nc.dram_tensor("w2t", [F, D], F32, kind="ExternalInput").ap()
    VN = ("tmk", "tmv", "tmr", "tmg", "dec", "tf", "lng", "lnb", "b2")
    vec_in = {n: nc.dram_tensor(n, [128, NT], F32, kind="ExternalInput").ap()
              for n in VN}
    b1_in = nc.dram_tensor("b1", [128, NF], F32, kind="ExternalInput").ap()
    lngrow_in = nc.dram_tensor("lngrow", [1, D], F32,
                               kind="ExternalInput").ap()
    y_out = nc.dram_tensor("y", [D, M], F32, kind="ExternalOutput").ap()

    qkv = nc.dram_tensor("qkv", [NT, D, 256], BF16).ap()
    qrg = nc.dram_tensor("qrg", [NT, D, 256], BF16).ap()
    qwo = nc.dram_tensor("qwo", [D, D], BF16).ap()
    wq1 = nc.dram_tensor("wq1", [D, F], BF16).ap()
    wq2 = nc.dram_tensor("wq2", [NT, F, 128], BF16).ap()
    xres_d = nc.dram_tensor("xres", [D, M], F32).ap()
    xwbf_d = nc.dram_tensor("xwbf", [D, T + 1], BF16).ap()

    sch = Sched()
    from contextlib import ExitStack
    with ExitStack() as ctx:
        e = ctx.enter_context

        A_XBF = e(nc.sbuf_tensor("A_XBF", [128, NT * (TC + 1) // 2 + 8],
                                 F32))
        A_MIX = e(nc.sbuf_tensor("A_MIX", [128, NT * TC], F32))
        A_WKV = e(nc.sbuf_tensor("A_WKV", [128, NT * TC], F32))
        A_O32 = e(nc.sbuf_tensor("A_O32", [128, NT * TC + NT // 2], F32))
        A_AO = e(nc.sbuf_tensor("A_AO", [128, NT * TC // 2], F32))
        A_SCR = e(nc.sbuf_tensor("A_SCR", [128, 16 * TC], F32))
        A_SLAB = e(nc.sbuf_tensor("A_SLAB", [128, NT * 256], F32))
        sqq_extra = 512 if NT * 64 < 512 else 0
        A_PREP = e(nc.sbuf_tensor("A_PREP", [128, 2 * NT * 64 + NT * 96
                                             + sqq_extra], F32))
        A_CON = e(nc.sbuf_tensor("A_CON", [128, 16 * NT + NF + 210], F32))
        CARRY = e(nc.sbuf_tensor("CARRY", [128, 2 * NT], F32))
        LNROW = e(nc.sbuf_tensor("LNROW", [1, 5 * TC], F32))
        PS = [e(nc.psum_tensor(f"PS{i}", [128, 512], F32)) for i in range(8)]
        sems = {s: e(nc.semaphore(s)) for s in SEMS}
        block = e(nc.Block())

        con = {}
        off = 0
        for n in VN + ("w", "eu"):
            con[n] = A_CON[:, off:off + NT]
            off += NT
        b1s = A_CON[:, off:off + NF]
        off += NF
        ones_bf = A_CON[:, off:off + 1].bitcast(BF16)[:, 0:1]
        off += 1
        gcol = A_CON[:, off:off + 1]
        off += 1
        thrs = A_CON[:, off:off + 4]
        off += 4
        racc = A_CON[:, off:off + 4]
        off += 4
        ones_f = A_CON[:, off:off + 1]
        off += 1
        ones_b = A_CON[0:1, off:off + 128]
        off += 128
        ones_bb = A_CON[0:1, off:off + 64].bitcast(BF16)[0:1, 0:128]
        off += 64

        car_a = CARRY[:, 0:NT]
        car_b = CARRY[:, NT:2 * NT]

        sY = TC + 1
        xbf = A_XBF[:].bitcast(BF16)
        mixA = A_MIX[:, 0:NT * TC // 2].bitcast(BF16)
        mixB = A_MIX[:, NT * TC // 2:NT * TC].bitcast(BF16)
        x32 = A_MIX[:, 0:NT * TC]
        wkvr = A_WKV[:, 0:NT * TC]
        resf = A_WKV[:, 0:NT * TC]
        o32 = A_O32[:, 0:NT * TC]
        aobf = A_AO[:].bitcast(BF16)
        resbf = A_AO[:].bitcast(BF16)
        slabs = [A_SLAB[:, 0:NT * 128].bitcast(BF16),
                 A_SLAB[:, NT * 128:NT * 256].bitcast(BF16)]
        pin = [A_PREP[:, 0:NT * 64], A_PREP[:, NT * 64:2 * NT * 64],
               A_SCR[:, 14 * TC:14 * TC + NT * 64]]
        pouts = [A_PREP[:, 2 * NT * 64 + j * NT * 32:
                        2 * NT * 64 + (j + 1) * NT * 32].bitcast(BF16)
                 for j in range(3)]

        sK = TKV + 1
        xbf_kv = A_O32[:].bitcast(BF16)
        mixKA = A_MIX[:].bitcast(BF16)
        mixKB = A_WKV[:].bitcast(BF16)
        xx_kv = A_AO[:].bitcast(BF16)

        def scr(i, w=TC):
            return A_SCR[:, i * TC:i * TC + w]

        def scrk(i, w=TKV):
            return A_SCR[:, i * TKV:i * TKV + w]

        row_mu = LNROW[0:1, 0:TC]
        row_rs = LNROW[0:1, TC:2 * TC]
        row_mr = LNROW[0:1, 2 * TC:3 * TC]
        row_e2 = row_mr

        lts = []
        for n in VN:
            lts.append(sch.dma("sync", "con",
                               lambda e_, d=con[n], s_=vec_in[n]:
                               e_.dma_start(out=d, in_=s_)))
        lts.append(sch.dma("sync", "con",
                           lambda e_: e_.dma_start(out=b1s, in_=b1_in)))
        lts.append(sch.dma("sync", "con", lambda e_: e_.dma_start(
            out=gcol, in_=gate_in)))
        tok_con = tmax(*lts)

        t_ = sch.op("scalar", lambda e_: e_.activation(con["w"], con["dec"],
                                                       AF.Exp), tok_con)
        tok_w = sch.op("scalar", lambda e_: e_.activation(con["w"], con["w"],
                                                          AF.Exp,
                                                          scale=-1.0), t_)
        tok_eu = sch.op("scalar", lambda e_: e_.activation(
            con["eu"], con["tf"], AF.Exp), tok_con)
        tok_ones = sch.op("vector", lambda e_: e_.memset(ones_bf, 1.0))
        tok_onesb = sch.op("vector", lambda e_: e_.memset(ones_b, 1.0))
        tok_onesbb = sch.op("vector", lambda e_: e_.memset(ones_bb, 1.0))
        tok_onesf = sch.op("vector", lambda e_: e_.memset(ones_f, 1.0))
        tok_car0 = sch.op("vector", lambda e_: e_.memset(CARRY[:], 0.0))

        tok_qkv = [None] * NT
        tok_qrg = [None] * NT
        tok_qwo = [None] * (D // 256)

        cast_chain = [None] * 6
        cast_rr = [0]
        tok_xwbf = {}

        def cast_dma(fn):
            j = cast_rr[0] % 6
            cast_rr[0] += 1
            tok = sch.dma("gpsimd", f"cast{j}", fn, cast_chain[j])
            cast_chain[j] = tok
            return tok

        def emit_cast_x(pos0, w_):
            a0 = 0 if pos0 == 0 else pos0 + 1
            tok_xwbf[pos0] = cast_dma(
                lambda e_, a=a0, b_=pos0 + w_ + 1:
                e_.dma_start(out=xwbf_d[:, a:b_], in_=xw[:, a:b_]))

        def emit_casts_kv():
            for ct in range(NT):
                tok_qkv[ct] = cast_dma(lambda e_, c=ct: e_.dma_start(
                    out=qkv[c], in_=wkv_in[c]))

        def emit_casts_rg():
            for ct in range(NT):
                tok_qrg[ct] = cast_dma(lambda e_, c=ct: e_.dma_start(
                    out=qrg[c], in_=wrg_in[c]))

        def emit_casts_wo():
            for j in range(D // 256):
                tok_qwo[j] = cast_dma(
                    lambda e_, a=j * 256, b_=(j + 1) * 256:
                    e_.dma_start(out=qwo[:, a:b_], in_=wo_in[:, a:b_]))

        n_p1 = F // 64
        n_prep = n_p1 + 2 * NF
        DH = D // 2
        pb_free = [tmax(), tmax(), tmax()]
        st = {"thr": None, "ps4": tmax(), "ps5": tmax(),
              "bc": [tmax(), tmax()], "q1": [], "q2": [], "rows": tmax(),
              "red": None, "lhl": [tmax(), tmax()]}

        def bc_pair(bank, row, dep_row, dep_bank, slot):
            hlv = LNROW[0:1, (3 + slot) * TC:(4 + slot) * TC].bitcast(BF16)
            hi = hlv[0:1, 0:TC]
            lo = hlv[0:1, TC:2 * TC]
            t_h = sch.op("scalar", lambda e_, o=hi, a=row:
                         e_.copy(o, a),
                         tmax(dep_row, st["lhl"][slot]))
            t_b = sch.op("scalar", lambda e_, o=row_mu, a=hi:
                         e_.copy(o, a), tmax(t_h, st["lhl"][slot ^ 1]))
            t_l = sch.op("vector", lambda e_, o=lo, a=row:
                         e_.tensor_tensor(o, a, row_mu, ALU.subtract), t_b)
            sch.pe(lambda e_, p=PS[bank], r_=hi:
                   e_.matmul(p[:], ones_bb, r_, start=True, stop=False),
                   tmax(t_h, tok_onesbb, dep_bank))
            sch.pe(lambda e_, p=PS[bank], r_=lo:
                   e_.matmul(p[:], ones_bb, r_, start=False, stop=True),
                   tmax(t_l), mark=True)
            tok = sch.pemark()
            st["lhl"][slot] = tok
            return tok

        def prep_views(i, dst):
            if i < n_p1:
                return (dst.rearrange("p (c f) -> p c f", f=64),
                        w1t[:, i * 64:(i + 1) * 64].rearrange(
                            "(c p) f -> p c f", p=128))
            j, h = divmod(i - n_p1, 2)
            return (dst.rearrange("p (c f) -> p c f", f=128),
                    w2t[j * 128:(j + 1) * 128, h * DH:(h + 1) * DH]
                    .rearrange("p (c f) -> p c f", f=128))

        def emit_thr(tok_red):
            tok_rmm = sch.pe(lambda e_: e_.matmul(
                PS[4][0:1, 0:2], ones_f, racc[:, 0:2], start=True,
                stop=True), tmax(tok_red, tok_onesf, st["ps4"]), mark=True)
            trow = LNROW[0:1, 0:2]
            tok_tr = sch.op("vector", lambda e_: e_.tensor_scalar(
                trow, PS[4][0:1, 0:2], 0.7 / (D * F), None, ALU.mult),
                tok_rmm)
            tok_bc = sch.pe(lambda e_: e_.matmul(
                PS[4][:, 2:4], ones_b, trow, start=True, stop=True),
                tmax(tok_tr, tok_onesb), mark=True)
            tok_thr = sch.op("vector", lambda e_: e_.tensor_copy(
                thrs[:, 0:2], PS[4][:, 2:4]), tmax(tok_bc, tok_con))
            tok_nthr = sch.op("vector", lambda e_: e_.tensor_scalar(
                thrs[:, 2:4], thrs[:, 0:2], -1.0, None, ALU.mult), tok_thr)
            st["thr"] = tok_nthr
            st["ps4"] = tmax(tok_thr)
            st["rows"] = tmax(tok_tr)

        UNITS = [("r", i) for i in range(n_prep)] + [("t", 0)]
        QOF = len(UNITS)
        UNITS = UNITS + [("q", i) for i in range(n_prep)]
        NPU = len(UNITS)
        p_load = {}
        p_done = {}
        cur = {"l": 0, "c": 0}

        def pin_of(s):
            kind, i = UNITS[s]
            if kind == "q" and i >= n_p1:
                return (s - QOF) % 2
            return s % 3

        def emit_load(s):
            kind, i = UNITS[s]
            if kind == "t":
                p_load[s] = tmax()
                return
            b = pin_of(s)
            dv, sv = prep_views(i, pin[b])
            if kind == "q":
                dep = tmax(p_done.get(s - 2), p_done.get(s - 3),
                           p_done.get(s - 4),
                           p_done.get(s - 1)
                           if (s >= 1 and UNITS[s - 1][0] != "t"
                               and pin_of(s - 1) == pin_of(s)) else None)
            else:
                dep = tmax(p_done.get(s - 3))
            p_load[s] = sch.dma("gpsimd", f"pin{b}",
                                lambda e_, d=dv, s_=sv:
                                e_.dma_start(out=d, in_=s_), dep)

        def emit_compute(s):
            kind, i = UNITS[s]
            b = pin_of(s)
            if kind == "t":
                emit_thr(st["red"])
                p_done[s] = tmax()
                return
            if kind == "r":
                m = 0 if i < n_p1 else 1
                rdep = tmax(p_load[s], st.get("racc"))
                if m == 0:
                    tok_r = sch.op("scalar", lambda e_, s_=pin[b]:
                                   e_.activation(s_, s_, AF.Abs,
                                                 accum_out=racc[:, 2:3]),
                                   rdep)
                else:
                    tok_r = sch.op("vector", lambda e_, s_=pin[b]:
                                   e_.tensor_reduce(
                                       racc[:, 2:3], s_, AX.X, ALU.add,
                                       apply_absolute_value=True), rdep)
                if i == 0 or i == n_p1:
                    tok_a = sch.op("vector", lambda e_, m_=m:
                                   e_.tensor_copy(racc[:, m_:m_ + 1],
                                                  racc[:, 2:3]), tok_r)
                else:
                    tok_a = sch.op("vector", lambda e_, m_=m:
                                   e_.tensor_tensor(
                                       racc[:, m_:m_ + 1],
                                       racc[:, m_:m_ + 1], racc[:, 2:3],
                                       ALU.add), tok_r)
                st["racc"] = tok_a
                st["red"] = tok_a
                p_done[s] = tmax(tok_r)
                return
            m = 0 if i < n_p1 else 1
            po = pouts[i % 3]
            tok_m = sch.op("vector", lambda e_, s_=pin[b], m_=m, o=po:
                           e_.tensor_scalar(o, s_,
                                            thrs[:, 2 + m_:3 + m_], None,
                                            ALU.is_le),
                           tmax(p_load[s], st["thr"],
                                st.get(("pout", i % 3))))
            tok_q = sch.op("vector", lambda e_, s_=pin[b], m_=m, o=po:
                           e_.scalar_tensor_tensor(
                               o, s_, thrs[:, 0 + m_:1 + m_], o,
                               ALU.is_ge, ALU.subtract), tok_m)
            p_done[s] = tmax(tok_q)
            if i < n_p1:
                qdst = wq1[:, i * 64:(i + 1) * 64].rearrange(
                    "(c p) f -> p c f", p=128)
                psrc = po.rearrange("p (c f) -> p c f", f=64)
            else:
                j, h = divmod(i - n_p1, 2)
                nct = DH // 128
                qdst = wq2[h * nct:(h + 1) * nct,
                           j * 128:(j + 1) * 128, :].rearrange(
                    "c p n -> p c n")
                psrc = po.rearrange("p (c f) -> p c f", f=128)
            tok_wq = sch.dma("gpsimd", f"qw{i % 3}",
                             lambda e_, d=qdst, s_=psrc:
                             e_.dma_start(out=d, in_=s_), tok_q)
            st[("pout", i % 3)] = tok_wq
            (st["q1"] if i < n_p1 else st["q2"]).append(tok_wq)

        n_ticks = (n_kv + 2 * n_full) * NT + n_full * (NT // 2 + NT)
        ATTN_LIM = QOF + n_p1
        per_tick = -(-(ATTN_LIM + 3) // max(1, n_ticks - 2 * NT))

        def prep_tick(k=None, limit=None, thr_ok=True):
            if no_prep:
                return
            lim = min(NPU, ATTN_LIM if limit is None else limit)
            k = per_tick if k is None else k
            for _ in range(k):
                ahead = 2 if cur["c"] >= QOF + n_p1 else 3
                while cur["l"] < min(lim, cur["c"] + ahead):
                    emit_load(cur["l"])
                    cur["l"] += 1
                if cur["c"] < lim:
                    if not thr_ok and UNITS[cur["c"]][0] == "t":
                        return
                    emit_compute(cur["c"])
                    cur["c"] += 1

        def q_run_to(n_q):
            lim = min(NPU, QOF + n_q)
            prep_tick(k=max(0, lim - cur["c"]), limit=lim)

        def q_front():
            return tmax(st.get(("pout", 0)), st.get(("pout", 1)),
                        st.get(("pout", 2)))

        bank_free = [tmax(), tmax(), tmax(), tmax()]
        bank_i = [0]

        def next_bank():
            b = bank_i[0] % 4
            bank_i[0] += 1
            return b

        slab_free = [tmax(), tmax()]
        slab_i = [0]

        def next_slab():
            b = slab_i[0] % 2
            slab_i[0] += 1
            return b

        def proj_group(lhs_fn, rhs_fn, bank, dep0, cj_dep=None):
            for cj in range(NT):
                d = dep0 if cj == 0 else ()
                if cj_dep is not None:
                    d = tmax(d, cj_dep(cj))
                sch.pe(lambda e_, p=PS[bank], l_=lhs_fn(cj), r_=rhs_fn(cj),
                       s0=(cj == 0), s1=(cj == NT - 1):
                       e_.matmul(p[:], l_, r_, start=s0, stop=s1),
                       d, mark=(cj == NT - 1))
            return sch.pemark()

        scrf = {}
        tok_carry = tmax(tok_car0)
        xbf_free = tmax()
        mix_free = tmax()
        wkv_free = tmax()
        o32_free = tmax()
        ao_free = tmax()
        xres_st = []

        for ci in range(n_kv):
            pos0 = ci * TKV

            xsrc = xw[:, pos0:pos0 + TKV + 1].rearrange("(c p) t -> p c t",
                                                        p=128)
            xdst = xbf_kv[:, 0:NT * sK].rearrange("p (c t) -> p c t", t=sK)
            if ci == 0:
                emit_cast_x(0, TKV)
                emit_casts_kv()
                for cc in range(1, n_kv):
                    emit_cast_x(cc * TKV, TKV)
                for cc in range(n_full):
                    emit_cast_x(n_kv * TKV + cc * TC, TC)
            if ci == max(0, n_kv - 2) and ci > 0:
                emit_casts_rg()
            xsrcb = xwbf_d[:, pos0:pos0 + TKV + 1].rearrange(
                "(c p) t -> p c t", p=128)
            tok_x = sch.dma("sync", "xch", lambda e_, d=xdst, s_=xsrcb:
                            e_.dma_start(out=d, in_=s_),
                            tmax(o32_free, st.get("xch"), tok_xwbf[pos0]))
            st["xch"] = tok_x

            mk_toks, mv_toks, xx_toks = [], [], []
            for cj in range(NT):
                xxs = xx_kv[:, (cj % 2) * TKV:(cj % 2) * TKV + TKV]
                prev_v = xbf_kv[:, cj * sK:cj * sK + TKV]
                x_v = xbf_kv[:, cj * sK + 1:cj * sK + 1 + TKV]
                t_xx = sch.op("vector", lambda e_, o=xxs, a=prev_v, b_=x_v:
                              e_.tensor_tensor(o, a, b_, ALU.subtract),
                              tmax(tok_x, scrf.get(("xxk", cj % 2)),
                                   ao_free if cj < 2 else ()))
                mk = mixKA[:, cj * TKV:(cj + 1) * TKV]
                mv = mixKB[:, cj * TKV:(cj + 1) * TKV]
                t_mk = sch.op("vector", lambda e_, o=mk, a=xxs, c=cj, b_=x_v:
                              e_.scalar_tensor_tensor(
                                  o, a, con["tmk"][:, c:c + 1], b_,
                                  ALU.mult, ALU.add),
                              tmax(t_xx, tok_con,
                                   mix_free if cj == 0 else ()))
                t_mv = sch.op("vector", lambda e_, o=mv, a=xxs, c=cj,
                              b_=x_v:
                              e_.scalar_tensor_tensor(
                                  o, a, con["tmv"][:, c:c + 1], b_,
                                  ALU.mult, ALU.add),
                              tmax(t_xx, tok_con,
                                   wkv_free if cj == 0 else ()))
                scrf[("xxk", cj % 2)] = tmax(t_mk, t_mv)
                mk_toks.append(t_mk)
                mv_toks.append(t_mv)
                xx_toks.append(t_xx)
            tok_mixk = tmax(*mk_toks)
            tok_mixv = tmax(*mv_toks)
            mix_use = []
            carry_next = []

            for ct in range(NT):
                prep_tick(2, thr_ok=False)
                sb = next_slab()
                sdst = slabs[sb].rearrange("p (c n) -> p c n", n=256)
                tok_sl = sch.dma("sync", f"slab{sb}",
                                 lambda e_, d=sdst,
                                 s_=qkv[ct].rearrange("(c p) n -> p c n",
                                                      p=128):
                                 e_.dma_start(out=d, in_=s_),
                                 tmax(slab_free[sb], tok_qkv[ct]))
                slab = slabs[sb]
                ek = scrk(0 + ct % 2)
                z = scrk(2 + ct % 2)
                sa = scrk(4)
                sb_ = scrk(5)
                ek_toks, z_toks = [], []
                for hf in range(TKV // 512):
                    bk = next_bank()
                    g1 = proj_group(
                        lambda cj: slab[:, cj * 256:cj * 256 + 128],
                        lambda cj, h=hf: mixKA[:, cj * TKV + h * 512:
                                               cj * TKV + h * 512 + 512],
                        bk, tmax(tok_sl, bank_free[bk]),
                        cj_dep=lambda cj: mk_toks[cj])
                    t_ek = sch.op("scalar",
                                  lambda e_, o=ek, h=hf, p=PS[bk]:
                                  e_.activation(o[:, h * 512:h * 512 + 512],
                                                p[:], AF.Exp),
                                  tmax(g1, scrf.get(("ekk", ct % 2))
                                       if hf == 0 else ()))
                    bank_free[bk] = tmax(t_ek)
                    ek_toks.append(t_ek)
                    bv = next_bank()
                    g2 = proj_group(
                        lambda cj: slab[:, cj * 256 + 128:cj * 256 + 256],
                        lambda cj, h=hf: mixKB[:, cj * TKV + h * 512:
                                               cj * TKV + h * 512 + 512],
                        bv, tmax(tok_sl, bank_free[bv]),
                        cj_dep=lambda cj: mv_toks[cj])
                    t_zc = sch.op("scalar",
                                  lambda e_, o=z, h=hf, p=PS[bv]:
                                  e_.copy(o[:, h * 512:h * 512 + 512],
                                          p[:]),
                                  tmax(g2, scrf.get(("zk", ct % 2))
                                       if hf == 0 else ()))
                    bank_free[bv] = tmax(t_zc)
                    t_z = sch.op("gpsimd",
                                 lambda e_, o=z, h=hf, a=ek:
                                 e_.tensor_tensor(
                                     o[:, h * 512:h * 512 + 512],
                                     o[:, h * 512:h * 512 + 512],
                                     a[:, h * 512:h * 512 + 512],
                                     ALU.mult),
                                 tmax(t_zc, ek_toks[hf]))
                    z_toks.append(t_z)
                    mix_use.append(tmax(g1, g2))
                slab_free[sb] = mix_use[-1]
                tok_sa = sch.op("vector",
                                lambda e_, o=sa, c=ct, d1=z:
                                e_.tensor_tensor_scan(
                                    o, con["w"][:, c:c + 1].broadcast_to(
                                        [128, TKV]), d1, car_a[:, c:c + 1],
                                    ALU.mult, ALU.add),
                                tmax(z_toks, tok_w, tok_carry,
                                     scrf.get("sak")))
                tok_sb = sch.op("vector",
                                lambda e_, o=sb_, c=ct, d1=ek:
                                e_.tensor_tensor_scan(
                                    o, con["w"][:, c:c + 1].broadcast_to(
                                        [128, TKV]), d1, car_b[:, c:c + 1],
                                    ALU.mult, ALU.add),
                                tmax(ek_toks, tok_w, tok_carry,
                                     scrf.get("sbk")))
                gated = False
                t_ca = sch.op("vector", lambda e_, c=ct, s_=sa:
                              e_.tensor_copy(car_a[:, c:c + 1],
                                             s_[:, TKV - 1:TKV]), tok_sa)
                t_cb = sch.op("vector", lambda e_, c=ct, s_=sb_:
                              e_.tensor_copy(car_b[:, c:c + 1],
                                             s_[:, TKV - 1:TKV]), tok_sb)
                carry_next.append(tmax(t_ca, t_cb))
                scrf["sak"] = tmax(t_ca)
                scrf["sbk"] = tmax(t_cb)
                scrf[("ekk", ct % 2)] = tmax(tok_sb, *z_toks)
                scrf[("zk", ct % 2)] = tmax(tok_sa)
            tok_carry = tmax(*carry_next)
            if ci == n_kv - 1:
                t_ga = sch.op("vector", lambda e_: e_.tensor_scalar(
                    car_a[:], car_a[:], gcol, None, ALU.mult),
                    tmax(tok_carry, tok_con))
                t_gb = sch.op("vector", lambda e_: e_.tensor_scalar(
                    car_b[:], car_b[:], gcol, None, ALU.mult),
                    tmax(tok_carry, tok_con))
                tok_carry = tmax(t_ga, t_gb)
            mix_free = tmax(*mix_use)
            wkv_free = tmax(*mix_use)
            o32_free = tmax(*(xx_toks + mk_toks + mv_toks))
            ao_free = tmax(*(mk_toks + mv_toks))

        if debug_stop == "kv":
            t_dbg = sch.dma("sync", "y", lambda e_: e_.dma_start(
                out=y_out[:, 0:M], in_=xw[:, 0:M]),
                tmax(tok_con, tok_carry))
            sch.wait("sync", t_dbg)
            emit_all(block, sch, sems)
            return nc

        for fci in range(n_full):
            pos0 = n_kv * TKV + fci * TC

            xsrc = xw[:, pos0:pos0 + TC + 1].rearrange("(c p) t -> p c t",
                                                       p=128)
            xdst = xbf[:, 0:NT * sY].rearrange("p (c t) -> p c t", t=sY)
            if tok_qrg[0] is None:
                emit_casts_rg()
            if fci == 0:
                emit_casts_wo()
            xsrcb = xwbf_d[:, pos0:pos0 + TC + 1].rearrange(
                "(c p) t -> p c t", p=128)
            tok_x = sch.dma("sync", "xch", lambda e_, d=xdst, s_=xsrcb:
                            e_.dma_start(out=d, in_=s_),
                            tmax(xbf_free, st.get("xch"), tok_xwbf[pos0]))
            st["xch"] = tok_x

            mk_toks, mv_toks, xx_toks = [], [], []
            for cj in range(NT):
                xxs = scr(10).bitcast(BF16)[:, (cj % 2) * TC:
                                            (cj % 2) * TC + TC]
                prev_v = xbf[:, cj * sY:cj * sY + TC]
                x_v = xbf[:, cj * sY + 1:cj * sY + 1 + TC]
                t_xx = sch.op("vector", lambda e_, o=xxs, a=prev_v, b_=x_v:
                              e_.tensor_tensor(o, a, b_, ALU.subtract),
                              tmax(tok_x, scrf.get(("xx", cj % 2)),
                                   scrf.get(("osq", cj % 2))))
                mk = mixA[:, cj * TC:(cj + 1) * TC]
                mv = mixB[:, cj * TC:(cj + 1) * TC]
                t_mk = sch.op("vector", lambda e_, o=mk, a=xxs, c=cj, b_=x_v:
                              e_.scalar_tensor_tensor(
                                  o, a, con["tmk"][:, c:c + 1], b_,
                                  ALU.mult, ALU.add),
                              tmax(t_xx, tok_con,
                                   mix_free if cj == 0 else ()))
                t_mv = sch.op("vector", lambda e_, o=mv, a=xxs, c=cj,
                              b_=x_v:
                              e_.scalar_tensor_tensor(
                                  o, a, con["tmv"][:, c:c + 1], b_,
                                  ALU.mult, ALU.add),
                              tmax(t_xx, tok_con,
                                   mix_free if cj == 0 else ()))
                scrf[("xx", cj % 2)] = tmax(t_mk, t_mv)
                mk_toks.append(t_mk)
                mv_toks.append(t_mv)
                xx_toks.append(t_xx)
            tok_mixk = tmax(*mk_toks)
            tok_mixv = tmax(*mv_toks)

            wkvr_toks = [None] * NT
            kv_use = []
            carry_next = []
            for ct in range(NT):
                prep_tick(2)
                sb = next_slab()
                sdst = slabs[sb].rearrange("p (c n) -> p c n", n=256)
                tok_sl = sch.dma("sync", f"slab{sb}",
                                 lambda e_, d=sdst,
                                 s_=qkv[ct].rearrange("(c p) n -> p c n",
                                                      p=128):
                                 e_.dma_start(out=d, in_=s_),
                                 tmax(slab_free[sb], tok_qkv[ct]))
                slab = slabs[sb]

                bk = next_bank()
                gk = proj_group(lambda cj: slab[:, cj * 256:cj * 256 + 128],
                                lambda cj: mixA[:, cj * TC:(cj + 1) * TC],
                                bk, tmax(tok_sl, bank_free[bk]),
                                cj_dep=lambda cj: mk_toks[cj])
                ek = scr(0 + ct % 2)
                tok_ek = sch.op("scalar", lambda e_, o=ek, p=PS[bk]:
                                e_.activation(o, p[:], AF.Exp),
                                tmax(gk, scrf.get(("ek", ct % 2))))
                bank_free[bk] = tmax(tok_ek)

                bv = next_bank()
                gv = proj_group(lambda cj: slab[:, cj * 256 + 128:
                                                cj * 256 + 256],
                                lambda cj: mixB[:, cj * TC:(cj + 1) * TC],
                                bv, tmax(tok_sl, bank_free[bv]),
                                cj_dep=lambda cj: mv_toks[cj])
                slab_free[sb] = tmax(gk, gv)
                kv_use.append(tmax(gk, gv))

                z = scr(2 + ct % 2)
                tok_zc = sch.op("scalar", lambda e_, o=z, p=PS[bv]:
                                e_.copy(o, p[:]),
                                tmax(gv, scrf.get(("z", ct % 2))))
                bank_free[bv] = tmax(tok_zc)
                tok_z = sch.op("vector", lambda e_, o=z, a=ek:
                               e_.tensor_tensor(o, o, a, ALU.mult),
                               tmax(tok_zc, tok_ek))

                sa = scr(4 + ct % 2)
                sb_ = scr(6 + ct % 2)
                tok_sa = sch.op("vector", lambda e_, o=sa, c=ct, d1=z:
                                e_.tensor_tensor_scan(
                                    o, con["w"][:, c:c + 1].broadcast_to(
                                        [128, TC]), d1, car_a[:, c:c + 1],
                                    ALU.mult, ALU.add),
                                tmax(tok_z, tok_w, tok_carry,
                                     scrf.get(("sa", ct % 2))))
                tok_sb = sch.op("vector", lambda e_, o=sb_, c=ct, d1=ek:
                                e_.tensor_tensor_scan(
                                    o, con["w"][:, c:c + 1].broadcast_to(
                                        [128, TC]), d1, car_b[:, c:c + 1],
                                    ALU.mult, ALU.add),
                                tmax(tok_ek, tok_w, tok_carry,
                                     scrf.get(("sb", ct % 2))))
                num = scr(8)
                den = scr(9)
                t_nb = sch.op("vector",
                              lambda e_, o=num, zz=z, s_=sa, c=ct:
                              e_.scalar_tensor_tensor(
                                  o[:, 1:TC], zz[:, 1:TC],
                                  con["eu"][:, c:c + 1], s_[:, 0:TC - 1],
                                  ALU.mult, ALU.add),
                              tmax(tok_sa, tok_eu,
                                   scrf.get("num")))
                t_nt = sch.op("vector",
                              lambda e_, o=num, zz=z, c=ct:
                              e_.scalar_tensor_tensor(
                                  o[:, 0:1], zz[:, 0:1],
                                  con["eu"][:, c:c + 1], car_a[:, c:c + 1],
                                  ALU.mult, ALU.add),
                              tmax(tok_z, tok_eu, tok_carry,
                                   scrf.get("num")))
                t_d1 = sch.op("scalar",
                              lambda e_, o=den, ee=ek, c=ct:
                              e_.mul(o[:, 1:TC], ee[:, 1:TC],
                                     con["eu"][:, c:c + 1]),
                              tmax(tok_ek, tok_eu,
                                   scrf.get("den")))
                t_d2 = sch.op("gpsimd",
                              lambda e_, o=den, s_=sb_:
                              e_.tensor_tensor(o[:, 1:TC], o[:, 1:TC],
                                               s_[:, 0:TC - 1], ALU.add),
                              tmax(t_d1, tok_sb))
                t_dt = sch.op("vector",
                              lambda e_, o=den, ee=ek, c=ct:
                              e_.scalar_tensor_tensor(
                                  o[:, 0:1], ee[:, 0:1],
                                  con["eu"][:, c:c + 1], car_b[:, c:c + 1],
                                  ALU.mult, ALU.add),
                              tmax(tok_ek, tok_eu, tok_carry,
                                   scrf.get("den")))
                t_ca = sch.op("vector", lambda e_, c=ct, s_=sa:
                              e_.tensor_copy(car_a[:, c:c + 1],
                                             s_[:, TC - 1:TC]),
                              tmax(tok_sa, t_nt))
                t_cb = sch.op("vector", lambda e_, c=ct, s_=sb_:
                              e_.tensor_copy(car_b[:, c:c + 1],
                                             s_[:, TC - 1:TC]),
                              tmax(tok_sb, t_dt))
                carry_next.append(tmax(t_ca, t_cb))
                t_l = sch.op("scalar", lambda e_, o=den:
                             e_.activation(o, o, AF.Ln), tmax(t_d2, t_dt))
                t_e = sch.op("scalar", lambda e_, o=den:
                             e_.activation(o, o, AF.Exp, scale=-1.0), t_l)
                wv_ = wkvr[:, ct * TC:(ct + 1) * TC]
                t_wk = sch.op("gpsimd", lambda e_, o=wv_, a=num, b_=den:
                              e_.tensor_tensor(o, a, b_, ALU.mult),
                              tmax(t_nb, t_nt, t_e,
                                   wkv_free if ct == 0 else ()))
                wkvr_toks[ct] = t_wk
                scrf["num"] = tmax(t_wk)
                scrf["den"] = tmax(t_wk)
                scrf[("sa", ct % 2)] = tmax(t_nb, t_ca)
                scrf[("sb", ct % 2)] = tmax(t_d2, t_cb)
                scrf[("ek", ct % 2)] = tmax(t_d1, t_dt, tok_sb, tok_z)
                scrf[("z", ct % 2)] = tmax(t_nb, t_nt, tok_sa)
            tok_carry = tmax(*carry_next)
            kv_mix_done = tmax(*kv_use)
            if full_upto == "kv":
                mix_free = kv_mix_done
                wkv_free = tmax(*[t for t in wkvr_toks if t])
                xbf_free = tmax(*(xx_toks + mk_toks + mv_toks))
                o32_free = tmax()
                ao_free = tmax()
                continue

            mr_toks, mg_toks = [], []
            for cj in range(NT):
                xxs = scr(10).bitcast(BF16)[:, (cj % 2) * TC:
                                            (cj % 2) * TC + TC]
                prev_v = xbf[:, cj * sY:cj * sY + TC]
                x_v = xbf[:, cj * sY + 1:cj * sY + 1 + TC]
                t_xx = sch.op("vector", lambda e_, o=xxs, a=prev_v, b_=x_v:
                              e_.tensor_tensor(o, a, b_, ALU.subtract),
                              tmax(tok_x, scrf.get(("xx", cj % 2)),
                                   scrf.get(("osq", cj % 2))))
                mr = mixA[:, cj * TC:(cj + 1) * TC]
                mg = mixB[:, cj * TC:(cj + 1) * TC]
                t_mr = sch.op("vector", lambda e_, o=mr, a=xxs, c=cj, b_=x_v:
                              e_.scalar_tensor_tensor(
                                  o, a, con["tmr"][:, c:c + 1], b_,
                                  ALU.mult, ALU.add),
                              tmax(t_xx, tok_con,
                                   kv_mix_done if cj == 0 else ()))
                t_mg = sch.op("vector", lambda e_, o=mg, a=xxs, c=cj,
                              b_=x_v:
                              e_.scalar_tensor_tensor(
                                  o, a, con["tmg"][:, c:c + 1], b_,
                                  ALU.mult, ALU.add),
                              tmax(t_xx, tok_con,
                                   kv_mix_done if cj == 0 else ()))
                scrf[("xx", cj % 2)] = tmax(t_mr, t_mg)
                mr_toks.append(t_mr)
                mg_toks.append(t_mg)
            tok_mixr = tmax(*mr_toks)
            tok_mixg = tmax(*mg_toks)
            xbf_free = tmax(*(xx_toks + mr_toks + mg_toks))

            ao_toks = [None] * NT
            rg_use = []
            for ct in range(NT):
                prep_tick()
                sb = next_slab()
                sdst = slabs[sb].rearrange("p (c n) -> p c n", n=256)
                tok_sl = sch.dma("sync", f"slab{sb}",
                                 lambda e_, d=sdst,
                                 s_=qrg[ct].rearrange("(c p) n -> p c n",
                                                      p=128):
                                 e_.dma_start(out=d, in_=s_),
                                 tmax(slab_free[sb], tok_qrg[ct]))
                slab = slabs[sb]
                br = next_bank()
                gr = proj_group(lambda cj: slab[:, cj * 256:cj * 256 + 128],
                                lambda cj: mixA[:, cj * TC:(cj + 1) * TC],
                                br, tmax(tok_sl, bank_free[br]),
                                cj_dep=lambda cj: mr_toks[cj])
                sigr = scr(11)
                t_sr = sch.op("scalar", lambda e_, o=sigr, p=PS[br]:
                              e_.activation(o, p[:], AF.Sigmoid),
                              tmax(gr, scrf.get("sigr")))
                bank_free[br] = tmax(t_sr)
                bg = next_bank()
                gg = proj_group(lambda cj: slab[:, cj * 256 + 128:
                                                cj * 256 + 256],
                                lambda cj: mixB[:, cj * TC:(cj + 1) * TC],
                                bg, tmax(tok_sl, bank_free[bg]),
                                cj_dep=lambda cj: mg_toks[cj])
                slab_free[sb] = tmax(gr, gg)
                rg_use.append(tmax(gr, gg))
                sigg = scr(12)
                t_sg = sch.op("scalar", lambda e_, o=sigg, p=PS[bg]:
                              e_.activation(o, p[:], AF.Sigmoid),
                              tmax(gg, scrf.get("sigg")))
                bank_free[bg] = tmax(t_sg)
                t_p = sch.op("vector", lambda e_, a=sigr, b_=sigg:
                             e_.tensor_tensor(a, a, b_, ALU.mult),
                             tmax(t_sr, t_sg))
                scrf["sigg"] = tmax(t_p)
                aov = aobf[:, ct * TC:(ct + 1) * TC]
                t_ao = sch.op("gpsimd", lambda e_, o=aov, a=sigr, c=ct:
                              e_.tensor_tensor(
                                  o, wkvr[:, c * TC:(c + 1) * TC], a,
                                  ALU.mult),
                              tmax(t_p, wkvr_toks[ct],
                                   ao_free if ct == 0 else ()))
                scrf["sigr"] = tmax(t_ao)
                ao_toks[ct] = t_ao
            tok_ao_all = tmax(*ao_toks)
            rg_mm_done = tmax(*rg_use)

            if full_upto == "rg":
                mix_free = rg_mm_done
                wkv_free = tmax(*[t for t in wkvr_toks if t])
                ao_free = tok_ao_all
                o32_free = tmax()
                continue

            x32src = xw[:, pos0 + 1:pos0 + TC + 1].rearrange(
                "(c p) t -> p c t", p=128)
            tok_x32 = sch.dma("scalar", "x32", lambda e_, s_=x32src:
                              e_.dma_start(out=x32.rearrange(
                                  "p (c t) -> p c t", t=TC), in_=s_),
                              tmax(rg_mm_done, st.get("x32")))
            st["x32"] = tok_x32

            o32_toks = [None] * NT
            tok_statmm = tok_statmq = None
            for ct2 in range(0, NT, 2):
                prep_tick()
                sb = next_slab()
                sdst = slabs[sb].rearrange("p (c n) -> p c n", n=256)
                tok_sl = sch.dma("sync", f"slab{sb}",
                                 lambda e_, d=sdst,
                                 s_=qwo[:, ct2 * 128:(ct2 + 2) * 128]
                                 .rearrange("(c p) n -> p c n", p=128):
                                 e_.dma_start(out=d, in_=s_),
                                 tmax(slab_free[sb], tok_qwo[ct2 // 2]))
                slab = slabs[sb]
                for ctl in range(2):
                    ct = ct2 + ctl
                    bo = next_bank()
                    go = proj_group(
                        lambda cj, l_=ctl: slab[:, cj * 256 + l_ * 128:
                                                cj * 256 + l_ * 128 + 128],
                        lambda cj: aobf[:, cj * TC:(cj + 1) * TC],
                        bo, tmax(tok_sl, bank_free[bo]),
                        cj_dep=lambda cj: ao_toks[cj])
                    ov = o32[:, ct * TC:(ct + 1) * TC]
                    t_o = sch.op("scalar", lambda e_, o=ov, p=PS[bo]:
                                 e_.copy(o, p[:]),
                                 tmax(go, o32_free if ct == 0 else ()))
                    obv = scr(13).bitcast(BF16)[:, (ct % 2) * TC:
                                                (ct % 2) * TC + TC]
                    if wo_level >= 2:
                        t_ob = sch.op("vector", lambda e_, o=obv, s_=ov:
                                      e_.tensor_copy(o, s_),
                                      tmax(t_o, scrf.get(("obf", ct % 2))))
                    else:
                        t_ob = t_o
                    bank_free[bo] = tmax(t_o)
                    o32_toks[ct] = t_o
                    sqv = scr(10).bitcast(BF16)[:, (ct % 2) * TC:
                                                (ct % 2) * TC + TC]
                    if wo_level >= 3:
                        t_sq = sch.op("scalar", lambda e_, o=sqv, a=obv:
                                      e_.activation(o, a, AF.Square),
                                      tmax(t_ob, scrf.get(("xx", ct % 2)),
                                           scrf.get(("osq", ct % 2))))
                    else:
                        t_sq = t_ob
                    if wo_level >= 4:
                        t_ms = sch.pe(lambda e_, r_=obv, s0=(ct == 0),
                                      s1=(ct == NT - 1):
                                      e_.matmul(PS[4][0:1, 0:TC], ones_bf,
                                                r_, start=s0, stop=s1),
                                      tmax(t_ob, tok_ones,
                                           st["ps4"] if ct == 0 else ()),
                                      mark=True)
                        t_mq = sch.pe(lambda e_, r_=sqv, s0=(ct == 0),
                                      s1=(ct == NT - 1):
                                      e_.matmul(PS[5][0:1, 0:TC], ones_bf,
                                                r_, start=s0, stop=s1),
                                      tmax(t_sq, tok_ones,
                                           st["ps5"] if ct == 0 else ()),
                                      mark=True)
                        scrf[("obf", ct % 2)] = tmax(t_ms)
                        scrf[("osq", ct % 2)] = tmax(t_mq)
                        if ct == NT - 1:
                            tok_statmm, tok_statmq = t_ms, t_mq
                    else:
                        scrf[("obf", ct % 2)] = tmax(t_ob)
                        scrf[("osq", ct % 2)] = tmax(t_sq)
                        tok_statmm = tok_statmq = None
                slab_free[sb] = sch.pemark()
            ao_free = sch.pemark()

            if full_upto == "wo":
                mix_free = tmax(rg_mm_done, tok_x32)
                wkv_free = tmax(*[t for t in wkvr_toks if t])
                o32_free = tmax(*[t for t in o32_toks if t])
                if wo_level >= 4:
                    t_mu = sch.op("vector", lambda e_: e_.tensor_scalar(
                        row_mu, PS[4][0:1, 0:TC], 1.0 / D, None, ALU.mult),
                        tmax(tok_statmm, st["rows"]))
                    t_e2 = sch.op("vector", lambda e_: e_.tensor_scalar(
                        row_e2, PS[5][0:1, 0:TC], 1.0 / D, None, ALU.mult),
                        tmax(tok_statmq, st["rows"]))
                    st["ps4"] = tmax(t_mu)
                    st["ps5"] = tmax(t_e2)
                    st["rows"] = tmax(t_mu, t_e2)
                continue

            def ln_rows(tok_s, tok_q):
                t_mu = sch.op("vector", lambda e_: e_.tensor_scalar(
                    row_mu, PS[4][0:1, 0:TC], 1.0 / D, None, ALU.mult),
                    tmax(tok_s, st["rows"]))
                t_e2 = sch.op("vector", lambda e_: e_.tensor_scalar(
                    row_e2, PS[5][0:1, 0:TC], 1.0 / D, None, ALU.mult),
                    tmax(tok_q, st["rows"]))
                st["ps4"] = tmax(t_mu)
                st["ps5"] = tmax(t_e2)
                t_m2 = sch.op("vector", lambda e_: e_.tensor_tensor(
                    row_rs, row_mu, row_mu, ALU.mult), t_mu)
                t_var = sch.op("vector", lambda e_: e_.scalar_tensor_tensor(
                    row_rs, row_e2, EPS, row_rs, ALU.add, ALU.subtract),
                    tmax(t_e2, t_m2))
                t_ln = sch.op("scalar", lambda e_: e_.activation(
                    row_rs, row_rs, AF.Ln), t_var)
                t_rs = sch.op("scalar", lambda e_: e_.activation(
                    row_rs, row_rs, AF.Exp, scale=-0.5), t_ln)
                t_mr = sch.op("vector", lambda e_: e_.tensor_tensor(
                    row_mr, row_mu, row_rs, ALU.mult), tmax(t_rs, t_mu))
                return t_rs, t_mr, t_mu

            t_rs, t_mr, t_mu0 = ln_rows(tok_statmm, tok_statmq)
            t_p6 = bc_pair(6, row_rs, t_rs, st["bc"][0], 0)
            t_p7 = bc_pair(7, row_mr, t_mr, st["bc"][1], 1)

            bc_toks = [t_p6, t_p7]
            res_toks = [None] * NT
            rb_toks = []
            tts, tqs = [], []
            stat2s = stat2q = None
            tvs = []
            for ct in range(NT):
                prep_tick(1)
                uv = scr(4 + ct % 2)
                t_u = sch.op("scalar", lambda e_, o=uv, c=ct:
                             e_.mul(o, o32[:, c * TC:(c + 1) * TC],
                                    con["lng"][:, c:c + 1]),
                             tmax(o32_toks[ct], tok_con,
                                  scrf.get(("sa", ct % 2))))
                tv = scr(0 + ct % 2)
                t_t = sch.op("vector", lambda e_, o=tv, a=uv:
                             e_.tensor_tensor(o, a, PS[6][:], ALU.mult),
                             tmax(t_p6, t_u, scrf.get(("ek", ct % 2))))
                vv = scr(6 + ct % 2)
                t_v = sch.op("scalar", lambda e_, o=vv, c=ct:
                             e_.mul(o, PS[7][:], con["lng"][:, c:c + 1]),
                             tmax(t_p7, tok_con,
                                  scrf.get(("sb", ct % 2))))
                tvs.append(t_v)
                qv = scr(2 + ct % 2)
                t_q = sch.op("gpsimd", lambda e_, o=qv, a=vv, c=ct:
                             e_.tensor_tensor(
                                 o, a, x32[:, c * TC:(c + 1) * TC],
                                 ALU.subtract),
                             tmax(t_v, tok_x32,
                                  scrf.get(("z", ct % 2))))
                rv = resf[:, ct * TC:(ct + 1) * TC]
                t_res = sch.op("vector", lambda e_, o=rv, a=tv, b_=qv, c=ct:
                               e_.scalar_tensor_tensor(
                                   o, a, con["lnb"][:, c:c + 1], b_,
                                   ALU.add, ALU.subtract),
                               tmax(t_t, t_q, tok_con))
                tts.append(t_t)
                tqs.append(t_q)
                scrf[("ek", ct % 2)] = tmax(t_res)
                scrf[("z", ct % 2)] = tmax(t_res)
                scrf[("sa", ct % 2)] = tmax(t_t)
                scrf[("sb", ct % 2)] = tmax(t_q)
                res_toks[ct] = t_res
                rbv = resbf[:, ct * TC:(ct + 1) * TC]
                t_rb = sch.op("vector", lambda e_, o=rbv, a=rv:
                              e_.tensor_copy(o, a),
                              tmax(t_res, ao_free if ct == 0 else ()))
                sqv = scr(10).bitcast(BF16)[:, (ct % 2) * TC:
                                            (ct % 2) * TC + TC]
                t_sq = sch.op("scalar", lambda e_, o=sqv, a=rbv:
                              e_.activation(o, a, AF.Square),
                              tmax(t_rb, scrf.get(("osq", ct % 2))))
                t_ms = sch.pe(lambda e_, r_=rbv, s0=(ct == 0),
                              s1=(ct == NT - 1):
                              e_.matmul(PS[4][0:1, 0:TC], ones_bf, r_,
                                        start=s0, stop=s1),
                              tmax(t_rb, tok_ones,
                                   st["ps4"] if ct == 0 else ()),
                              mark=True)
                t_mq = sch.pe(lambda e_, r_=sqv, s0=(ct == 0),
                              s1=(ct == NT - 1):
                              e_.matmul(PS[5][0:1, 0:TC], ones_bf, r_,
                                        start=s0, stop=s1),
                              tmax(t_sq, tok_ones,
                                   st["ps5"] if ct == 0 else ()),
                              mark=True)
                scrf[("osq", ct % 2)] = tmax(t_mq)
                rb_toks.append(t_ms)
                if ct == NT - 1:
                    stat2s, stat2q = t_ms, t_mq

            st["bc"][0] = tmax(*tts)
            st["bc"][1] = tmax(*tvs)

            if full_upto == "innerln":
                mix_free = tmax(rg_mm_done, *tqs)
                wkv_free = tmax(*res_toks)
                o32_free = tmax(*[t for t in o32_toks if t])
                ao_free = tmax(*rb_toks)
                st["rows"] = tmax(st["rows"], bc_toks)
                continue

            st["rows"] = tmax(st["rows"], bc_toks)
            t_rs, t_mr, _ = ln_rows(stat2s, stat2q)
            t_b6 = bc_pair(6, row_rs, t_rs, st["bc"][0], 0)
            t_b7 = bc_pair(7, row_mr, t_mr, st["bc"][1], 1)
            st["rows"] = tmax(st["rows"], t_b6, t_b7)
            res3 = resf.rearrange("p (c t) -> p c t", t=TC)
            o323 = o32.rearrange("p (c t) -> p c t", t=TC)
            ps6b = PS[6][:].rearrange("p (a t) -> p a t", a=1).broadcast_to(
                [128, NT, TC])
            ps7b = PS[7][:].rearrange("p (a t) -> p a t", a=1).broadcast_to(
                [128, NT, TC])
            t_t2 = sch.op("vector", lambda e_: e_.tensor_tensor(
                o323, res3, ps6b, ALU.mult),
                tmax(t_b6, *res_toks))
            t_xr = sch.op("vector", lambda e_: e_.tensor_tensor(
                res3, o323, ps7b, ALU.subtract), tmax(t_b7, t_t2))
            st["bc"][0] = tmax(t_t2)
            st["bc"][1] = tmax(t_xr)
            o32_free = tmax(t_xr)
            t_st = sch.dma("scalar", "xrst", lambda e_, o=fci: e_.dma_start(
                out=xres_d[:, o * TC:(o + 1) * TC].rearrange(
                    "(c p) t -> p c t", p=128), in_=res3),
                tmax(t_xr, st.get("xrst")))
            st["xrst"] = t_st
            xres_st.append(t_st)
            wkv_free = tmax(t_st, t_xr)
            ao_free = tmax(*rb_toks)
            mix_free = tmax(rg_mm_done, *tqs)

        tok_attn = tmax(wkv_free, o32_free, ao_free, mix_free, xbf_free,
                        tok_carry, *xres_st)


        if debug_stop == "attn":
            t_dbg = sch.dma("sync", "y", lambda e_: e_.dma_start(
                out=y_out[:, 0:M], in_=xres_d[:, 0:M]),
                tmax(tok_attn, st.get("rows"), st.get("ps4"), st.get("ps5")))
            sch.wait("sync", t_dbg)
            emit_all(block, sch, sems)
            return nc


        xresbf = A_MIX[:, 0:NT * TFF // 2].bitcast(BF16)
        h_ap = [A_WKV[:].bitcast(BF16), A_O32[:, 0:NT * TC].bitcast(BF16)]
        accq_ap = [A_XBF[:, 0:NT * 256], A_AO[:, 0:NT * 256],
                   A_SCR[:, 0:NT * 256], A_SCR[:, NT * 256:2 * NT * 256]]
        xq32s = [A_O32[:, 0:NT * 256], A_O32[:, NT * 256:2 * NT * 256]]
        resbf2 = A_WKV[:].bitcast(BF16)
        if sqq_extra:
            sqq = A_PREP[:, 2 * NT * 64 + NT * 96:].bitcast(BF16)
        else:
            sqq = A_PREP[:, 2 * NT * 64 + NT * 32:
                         2 * NT * 64 + NT * 96].bitcast(BF16)

        xrb_free = tmax(tok_attn)
        h_free = tmax(tok_attn)
        acc_free = [tmax(tok_attn), tmax(tok_attn),
                    tmax(tok_attn), tmax(tok_attn)]
        xq_free = [tmax(pb_free[0], pb_free[1], tok_attn),
                   tmax(pb_free[0], pb_free[1], tok_attn)]
        sq_free = {}
        y_toks = []

        for oc in range(n_out):
            t0 = oc * TFF
            xrb_toks = []
            for qi in range(NQ):
                qb = qi % 2
                src = xres_d[:, t0 + qi * 256:t0 + (qi + 1) * 256].rearrange(
                    "(c p) t -> p c t", p=128)
                tok_l = sch.dma("scalar", "xqa" if qb == 0 else "xqb",
                                lambda e_, s_=src, q=qb:
                                e_.dma_start(out=xq32s[q].rearrange(
                                    "p (c t) -> p c t", t=256), in_=s_),
                                tmax(xq_free[qb],
                                     xres_st[(t0 + qi * 256) // TC]))
                dstv = xresbf.rearrange("p (c t) -> p c t", t=TFF)[
                    :, :, qi * 256:(qi + 1) * 256]
                t_c = sch.op("vector", lambda e_, d=dstv, q=qb:
                             e_.tensor_copy(d, xq32s[q].rearrange(
                                 "p (c t) -> p c t", t=256)),
                             tmax(tok_l, xrb_free if qi == 0 else ()))
                xq_free[qb] = tmax(t_c)
                xrb_toks.append(t_c)
            tok_xrb = tmax(*xrb_toks)

            acc_toks = [None] * 4
            h_tok = {}
            HGOF = (NFG // 2) * TFF

            def mm1_group(g):
                for fts in range(0, NFG, 2):
                    ft0 = g * NFG + fts
                    q_run_to(n_p1 + g * NFG * 2 + (fts + 2) * 2 + 4)
                    sb = next_slab()
                    sdst = slabs[sb].rearrange("p (c n) -> p c n", n=256)
                    tok_sl = sch.dma("sync", f"slab{sb}",
                                 lambda e_, d=sdst,
                                     s_=wq1[:, ft0 * 128:(ft0 + 2) * 128]
                                     .rearrange("(c p) n -> p c n", p=128):
                                     e_.dma_start(out=d, in_=s_),
                                     tmax(slab_free[sb],
                                          *st["q1"][ft0 * 2:ft0 * 2 + 4]))
                    slab = slabs[sb]
                    for ftl in range(2):
                        ft = fts + ftl
                        ftg = ft0 + ftl
                        hxs = NFG // 2
                        for hf in range(NH):
                            bb = next_bank()
                            gm = proj_group(
                                lambda cj, l_=ftl: slab[
                                    :, cj * 256 + l_ * 128:
                                    cj * 256 + l_ * 128 + 128],
                                lambda cj, h=hf: xresbf[
                                    :, cj * TFF + h * 512:
                                    cj * TFF + h * 512 + 512],
                                bb, tmax(tok_sl, xrb_toks[2 * hf],
                                         xrb_toks[2 * hf + 1],
                                         bank_free[bb],
                                         h_free if (fts == 0 and ftl == 0
                                                    and hf == 0 and g == 0)
                                         else ()))
                            hx, hcol = divmod(ft, hxs)
                            hv = h_ap[hx][:, (g % 2) * HGOF + hcol * TFF
                                          + hf * 512:
                                          (g % 2) * HGOF + hcol * TFF
                                          + hf * 512 + 512]
                            t_h = sch.op("scalar",
                                         lambda e_, o=hv, p=PS[bb], f=ftg:
                                         e_.activation(o, p[:], AF.Gelu,
                                                       bias=b1s[:, f:f + 1]),
                                         tmax(gm, tok_con))
                            bank_free[bb] = tmax(t_h)
                            h_tok[(g, ft, hf)] = t_h
                    slab_free[sb] = sch.pemark()

            def mm2_group(g):
                for ct in range(NT):
                    if ct == 0:
                        q_run_to(n_p1 + (g + 1) * NFG * 2)
                    sb = next_slab()
                    sdst = slabs[sb][:, 0:NFG * 128].rearrange(
                        "p (a n) -> p a n", n=128)
                    tok_sl = sch.dma("sync", f"slab{sb}",
                                 lambda e_, d=sdst,
                                     s_=wq2[ct, g * NFG * 128:
                                            (g + 1) * NFG * 128, :]
                                     .rearrange("(a p) n -> p a n", p=128):
                                     e_.dma_start(out=d, in_=s_),
                                     tmax(slab_free[sb], q_front()))
                    slab = slabs[sb]
                    for hf in range(NH):
                        bb = next_bank()
                        for ft in range(NFG):
                            hx, hcol = divmod(ft, NFG // 2)
                            lt = slab[:, ft * 128:(ft + 1) * 128]
                            rh = h_ap[hx][:, (g % 2) * HGOF + hcol * TFF
                                          + hf * 512:
                                          (g % 2) * HGOF + hcol * TFF
                                          + hf * 512 + 512]
                            sch.pe(lambda e_, p=PS[bb], l_=lt, r_=rh,
                                   s0=(ft == 0), s1=(ft == NFG - 1):
                                   e_.matmul(p[:], l_, r_, start=s0,
                                             stop=s1),
                                   tmax(h_tok[(g, ft, hf)],
                                        tmax(tok_sl, bank_free[bb])
                                        if ft == 0 else ()),
                                   mark=(ft == NFG - 1))
                        gmm = sch.pemark()
                        evs = []
                        for h2 in range(2):
                            qi = hf * 2 + h2
                            av = accq_ap[qi][:, ct * 256:(ct + 1) * 256]
                            pv = PS[bb][:, h2 * 256:h2 * 256 + 256]
                            if g == 0:
                                t_a = sch.op("vector",
                                             lambda e_, o=av, p=pv:
                                             e_.tensor_copy(o, p),
                                             tmax(gmm, acc_free[qi]
                                                  if ct == 0 else ()))
                            else:
                                t_a = sch.op("vector",
                                             lambda e_, o=av, p=pv:
                                             e_.tensor_tensor(o, o, p,
                                                              ALU.add),
                                             tmax(gmm, acc_toks[qi]))
                            acc_toks[qi] = tmax(t_a, acc_toks[qi])
                            evs.append(t_a)
                        bank_free[bb] = tmax(*evs)
                    slab_free[sb] = sch.pemark()

            mm1_group(0)
            for g in range(GF):
                if g + 1 < GF:
                    mm1_group(g + 1)
                mm2_group(g)
            h_free = sch.pemark()

            ep_toks = [[None] * NT for _ in range(4)]
            rb2_toks = [None] * 4
            for qi in range(NQ):
                qb = qi % 2
                src = xres_d[:, t0 + qi * 256:t0 + (qi + 1) * 256].rearrange(
                    "(c p) t -> p c t", p=128)
                tok_l = sch.dma("scalar", "xqa" if qb == 0 else "xqb",
                                lambda e_, s_=src, q=qb:
                                e_.dma_start(out=xq32s[q].rearrange(
                                    "p (c t) -> p c t", t=256), in_=s_),
                                tmax(xq_free[qb], h_free))
                qtoks = []
                for ct in range(NT):
                    av = accq_ap[qi][:, ct * 256:(ct + 1) * 256]
                    xv = xq32s[qb][:, ct * 256:(ct + 1) * 256]
                    t_ep = sch.op("vector",
                                  lambda e_, o=av, x_=xv, c=ct:
                                  e_.scalar_tensor_tensor(
                                      o, o, con["b2"][:, c:c + 1], x_,
                                      ALU.add, ALU.add),
                                  tmax(tok_l, acc_toks[qi], tok_con))
                    ep_toks[qi][ct] = t_ep
                    qtoks.append(t_ep)
                xq_free[qb] = tmax(*qtoks)
                dstv = resbf2[:, 0:NT * TFF].rearrange(
                    "p (c t) -> p c t", t=TFF)[:, :, qi * 256:(qi + 1) * 256]
                rb2_toks[qi] = sch.op("vector",
                                      lambda e_, d=dstv, s_=accq_ap[qi]
                                      .rearrange("p (c t) -> p c t", t=256):
                                      e_.tensor_copy(d, s_),
                                      tmax(qtoks, h_free))

            for hf in range(NH):
                s2s = s2q = None
                for ct in range(NT):
                    rbv = resbf2[:, ct * TFF + hf * 512:
                                 ct * TFF + hf * 512 + 512]
                    sqv = sqq[:, (ct % 2) * 512:(ct % 2) * 512 + 512]
                    t_sq = sch.op("scalar", lambda e_, o=sqv, a=rbv:
                                  e_.activation(o, a, AF.Square),
                                  tmax(rb2_toks[hf * 2],
                                       rb2_toks[hf * 2 + 1],
                                       sq_free.get(ct % 2), q_front()))
                    t_ms = sch.pe(lambda e_, r_=rbv, s0=(ct == 0),
                                  s1=(ct == NT - 1):
                                  e_.matmul(PS[4][0:1, 0:TC], ones_bf, r_,
                                            start=s0, stop=s1),
                                  tmax(rb2_toks[hf * 2],
                                       rb2_toks[hf * 2 + 1], tok_ones,
                                       st["ps4"] if ct == 0 else ()),
                                  mark=True)
                    t_mq = sch.pe(lambda e_, r_=sqv, s0=(ct == 0),
                                  s1=(ct == NT - 1):
                                  e_.matmul(PS[5][0:1, 0:TC], ones_bf, r_,
                                            start=s0, stop=s1),
                                  tmax(t_sq, tok_ones,
                                       st["ps5"] if ct == 0 else ()),
                                  mark=True)
                    sq_free[ct % 2] = tmax(t_mq)
                    if ct == NT - 1:
                        s2s, s2q = t_ms, t_mq
                t_rs, t_mr, _ = ln_rows(s2s, s2q)
                t_b6 = bc_pair(6, row_rs, t_rs, st["bc"][0], 0)
                t_b7 = bc_pair(7, row_mr, t_mr, st["bc"][1], 1)
                st["rows"] = tmax(st["rows"], t_b6, t_b7)
                for h2 in range(2):
                    qi = hf * 2 + h2
                    a3 = accq_ap[qi].rearrange("p (c t) -> p c t", t=256)
                    p6v = PS[6][:, h2 * 256:h2 * 256 + 256].rearrange(
                        "p (a t) -> p a t", a=1).broadcast_to([128, NT, 256])
                    p7v = PS[7][:, h2 * 256:h2 * 256 + 256].rearrange(
                        "p (a t) -> p a t", a=1).broadcast_to([128, NT, 256])
                    t_a1 = sch.op("vector", lambda e_, o=a3, p=p6v:
                                  e_.tensor_tensor(o, o, p, ALU.mult),
                                  tmax(t_b6, *ep_toks[qi], rb2_toks[qi]))
                    t_a2 = sch.op("vector", lambda e_, o=a3, p=p7v:
                                  e_.tensor_tensor(o, o, p, ALU.subtract),
                                  tmax(t_b7, t_a1))
                    st["bc"][0] = tmax(st["bc"][0], t_a1)
                    st["bc"][1] = tmax(st["bc"][1], t_a2)
                    t_y = sch.dma("scalar", f"y{qi}",
                                  lambda e_, q=qi, t_=t0:
                                  e_.dma_start(
                        out=y_out[:, t_ + q * 256:t_ + (q + 1) * 256]
                        .rearrange("(c p) t -> p c t", p=128),
                        in_=accq_ap[q].rearrange("p (c t) -> p c t", t=256)),
                        t_a2)
                    y_toks.append(t_y)
                    acc_free[qi] = tmax(t_y)
            h_free = tmax(h_free, sch.pemark())

        sch.wait("sync", tmax(*y_toks))
        emit_all(block, sch, sems)

    return nc


def emit_all(block, sch, sems):
    @block.sync
    def _(e_):
        sch.run("sync", sems, e_)

    @block.scalar
    def _(e_):
        sch.run("scalar", sems, e_)

    @block.vector
    def _(e_):
        sch.run("vector", sems, e_)

    @block.gpsimd
    def _(e_):
        sch.run("gpsimd", sems, e_)

    @block.tensor
    def _(e_):
        sch.run("tensor", sems, e_)



def _prep_inputs(x, td, tf, tmk, tmv, tmr, tmg, Wr, Wk, Wv, Wg, Wo,
                 ln_g, ln_b, w1, b1, w2, b2, D, T, F):
    NT = D // 128
    NF = F // 128
    B = x.shape[0]

    def vec128(v):
        return np.ascontiguousarray(
            np.asarray(v, np.float32).reshape(NT, 128).T)

    WkT = np.ascontiguousarray(np.asarray(Wk, np.float32).T)
    WvT = np.ascontiguousarray(np.asarray(Wv, np.float32).T)
    WrT = np.ascontiguousarray(np.asarray(Wr, np.float32).T)
    WgT = np.ascontiguousarray(np.asarray(Wg, np.float32).T)
    WoT = np.ascontiguousarray(np.asarray(Wo, np.float32).T)
    wkv_cat = np.empty((NT, D, 256), np.float32)
    wrg_cat = np.empty((NT, D, 256), np.float32)
    for ct in range(NT):
        wkv_cat[ct, :, 0:128] = WkT[:, ct * 128:(ct + 1) * 128]
        wkv_cat[ct, :, 128:256] = WvT[:, ct * 128:(ct + 1) * 128]
        wrg_cat[ct, :, 0:128] = WrT[:, ct * 128:(ct + 1) * 128]
        wrg_cat[ct, :, 128:256] = WgT[:, ct * 128:(ct + 1) * 128]

    common = {
        "wkv": wkv_cat, "wrg": wrg_cat, "wo": WoT,
        "w1t": np.ascontiguousarray(np.asarray(w1, np.float32).T),
        "w2t": np.ascontiguousarray(np.asarray(w2, np.float32).T),
        "tmk": vec128(tmk), "tmv": vec128(tmv), "tmr": vec128(tmr),
        "tmg": vec128(tmg), "dec": vec128(td), "tf": vec128(tf),
        "lng": vec128(ln_g), "lnb": vec128(ln_b), "b2": vec128(b2),
        "b1": np.ascontiguousarray(
            np.asarray(b1, np.float32).reshape(NF, 128).T),
        "lngrow": np.ascontiguousarray(
            np.asarray(ln_g, np.float32).reshape(1, D)),
    }

    xT = np.transpose(np.asarray(x, np.float32), (0, 2, 1))
    in_maps = []
    for core in range(2 * B):
        b = core // 2
        half = core % 2
        xb = xT[b]
        if half == 1:
            win = xb
        else:
            win = np.concatenate([xb[:, T // 2:], xb[:, :T // 2]], axis=1)
            win = np.ascontiguousarray(win)
            win[:, T // 2 - 1] = xb[:, 1]
        wext = np.concatenate([win[:, 1:2], win], axis=1)
        m = {"xw": np.ascontiguousarray(wext),
             "gate": np.full((128, 1), float(half), np.float32)}
        m.update(common)
        in_maps.append(m)
    return in_maps


def kernel(**inp):
    D, T, F = 2048, 4096, 8192
    B = 4
    x = np.asarray(inp["L0_attn_x"], np.float32)
    in_maps = _prep_inputs(
        x, inp["L0_attn_time_decay"], inp["L0_attn_time_first"],
        inp["L0_attn_tm_k"], inp["L0_attn_tm_v"], inp["L0_attn_tm_r"],
        inp["L0_attn_tm_g"], inp["L0_attn_Wr"], inp["L0_attn_Wk"],
        inp["L0_attn_Wv"], inp["L0_attn_Wg"], inp["L0_attn_Wo"],
        inp["L0_attn_ln_g"], inp["L0_attn_ln_b"], inp["L0_ffn_w1"],
        inp["L0_ffn_b1"], inp["L0_ffn_w2"], inp["L0_ffn_b2"], D, T, F)

    nc = build(D=D, T=T, F=F)
    res = run_bass_kernel_spmd(nc, in_maps, core_ids=list(range(8)))

    out = np.empty((B, T, D), np.float32)
    for core in range(8):
        b = core // 2
        half = core % 2
        yc = res.results[core]["y"]
        if half == 0:
            out[b, 0:T // 2, :] = yc.T
        else:
            out[b, T // 2:T, :] = yc.T
    return out


# version 32
# speedup vs baseline: 1.2681x; 1.0094x over previous
import numpy as np
import ml_dtypes

import concourse.bass as bass
import concourse.mybir as mybir
from concourse.bass_utils import run_bass_kernel_spmd

F32 = mybir.dt.float32
BF16 = mybir.dt.bfloat16
AF = mybir.ActivationFunctionType
ALU = mybir.AluOpType
AX = mybir.AxisListType

EPS = 1e-5

ENGS = ("sync", "scalar", "vector", "gpsimd", "tensor")
STREAMS = ("con", "cast0", "cast1", "cast2", "cast3", "cast4", "cast5",
           "slab0", "slab1", "xch", "x32", "pin0", "pin1",
           "qw0", "qw1", "qw2", "pin2", "xrst", "xqa", "xqb",
           "y0", "y1", "y2", "y3", "slab2", "slab3")
SEMS = ("sV", "sS", "sG", "sP") + STREAMS
SELF = {"vector": "sV", "scalar": "sS", "gpsimd": "sG"}


def tmax(*xs):
    best = {}
    stack = list(xs)
    while stack:
        t = stack.pop()
        if t is None or t == ():
            continue
        if isinstance(t, list):
            stack.extend(t)
            continue
        s, v = t
        if v is not None and v > best.get(s, -1):
            best[s] = v
    return [(s, v) for s, v in best.items()]


class Sched:
    def __init__(self):
        self.q = {e: [] for e in ENGS}
        self.cnt = {s: 0 for s in SEMS}
        self.floor = {e: {s: 0 for s in SEMS} for e in ENGS}

    def _emit(self, eng, fn, deps, inc, by):
        waits = []
        fl = self.floor[eng]
        for s, v in tmax(deps):
            if v > fl[s]:
                waits.append((s, v))
                fl[s] = v
        tok = None
        if inc is not None:
            self.cnt[inc] += by
            tok = (inc, self.cnt[inc])
        self.q[eng].append((waits, fn, inc, by))
        return tok

    def op(self, eng, fn, deps=()):
        s = SELF[eng]
        return self._emit(eng, fn, tmax(deps, (s, self.cnt[s])), s, 1)

    def pe(self, fn, deps=(), mark=False):
        return self._emit("tensor", fn, deps, "sP" if mark else None, 1)

    def pemark(self):
        return ("sP", self.cnt["sP"])

    def dma(self, eng, stream, fn, deps=()):
        return self._emit(eng, fn, deps, stream, 16)

    def wait(self, eng, deps):
        self._emit(eng, None, deps, None, 0)

    def run(self, name, sems, e):
        for waits, fn, inc, by in self.q[name]:
            for s, v in waits:
                e.wait_ge(sems[s], v)
            if fn is None:
                continue
            inst = fn(e)
            if inc is not None:
                inst.then_inc(sems[inc], by)


def build(D=2048, T=4096, F=8192, TC=512, TKV=1024, TFF=1024, GF=4,
          debug_stop=None, no_prep=False, full_upto=None, wo_level=4):
    NT = D // 128
    NF = F // 128
    M = T // 2
    n_kv = M // TKV
    n_full = M // TC
    n_out = M // TFF
    NFG = NF // GF
    NH = TFF // 512
    NQ = TFF // 256

    nc = bass.Bass("TRN2", target_bir_lowering=False)

    xw = nc.dram_tensor("xw", [D, T + 1], F32, kind="ExternalInput").ap()
    gate_in = nc.dram_tensor("gate", [128, 1], F32, kind="ExternalInput").ap()
    wkv_in = nc.dram_tensor("wkv", [NT, D, 256], F32,
                            kind="ExternalInput").ap()
    wrg_in = nc.dram_tensor("wrg", [NT, D, 256], F32,
                            kind="ExternalInput").ap()
    wo_in = nc.dram_tensor("wo", [D, D], F32, kind="ExternalInput").ap()
    w1t = nc.dram_tensor("w1t", [D, F], F32, kind="ExternalInput").ap()
    w2t = nc.dram_tensor("w2t", [F, D], F32, kind="ExternalInput").ap()
    VN = ("tmk", "tmv", "tmr", "tmg", "dec", "tf", "lng", "lnb", "b2")
    vec_in = {n: nc.dram_tensor(n, [128, NT], F32, kind="ExternalInput").ap()
              for n in VN}
    b1_in = nc.dram_tensor("b1", [128, NF], F32, kind="ExternalInput").ap()
    lngrow_in = nc.dram_tensor("lngrow", [1, D], F32,
                               kind="ExternalInput").ap()
    y_out = nc.dram_tensor("y", [D, M], F32, kind="ExternalOutput").ap()

    qkv = nc.dram_tensor("qkv", [NT, D, 256], BF16).ap()
    qrg = nc.dram_tensor("qrg", [NT, D, 256], BF16).ap()
    qwo = nc.dram_tensor("qwo", [D, D], BF16).ap()
    wq1 = nc.dram_tensor("wq1", [D, F], BF16).ap()
    wq2 = nc.dram_tensor("wq2", [NT, F, 128], BF16).ap()
    xres_d = nc.dram_tensor("xres", [D, M], F32).ap()
    xwbf_d = nc.dram_tensor("xwbf", [D, T + 1], BF16).ap()

    sch = Sched()
    from contextlib import ExitStack
    with ExitStack() as ctx:
        e = ctx.enter_context

        A_XBF = e(nc.sbuf_tensor("A_XBF", [128, NT * (TC + 1) // 2 + 8],
                                 F32))
        A_MIX = e(nc.sbuf_tensor("A_MIX", [128, NT * TC], F32))
        A_WKV = e(nc.sbuf_tensor("A_WKV", [128, NT * TC], F32))
        A_O32 = e(nc.sbuf_tensor("A_O32", [128, NT * TC + NT // 2], F32))
        A_AO = e(nc.sbuf_tensor("A_AO", [128, NT * TC // 2], F32))
        A_SCR = e(nc.sbuf_tensor("A_SCR", [128, 16 * TC], F32))
        A_SLAB = e(nc.sbuf_tensor("A_SLAB", [128, NT * 256], F32))
        sqq_extra = 512 if NT * 64 < 512 else 0
        A_PREP = e(nc.sbuf_tensor("A_PREP", [128, 2 * NT * 64 + NT * 96
                                             + sqq_extra], F32))
        A_CON = e(nc.sbuf_tensor("A_CON", [128, 16 * NT + NF + 210], F32))
        CARRY = e(nc.sbuf_tensor("CARRY", [128, 2 * NT], F32))
        LNROW = e(nc.sbuf_tensor("LNROW", [1, 5 * TC], F32))
        PS = [e(nc.psum_tensor(f"PS{i}", [128, 512], F32)) for i in range(8)]
        sems = {s: e(nc.semaphore(s)) for s in SEMS}
        block = e(nc.Block())

        con = {}
        off = 0
        for n in VN + ("w", "eu"):
            con[n] = A_CON[:, off:off + NT]
            off += NT
        b1s = A_CON[:, off:off + NF]
        off += NF
        ones_bf = A_CON[:, off:off + 1].bitcast(BF16)[:, 0:1]
        off += 1
        gcol = A_CON[:, off:off + 1]
        off += 1
        thrs = A_CON[:, off:off + 4]
        off += 4
        racc = A_CON[:, off:off + 4]
        off += 4
        ones_f = A_CON[:, off:off + 1]
        off += 1
        ones_b = A_CON[0:1, off:off + 128]
        off += 128
        ones_bb = A_CON[0:1, off:off + 64].bitcast(BF16)[0:1, 0:128]
        off += 64

        car_a = CARRY[:, 0:NT]
        car_b = CARRY[:, NT:2 * NT]

        sY = TC + 1
        xbf = A_XBF[:].bitcast(BF16)
        mixA = A_MIX[:, 0:NT * TC // 2].bitcast(BF16)
        mixB = A_MIX[:, NT * TC // 2:NT * TC].bitcast(BF16)
        x32 = A_MIX[:, 0:NT * TC]
        wkvr = A_WKV[:, 0:NT * TC]
        resf = A_WKV[:, 0:NT * TC]
        o32 = A_O32[:, 0:NT * TC]
        aobf = A_AO[:].bitcast(BF16)
        resbf = A_AO[:].bitcast(BF16)
        slabs = [A_SLAB[:, 0:NT * 128].bitcast(BF16),
                 A_SLAB[:, NT * 128:NT * 256].bitcast(BF16)]
        pin = [A_PREP[:, 0:NT * 64], A_PREP[:, NT * 64:2 * NT * 64],
               A_SCR[:, 14 * TC:14 * TC + NT * 64]]
        pouts = [A_PREP[:, 2 * NT * 64 + j * NT * 32:
                        2 * NT * 64 + (j + 1) * NT * 32].bitcast(BF16)
                 for j in range(3)]

        sK = TKV + 1
        xbf_kv = A_O32[:].bitcast(BF16)
        mixKA = A_MIX[:].bitcast(BF16)
        mixKB = A_WKV[:].bitcast(BF16)
        xx_kv = A_AO[:].bitcast(BF16)

        def scr(i, w=TC):
            return A_SCR[:, i * TC:i * TC + w]

        def scrk(i, w=TKV):
            return A_SCR[:, i * TKV:i * TKV + w]

        row_mu = LNROW[0:1, 0:TC]
        row_rs = LNROW[0:1, TC:2 * TC]
        row_mr = LNROW[0:1, 2 * TC:3 * TC]
        row_e2 = row_mr

        lts = []
        for n in VN:
            lts.append(sch.dma("sync", "con",
                               lambda e_, d=con[n], s_=vec_in[n]:
                               e_.dma_start(out=d, in_=s_)))
        lts.append(sch.dma("sync", "con",
                           lambda e_: e_.dma_start(out=b1s, in_=b1_in)))
        lts.append(sch.dma("sync", "con", lambda e_: e_.dma_start(
            out=gcol, in_=gate_in)))
        tok_con = tmax(*lts)

        t_ = sch.op("scalar", lambda e_: e_.activation(con["w"], con["dec"],
                                                       AF.Exp), tok_con)
        tok_w = sch.op("scalar", lambda e_: e_.activation(con["w"], con["w"],
                                                          AF.Exp,
                                                          scale=-1.0), t_)
        tok_eu = sch.op("scalar", lambda e_: e_.activation(
            con["eu"], con["tf"], AF.Exp), tok_con)
        tok_ones = sch.op("vector", lambda e_: e_.memset(ones_bf, 1.0))
        tok_onesb = sch.op("vector", lambda e_: e_.memset(ones_b, 1.0))
        tok_onesbb = sch.op("vector", lambda e_: e_.memset(ones_bb, 1.0))
        tok_onesf = sch.op("vector", lambda e_: e_.memset(ones_f, 1.0))
        tok_car0 = sch.op("vector", lambda e_: e_.memset(CARRY[:], 0.0))

        tok_qkv = [None] * NT
        tok_qrg = [None] * NT
        tok_qwo = [None] * (D // 256)

        cast_chain = [None] * 6
        cast_rr = [0]
        tok_xwbf = {}

        def cast_dma(fn):
            j = cast_rr[0] % 6
            cast_rr[0] += 1
            tok = sch.dma("gpsimd", f"cast{j}", fn, cast_chain[j])
            cast_chain[j] = tok
            return tok

        def emit_cast_x(pos0, w_):
            a0 = 0 if pos0 == 0 else pos0 + 1
            tok_xwbf[pos0] = cast_dma(
                lambda e_, a=a0, b_=pos0 + w_ + 1:
                e_.dma_start(out=xwbf_d[:, a:b_], in_=xw[:, a:b_]))

        def emit_casts_kv():
            for ct in range(NT):
                tok_qkv[ct] = cast_dma(lambda e_, c=ct: e_.dma_start(
                    out=qkv[c], in_=wkv_in[c]))

        def emit_casts_rg():
            for ct in range(NT):
                tok_qrg[ct] = cast_dma(lambda e_, c=ct: e_.dma_start(
                    out=qrg[c], in_=wrg_in[c]))

        def emit_casts_wo():
            for j in range(D // 256):
                tok_qwo[j] = cast_dma(
                    lambda e_, a=j * 256, b_=(j + 1) * 256:
                    e_.dma_start(out=qwo[:, a:b_], in_=wo_in[:, a:b_]))

        n_p1 = F // 64
        n_prep = n_p1 + 2 * NF
        DH = D // 2
        pb_free = [tmax(), tmax(), tmax()]
        st = {"thr": None, "ps4": tmax(), "ps5": tmax(),
              "bc": [tmax(), tmax()], "q1": [], "q2": [], "rows": tmax(),
              "red": None, "lhl": [tmax(), tmax()]}

        def bc_pair(bank, row, dep_row, dep_bank, slot):
            hlv = LNROW[0:1, (3 + slot) * TC:(4 + slot) * TC].bitcast(BF16)
            hi = hlv[0:1, 0:TC]
            lo = hlv[0:1, TC:2 * TC]
            t_h = sch.op("scalar", lambda e_, o=hi, a=row:
                         e_.copy(o, a),
                         tmax(dep_row, st["lhl"][slot]))
            t_b = sch.op("scalar", lambda e_, o=row_mu, a=hi:
                         e_.copy(o, a), tmax(t_h, st["lhl"][slot ^ 1]))
            t_l = sch.op("vector", lambda e_, o=lo, a=row:
                         e_.tensor_tensor(o, a, row_mu, ALU.subtract), t_b)
            sch.pe(lambda e_, p=PS[bank], r_=hi:
                   e_.matmul(p[:], ones_bb, r_, start=True, stop=False),
                   tmax(t_h, tok_onesbb, dep_bank))
            sch.pe(lambda e_, p=PS[bank], r_=lo:
                   e_.matmul(p[:], ones_bb, r_, start=False, stop=True),
                   tmax(t_l), mark=True)
            tok = sch.pemark()
            st["lhl"][slot] = tok
            return tok

        def prep_views(i, dst):
            if i < n_p1:
                return (dst.rearrange("p (c f) -> p c f", f=64),
                        w1t[:, i * 64:(i + 1) * 64].rearrange(
                            "(c p) f -> p c f", p=128))
            j, h = divmod(i - n_p1, 2)
            return (dst.rearrange("p (c f) -> p c f", f=128),
                    w2t[j * 128:(j + 1) * 128, h * DH:(h + 1) * DH]
                    .rearrange("p (c f) -> p c f", f=128))

        def emit_thr(tok_red):
            tok_rmm = sch.pe(lambda e_: e_.matmul(
                PS[4][0:1, 0:2], ones_f, racc[:, 0:2], start=True,
                stop=True), tmax(tok_red, tok_onesf, st["ps4"]), mark=True)
            trow = LNROW[0:1, 0:2]
            tok_tr = sch.op("vector", lambda e_: e_.tensor_scalar(
                trow, PS[4][0:1, 0:2], 0.7 / (D * F), None, ALU.mult),
                tok_rmm)
            tok_bc = sch.pe(lambda e_: e_.matmul(
                PS[4][:, 2:4], ones_b, trow, start=True, stop=True),
                tmax(tok_tr, tok_onesb), mark=True)
            tok_thr = sch.op("vector", lambda e_: e_.tensor_copy(
                thrs[:, 0:2], PS[4][:, 2:4]), tmax(tok_bc, tok_con))
            tok_nthr = sch.op("vector", lambda e_: e_.tensor_scalar(
                thrs[:, 2:4], thrs[:, 0:2], -1.0, None, ALU.mult), tok_thr)
            st["thr"] = tok_nthr
            st["ps4"] = tmax(tok_thr)
            st["rows"] = tmax(tok_tr)

        UNITS = [("r", i) for i in range(n_prep)] + [("t", 0)]
        QOF = len(UNITS)
        UNITS = UNITS + [("q", i) for i in range(n_prep)]
        NPU = len(UNITS)
        p_load = {}
        p_done = {}
        cur = {"l": 0, "c": 0}

        def pin_of(s):
            kind, i = UNITS[s]
            if kind == "q" and i >= n_p1:
                return (s - QOF) % 2
            return s % 3

        def emit_load(s):
            kind, i = UNITS[s]
            if kind == "t":
                p_load[s] = tmax()
                return
            b = pin_of(s)
            dv, sv = prep_views(i, pin[b])
            if kind == "q":
                dep = tmax(p_done.get(s - 2), p_done.get(s - 3),
                           p_done.get(s - 4),
                           p_done.get(s - 1)
                           if (s >= 1 and UNITS[s - 1][0] != "t"
                               and pin_of(s - 1) == pin_of(s)) else None)
            else:
                dep = tmax(p_done.get(s - 3))
            p_load[s] = sch.dma("gpsimd", f"pin{b}",
                                lambda e_, d=dv, s_=sv:
                                e_.dma_start(out=d, in_=s_), dep)

        def emit_compute(s):
            kind, i = UNITS[s]
            b = pin_of(s)
            if kind == "t":
                emit_thr(st["red"])
                p_done[s] = tmax()
                return
            if kind == "r":
                m = 0 if i < n_p1 else 1
                rdep = tmax(p_load[s], st.get("racc"))
                if m == 0:
                    tok_r = sch.op("scalar", lambda e_, s_=pin[b]:
                                   e_.activation(s_, s_, AF.Abs,
                                                 accum_out=racc[:, 2:3]),
                                   rdep)
                else:
                    tok_r = sch.op("vector", lambda e_, s_=pin[b]:
                                   e_.tensor_reduce(
                                       racc[:, 2:3], s_, AX.X, ALU.add,
                                       apply_absolute_value=True), rdep)
                if i == 0 or i == n_p1:
                    tok_a = sch.op("vector", lambda e_, m_=m:
                                   e_.tensor_copy(racc[:, m_:m_ + 1],
                                                  racc[:, 2:3]), tok_r)
                else:
                    tok_a = sch.op("vector", lambda e_, m_=m:
                                   e_.tensor_tensor(
                                       racc[:, m_:m_ + 1],
                                       racc[:, m_:m_ + 1], racc[:, 2:3],
                                       ALU.add), tok_r)
                st["racc"] = tok_a
                st["red"] = tok_a
                p_done[s] = tmax(tok_r)
                return
            m = 0 if i < n_p1 else 1
            po = pouts[i % 3]
            tok_m = sch.op("vector", lambda e_, s_=pin[b], m_=m, o=po:
                           e_.tensor_scalar(o, s_,
                                            thrs[:, 2 + m_:3 + m_], None,
                                            ALU.is_le),
                           tmax(p_load[s], st["thr"],
                                st.get(("pout", i % 3))))
            tok_q = sch.op("vector", lambda e_, s_=pin[b], m_=m, o=po:
                           e_.scalar_tensor_tensor(
                               o, s_, thrs[:, 0 + m_:1 + m_], o,
                               ALU.is_ge, ALU.subtract), tok_m)
            p_done[s] = tmax(tok_q)
            if i < n_p1:
                qdst = wq1[:, i * 64:(i + 1) * 64].rearrange(
                    "(c p) f -> p c f", p=128)
                psrc = po.rearrange("p (c f) -> p c f", f=64)
            else:
                j, h = divmod(i - n_p1, 2)
                nct = DH // 128
                qdst = wq2[h * nct:(h + 1) * nct,
                           j * 128:(j + 1) * 128, :].rearrange(
                    "c p n -> p c n")
                psrc = po.rearrange("p (c f) -> p c f", f=128)
            tok_wq = sch.dma("gpsimd", f"qw{i % 3}",
                             lambda e_, d=qdst, s_=psrc:
                             e_.dma_start(out=d, in_=s_), tok_q)
            st[("pout", i % 3)] = tok_wq
            (st["q1"] if i < n_p1 else st["q2"]).append(tok_wq)

        n_ticks = (n_kv + 2 * n_full) * NT + n_full * (NT // 2 + NT)
        ATTN_LIM = QOF + n_p1
        per_tick = -(-(ATTN_LIM + 3) // max(1, n_ticks - 2 * NT))

        def prep_tick(k=None, limit=None, thr_ok=True):
            if no_prep:
                return
            lim = min(NPU, ATTN_LIM if limit is None else limit)
            k = per_tick if k is None else k
            for _ in range(k):
                ahead = 2 if cur["c"] >= QOF + n_p1 else 3
                while cur["l"] < min(lim, cur["c"] + ahead):
                    emit_load(cur["l"])
                    cur["l"] += 1
                if cur["c"] < lim:
                    if not thr_ok and UNITS[cur["c"]][0] == "t":
                        return
                    emit_compute(cur["c"])
                    cur["c"] += 1

        def q_run_to(n_q):
            lim = min(NPU, QOF + n_q)
            prep_tick(k=max(0, lim - cur["c"]), limit=lim)

        def q_front():
            return tmax(st.get(("pout", 0)), st.get(("pout", 1)),
                        st.get(("pout", 2)))

        bank_free = [tmax(), tmax(), tmax(), tmax()]
        bank_i = [0]

        def next_bank():
            b = bank_i[0] % 4
            bank_i[0] += 1
            return b

        slab_free = [tmax(), tmax()]
        slab_i = [0]

        def next_slab():
            b = slab_i[0] % 2
            slab_i[0] += 1
            return b

        def slab_load(sb, dv, sv, dep):
            c = dv.shape[1]
            t1 = sch.dma("sync", f"slab{sb}",
                         lambda e_, d=dv[:, 0:c // 2, :],
                         s_=sv[:, 0:c // 2, :]:
                         e_.dma_start(out=d, in_=s_), dep)
            t2 = sch.dma("sync", f"slab{sb + 2}",
                         lambda e_, d=dv[:, c // 2:c, :],
                         s_=sv[:, c // 2:c, :]:
                         e_.dma_start(out=d, in_=s_), dep)
            return tmax(t1, t2)

        def proj_group(lhs_fn, rhs_fn, bank, dep0, cj_dep=None):
            for cj in range(NT):
                d = dep0 if cj == 0 else ()
                if cj_dep is not None:
                    d = tmax(d, cj_dep(cj))
                sch.pe(lambda e_, p=PS[bank], l_=lhs_fn(cj), r_=rhs_fn(cj),
                       s0=(cj == 0), s1=(cj == NT - 1):
                       e_.matmul(p[:], l_, r_, start=s0, stop=s1),
                       d, mark=(cj == NT - 1))
            return sch.pemark()

        scrf = {}
        tok_carry = tmax(tok_car0)
        xbf_free = tmax()
        mix_free = tmax()
        wkv_free = tmax()
        o32_free = tmax()
        ao_free = tmax()
        xres_st = []

        for ci in range(n_kv):
            pos0 = ci * TKV

            xsrc = xw[:, pos0:pos0 + TKV + 1].rearrange("(c p) t -> p c t",
                                                        p=128)
            xdst = xbf_kv[:, 0:NT * sK].rearrange("p (c t) -> p c t", t=sK)
            if ci == 0:
                emit_cast_x(0, TKV)
                emit_casts_kv()
                for cc in range(1, n_kv):
                    emit_cast_x(cc * TKV, TKV)
                for cc in range(n_full):
                    emit_cast_x(n_kv * TKV + cc * TC, TC)
            if ci == max(0, n_kv - 2) and ci > 0:
                emit_casts_rg()
            xsrcb = xwbf_d[:, pos0:pos0 + TKV + 1].rearrange(
                "(c p) t -> p c t", p=128)
            tok_x = sch.dma("sync", "xch", lambda e_, d=xdst, s_=xsrcb:
                            e_.dma_start(out=d, in_=s_),
                            tmax(o32_free, st.get("xch"), tok_xwbf[pos0]))
            st["xch"] = tok_x

            mk_toks, mv_toks, xx_toks = [], [], []
            for cj in range(NT):
                xxs = xx_kv[:, (cj % 2) * TKV:(cj % 2) * TKV + TKV]
                prev_v = xbf_kv[:, cj * sK:cj * sK + TKV]
                x_v = xbf_kv[:, cj * sK + 1:cj * sK + 1 + TKV]
                t_xx = sch.op("vector", lambda e_, o=xxs, a=prev_v, b_=x_v:
                              e_.tensor_tensor(o, a, b_, ALU.subtract),
                              tmax(tok_x, scrf.get(("xxk", cj % 2)),
                                   ao_free if cj < 2 else ()))
                mk = mixKA[:, cj * TKV:(cj + 1) * TKV]
                mv = mixKB[:, cj * TKV:(cj + 1) * TKV]
                t_mk = sch.op("vector", lambda e_, o=mk, a=xxs, c=cj, b_=x_v:
                              e_.scalar_tensor_tensor(
                                  o, a, con["tmk"][:, c:c + 1], b_,
                                  ALU.mult, ALU.add),
                              tmax(t_xx, tok_con,
                                   mix_free if cj == 0 else ()))
                t_mv = sch.op("vector", lambda e_, o=mv, a=xxs, c=cj,
                              b_=x_v:
                              e_.scalar_tensor_tensor(
                                  o, a, con["tmv"][:, c:c + 1], b_,
                                  ALU.mult, ALU.add),
                              tmax(t_xx, tok_con,
                                   wkv_free if cj == 0 else ()))
                scrf[("xxk", cj % 2)] = tmax(t_mk, t_mv)
                mk_toks.append(t_mk)
                mv_toks.append(t_mv)
                xx_toks.append(t_xx)
            tok_mixk = tmax(*mk_toks)
            tok_mixv = tmax(*mv_toks)
            mix_use = []
            carry_next = []

            for ct in range(NT):
                prep_tick(2, thr_ok=False)
                sb = next_slab()
                sdst = slabs[sb].rearrange("p (c n) -> p c n", n=256)
                tok_sl = slab_load(
                    sb, sdst,
                    qkv[ct].rearrange("(c p) n -> p c n", p=128),
                    tmax(slab_free[sb], tok_qkv[ct]))
                slab = slabs[sb]
                ek = scrk(0 + ct % 2)
                z = scrk(2 + ct % 2)
                sa = scrk(4)
                sb_ = scrk(5)
                ek_toks, z_toks = [], []
                for hf in range(TKV // 512):
                    bk = next_bank()
                    g1 = proj_group(
                        lambda cj: slab[:, cj * 256:cj * 256 + 128],
                        lambda cj, h=hf: mixKA[:, cj * TKV + h * 512:
                                               cj * TKV + h * 512 + 512],
                        bk, tmax(tok_sl, bank_free[bk]),
                        cj_dep=lambda cj: mk_toks[cj])
                    t_ek = sch.op("scalar",
                                  lambda e_, o=ek, h=hf, p=PS[bk]:
                                  e_.activation(o[:, h * 512:h * 512 + 512],
                                                p[:], AF.Exp),
                                  tmax(g1, scrf.get(("ekk", ct % 2))
                                       if hf == 0 else ()))
                    bank_free[bk] = tmax(t_ek)
                    ek_toks.append(t_ek)
                    bv = next_bank()
                    g2 = proj_group(
                        lambda cj: slab[:, cj * 256 + 128:cj * 256 + 256],
                        lambda cj, h=hf: mixKB[:, cj * TKV + h * 512:
                                               cj * TKV + h * 512 + 512],
                        bv, tmax(tok_sl, bank_free[bv]),
                        cj_dep=lambda cj: mv_toks[cj])
                    t_zc = sch.op("scalar",
                                  lambda e_, o=z, h=hf, p=PS[bv]:
                                  e_.copy(o[:, h * 512:h * 512 + 512],
                                          p[:]),
                                  tmax(g2, scrf.get(("zk", ct % 2))
                                       if hf == 0 else ()))
                    bank_free[bv] = tmax(t_zc)
                    t_z = sch.op("vector",
                                 lambda e_, o=z, h=hf, a=ek:
                                 e_.tensor_tensor(
                                     o[:, h * 512:h * 512 + 512],
                                     o[:, h * 512:h * 512 + 512],
                                     a[:, h * 512:h * 512 + 512],
                                     ALU.mult),
                                 tmax(t_zc, ek_toks[hf]))
                    z_toks.append(t_z)
                    mix_use.append(tmax(g1, g2))
                slab_free[sb] = mix_use[-1]
                tok_sa = sch.op("vector",
                                lambda e_, o=sa, c=ct, d1=z:
                                e_.tensor_tensor_scan(
                                    o, con["w"][:, c:c + 1].broadcast_to(
                                        [128, TKV]), d1, car_a[:, c:c + 1],
                                    ALU.mult, ALU.add),
                                tmax(z_toks, tok_w, tok_carry,
                                     scrf.get("sak")))
                tok_sb = sch.op("vector",
                                lambda e_, o=sb_, c=ct, d1=ek:
                                e_.tensor_tensor_scan(
                                    o, con["w"][:, c:c + 1].broadcast_to(
                                        [128, TKV]), d1, car_b[:, c:c + 1],
                                    ALU.mult, ALU.add),
                                tmax(ek_toks, tok_w, tok_carry,
                                     scrf.get("sbk")))
                gated = False
                t_ca = sch.op("vector", lambda e_, c=ct, s_=sa:
                              e_.tensor_copy(car_a[:, c:c + 1],
                                             s_[:, TKV - 1:TKV]), tok_sa)
                t_cb = sch.op("vector", lambda e_, c=ct, s_=sb_:
                              e_.tensor_copy(car_b[:, c:c + 1],
                                             s_[:, TKV - 1:TKV]), tok_sb)
                carry_next.append(tmax(t_ca, t_cb))
                scrf["sak"] = tmax(t_ca)
                scrf["sbk"] = tmax(t_cb)
                scrf[("ekk", ct % 2)] = tmax(tok_sb, *z_toks)
                scrf[("zk", ct % 2)] = tmax(tok_sa)
            tok_carry = tmax(*carry_next)
            if ci == n_kv - 1:
                t_ga = sch.op("vector", lambda e_: e_.tensor_scalar(
                    car_a[:], car_a[:], gcol, None, ALU.mult),
                    tmax(tok_carry, tok_con))
                t_gb = sch.op("vector", lambda e_: e_.tensor_scalar(
                    car_b[:], car_b[:], gcol, None, ALU.mult),
                    tmax(tok_carry, tok_con))
                tok_carry = tmax(t_ga, t_gb)
            mix_free = tmax(*mix_use)
            wkv_free = tmax(*mix_use)
            o32_free = tmax(*(xx_toks + mk_toks + mv_toks))
            ao_free = tmax(*(mk_toks + mv_toks))

        if debug_stop == "kv":
            t_dbg = sch.dma("sync", "y", lambda e_: e_.dma_start(
                out=y_out[:, 0:M], in_=xw[:, 0:M]),
                tmax(tok_con, tok_carry))
            sch.wait("sync", t_dbg)
            emit_all(block, sch, sems)
            return nc

        for fci in range(n_full):
            pos0 = n_kv * TKV + fci * TC

            xsrc = xw[:, pos0:pos0 + TC + 1].rearrange("(c p) t -> p c t",
                                                       p=128)
            xdst = xbf[:, 0:NT * sY].rearrange("p (c t) -> p c t", t=sY)
            if tok_qrg[0] is None:
                emit_casts_rg()
            if fci == 0:
                emit_casts_wo()
            xsrcb = xwbf_d[:, pos0:pos0 + TC + 1].rearrange(
                "(c p) t -> p c t", p=128)
            tok_x = sch.dma("sync", "xch", lambda e_, d=xdst, s_=xsrcb:
                            e_.dma_start(out=d, in_=s_),
                            tmax(xbf_free, st.get("xch"), tok_xwbf[pos0]))
            st["xch"] = tok_x

            mk_toks, mv_toks, xx_toks = [], [], []
            for cj in range(NT):
                xxs = scr(10).bitcast(BF16)[:, (cj % 2) * TC:
                                            (cj % 2) * TC + TC]
                prev_v = xbf[:, cj * sY:cj * sY + TC]
                x_v = xbf[:, cj * sY + 1:cj * sY + 1 + TC]
                t_xx = sch.op("vector", lambda e_, o=xxs, a=prev_v, b_=x_v:
                              e_.tensor_tensor(o, a, b_, ALU.subtract),
                              tmax(tok_x, scrf.get(("xx", cj % 2)),
                                   scrf.get(("osq", cj % 2))))
                mk = mixA[:, cj * TC:(cj + 1) * TC]
                mv = mixB[:, cj * TC:(cj + 1) * TC]
                t_mk = sch.op("vector", lambda e_, o=mk, a=xxs, c=cj, b_=x_v:
                              e_.scalar_tensor_tensor(
                                  o, a, con["tmk"][:, c:c + 1], b_,
                                  ALU.mult, ALU.add),
                              tmax(t_xx, tok_con,
                                   mix_free if cj == 0 else ()))
                t_mv = sch.op("vector", lambda e_, o=mv, a=xxs, c=cj,
                              b_=x_v:
                              e_.scalar_tensor_tensor(
                                  o, a, con["tmv"][:, c:c + 1], b_,
                                  ALU.mult, ALU.add),
                              tmax(t_xx, tok_con,
                                   mix_free if cj == 0 else ()))
                scrf[("xx", cj % 2)] = tmax(t_mk, t_mv)
                mk_toks.append(t_mk)
                mv_toks.append(t_mv)
                xx_toks.append(t_xx)
            tok_mixk = tmax(*mk_toks)
            tok_mixv = tmax(*mv_toks)

            wkvr_toks = [None] * NT
            kv_use = []
            carry_next = []
            for ct in range(NT):
                prep_tick(2)
                sb = next_slab()
                sdst = slabs[sb].rearrange("p (c n) -> p c n", n=256)
                tok_sl = slab_load(
                    sb, sdst,
                    qkv[ct].rearrange("(c p) n -> p c n", p=128),
                    tmax(slab_free[sb], tok_qkv[ct]))
                slab = slabs[sb]

                bk = next_bank()
                gk = proj_group(lambda cj: slab[:, cj * 256:cj * 256 + 128],
                                lambda cj: mixA[:, cj * TC:(cj + 1) * TC],
                                bk, tmax(tok_sl, bank_free[bk]),
                                cj_dep=lambda cj: mk_toks[cj])
                ek = scr(0 + ct % 2)
                tok_ek = sch.op("scalar", lambda e_, o=ek, p=PS[bk]:
                                e_.activation(o, p[:], AF.Exp),
                                tmax(gk, scrf.get(("ek", ct % 2))))
                bank_free[bk] = tmax(tok_ek)

                bv = next_bank()
                gv = proj_group(lambda cj: slab[:, cj * 256 + 128:
                                                cj * 256 + 256],
                                lambda cj: mixB[:, cj * TC:(cj + 1) * TC],
                                bv, tmax(tok_sl, bank_free[bv]),
                                cj_dep=lambda cj: mv_toks[cj])
                slab_free[sb] = tmax(gk, gv)
                kv_use.append(tmax(gk, gv))

                z = scr(2 + ct % 2)
                tok_zc = sch.op("scalar", lambda e_, o=z, p=PS[bv]:
                                e_.copy(o, p[:]),
                                tmax(gv, scrf.get(("z", ct % 2))))
                bank_free[bv] = tmax(tok_zc)
                tok_z = sch.op("vector", lambda e_, o=z, a=ek:
                               e_.tensor_tensor(o, o, a, ALU.mult),
                               tmax(tok_zc, tok_ek))

                sa = scr(4 + ct % 2)
                sb_ = scr(6 + ct % 2)
                tok_sa = sch.op("vector", lambda e_, o=sa, c=ct, d1=z:
                                e_.tensor_tensor_scan(
                                    o, con["w"][:, c:c + 1].broadcast_to(
                                        [128, TC]), d1, car_a[:, c:c + 1],
                                    ALU.mult, ALU.add),
                                tmax(tok_z, tok_w, tok_carry,
                                     scrf.get(("sa", ct % 2))))
                tok_sb = sch.op("vector", lambda e_, o=sb_, c=ct, d1=ek:
                                e_.tensor_tensor_scan(
                                    o, con["w"][:, c:c + 1].broadcast_to(
                                        [128, TC]), d1, car_b[:, c:c + 1],
                                    ALU.mult, ALU.add),
                                tmax(tok_ek, tok_w, tok_carry,
                                     scrf.get(("sb", ct % 2))))
                num = scr(8)
                den = scr(9)
                t_nb = sch.op("vector",
                              lambda e_, o=num, zz=z, s_=sa, c=ct:
                              e_.scalar_tensor_tensor(
                                  o[:, 1:TC], zz[:, 1:TC],
                                  con["eu"][:, c:c + 1], s_[:, 0:TC - 1],
                                  ALU.mult, ALU.add),
                              tmax(tok_sa, tok_eu,
                                   scrf.get("num")))
                t_nt = sch.op("vector",
                              lambda e_, o=num, zz=z, c=ct:
                              e_.scalar_tensor_tensor(
                                  o[:, 0:1], zz[:, 0:1],
                                  con["eu"][:, c:c + 1], car_a[:, c:c + 1],
                                  ALU.mult, ALU.add),
                              tmax(tok_z, tok_eu, tok_carry,
                                   scrf.get("num")))
                t_d1 = sch.op("scalar",
                              lambda e_, o=den, ee=ek, c=ct:
                              e_.mul(o[:, 1:TC], ee[:, 1:TC],
                                     con["eu"][:, c:c + 1]),
                              tmax(tok_ek, tok_eu,
                                   scrf.get("den")))
                t_d2 = sch.op("vector",
                              lambda e_, o=den, s_=sb_:
                              e_.tensor_tensor(o[:, 1:TC], o[:, 1:TC],
                                               s_[:, 0:TC - 1], ALU.add),
                              tmax(t_d1, tok_sb))
                t_dt = sch.op("vector",
                              lambda e_, o=den, ee=ek, c=ct:
                              e_.scalar_tensor_tensor(
                                  o[:, 0:1], ee[:, 0:1],
                                  con["eu"][:, c:c + 1], car_b[:, c:c + 1],
                                  ALU.mult, ALU.add),
                              tmax(tok_ek, tok_eu, tok_carry,
                                   scrf.get("den")))
                t_ca = sch.op("vector", lambda e_, c=ct, s_=sa:
                              e_.tensor_copy(car_a[:, c:c + 1],
                                             s_[:, TC - 1:TC]),
                              tmax(tok_sa, t_nt))
                t_cb = sch.op("vector", lambda e_, c=ct, s_=sb_:
                              e_.tensor_copy(car_b[:, c:c + 1],
                                             s_[:, TC - 1:TC]),
                              tmax(tok_sb, t_dt))
                carry_next.append(tmax(t_ca, t_cb))
                t_l = sch.op("scalar", lambda e_, o=den:
                             e_.activation(o, o, AF.Ln), tmax(t_d2, t_dt))
                t_e = sch.op("scalar", lambda e_, o=den:
                             e_.activation(o, o, AF.Exp, scale=-1.0), t_l)
                wv_ = wkvr[:, ct * TC:(ct + 1) * TC]
                t_wk = sch.op("vector", lambda e_, o=wv_, a=num, b_=den:
                              e_.tensor_tensor(o, a, b_, ALU.mult),
                              tmax(t_nb, t_nt, t_e,
                                   wkv_free if ct == 0 else ()))
                wkvr_toks[ct] = t_wk
                scrf["num"] = tmax(t_wk)
                scrf["den"] = tmax(t_wk)
                scrf[("sa", ct % 2)] = tmax(t_nb, t_ca)
                scrf[("sb", ct % 2)] = tmax(t_d2, t_cb)
                scrf[("ek", ct % 2)] = tmax(t_d1, t_dt, tok_sb, tok_z)
                scrf[("z", ct % 2)] = tmax(t_nb, t_nt, tok_sa)
            tok_carry = tmax(*carry_next)
            kv_mix_done = tmax(*kv_use)
            if full_upto == "kv":
                mix_free = kv_mix_done
                wkv_free = tmax(*[t for t in wkvr_toks if t])
                xbf_free = tmax(*(xx_toks + mk_toks + mv_toks))
                o32_free = tmax()
                ao_free = tmax()
                continue

            mr_toks, mg_toks = [], []
            for cj in range(NT):
                xxs = scr(10).bitcast(BF16)[:, (cj % 2) * TC:
                                            (cj % 2) * TC + TC]
                prev_v = xbf[:, cj * sY:cj * sY + TC]
                x_v = xbf[:, cj * sY + 1:cj * sY + 1 + TC]
                t_xx = sch.op("vector", lambda e_, o=xxs, a=prev_v, b_=x_v:
                              e_.tensor_tensor(o, a, b_, ALU.subtract),
                              tmax(tok_x, scrf.get(("xx", cj % 2)),
                                   scrf.get(("osq", cj % 2))))
                mr = mixA[:, cj * TC:(cj + 1) * TC]
                mg = mixB[:, cj * TC:(cj + 1) * TC]
                t_mr = sch.op("vector", lambda e_, o=mr, a=xxs, c=cj, b_=x_v:
                              e_.scalar_tensor_tensor(
                                  o, a, con["tmr"][:, c:c + 1], b_,
                                  ALU.mult, ALU.add),
                              tmax(t_xx, tok_con,
                                   kv_mix_done if cj == 0 else ()))
                t_mg = sch.op("vector", lambda e_, o=mg, a=xxs, c=cj,
                              b_=x_v:
                              e_.scalar_tensor_tensor(
                                  o, a, con["tmg"][:, c:c + 1], b_,
                                  ALU.mult, ALU.add),
                              tmax(t_xx, tok_con,
                                   kv_mix_done if cj == 0 else ()))
                scrf[("xx", cj % 2)] = tmax(t_mr, t_mg)
                mr_toks.append(t_mr)
                mg_toks.append(t_mg)
            tok_mixr = tmax(*mr_toks)
            tok_mixg = tmax(*mg_toks)
            xbf_free = tmax(*(xx_toks + mr_toks + mg_toks))

            ao_toks = [None] * NT
            rg_use = []
            for ct in range(NT):
                prep_tick()
                sb = next_slab()
                sdst = slabs[sb].rearrange("p (c n) -> p c n", n=256)
                tok_sl = slab_load(
                    sb, sdst,
                    qrg[ct].rearrange("(c p) n -> p c n", p=128),
                    tmax(slab_free[sb], tok_qrg[ct]))
                slab = slabs[sb]
                br = next_bank()
                gr = proj_group(lambda cj: slab[:, cj * 256:cj * 256 + 128],
                                lambda cj: mixA[:, cj * TC:(cj + 1) * TC],
                                br, tmax(tok_sl, bank_free[br]),
                                cj_dep=lambda cj: mr_toks[cj])
                sigr = scr(11)
                t_sr = sch.op("scalar", lambda e_, o=sigr, p=PS[br]:
                              e_.activation(o, p[:], AF.Sigmoid),
                              tmax(gr, scrf.get("sigr")))
                bank_free[br] = tmax(t_sr)
                bg = next_bank()
                gg = proj_group(lambda cj: slab[:, cj * 256 + 128:
                                                cj * 256 + 256],
                                lambda cj: mixB[:, cj * TC:(cj + 1) * TC],
                                bg, tmax(tok_sl, bank_free[bg]),
                                cj_dep=lambda cj: mg_toks[cj])
                slab_free[sb] = tmax(gr, gg)
                rg_use.append(tmax(gr, gg))
                sigg = scr(12)
                t_sg = sch.op("scalar", lambda e_, o=sigg, p=PS[bg]:
                              e_.activation(o, p[:], AF.Sigmoid),
                              tmax(gg, scrf.get("sigg")))
                bank_free[bg] = tmax(t_sg)
                t_p = sch.op("vector", lambda e_, a=sigr, b_=sigg:
                             e_.tensor_tensor(a, a, b_, ALU.mult),
                             tmax(t_sr, t_sg))
                scrf["sigg"] = tmax(t_p)
                aov = aobf[:, ct * TC:(ct + 1) * TC]
                t_ao = sch.op("vector", lambda e_, o=aov, a=sigr, c=ct:
                              e_.tensor_tensor(
                                  o, wkvr[:, c * TC:(c + 1) * TC], a,
                                  ALU.mult),
                              tmax(t_p, wkvr_toks[ct],
                                   ao_free if ct == 0 else ()))
                scrf["sigr"] = tmax(t_ao)
                ao_toks[ct] = t_ao
            tok_ao_all = tmax(*ao_toks)
            rg_mm_done = tmax(*rg_use)

            if full_upto == "rg":
                mix_free = rg_mm_done
                wkv_free = tmax(*[t for t in wkvr_toks if t])
                ao_free = tok_ao_all
                o32_free = tmax()
                continue

            x32src = xw[:, pos0 + 1:pos0 + TC + 1].rearrange(
                "(c p) t -> p c t", p=128)
            tok_x32 = sch.dma("scalar", "x32", lambda e_, s_=x32src:
                              e_.dma_start(out=x32.rearrange(
                                  "p (c t) -> p c t", t=TC), in_=s_),
                              tmax(rg_mm_done, st.get("x32")))
            st["x32"] = tok_x32

            o32_toks = [None] * NT
            tok_statmm = tok_statmq = None
            for ct2 in range(0, NT, 2):
                prep_tick()
                sb = next_slab()
                sdst = slabs[sb].rearrange("p (c n) -> p c n", n=256)
                tok_sl = slab_load(
                    sb, sdst,
                    qwo[:, ct2 * 128:(ct2 + 2) * 128]
                    .rearrange("(c p) n -> p c n", p=128),
                    tmax(slab_free[sb], tok_qwo[ct2 // 2]))
                slab = slabs[sb]
                for ctl in range(2):
                    ct = ct2 + ctl
                    bo = next_bank()
                    go = proj_group(
                        lambda cj, l_=ctl: slab[:, cj * 256 + l_ * 128:
                                                cj * 256 + l_ * 128 + 128],
                        lambda cj: aobf[:, cj * TC:(cj + 1) * TC],
                        bo, tmax(tok_sl, bank_free[bo]),
                        cj_dep=lambda cj: ao_toks[cj])
                    ov = o32[:, ct * TC:(ct + 1) * TC]
                    t_o = sch.op("scalar", lambda e_, o=ov, p=PS[bo]:
                                 e_.copy(o, p[:]),
                                 tmax(go, o32_free if ct == 0 else ()))
                    obv = scr(13).bitcast(BF16)[:, (ct % 2) * TC:
                                                (ct % 2) * TC + TC]
                    if wo_level >= 2:
                        t_ob = sch.op("vector", lambda e_, o=obv, s_=ov:
                                      e_.tensor_copy(o, s_),
                                      tmax(t_o, scrf.get(("obf", ct % 2))))
                    else:
                        t_ob = t_o
                    bank_free[bo] = tmax(t_o)
                    o32_toks[ct] = t_o
                    sqv = scr(10).bitcast(BF16)[:, (ct % 2) * TC:
                                                (ct % 2) * TC + TC]
                    if wo_level >= 3:
                        t_sq = sch.op("scalar", lambda e_, o=sqv, a=obv:
                                      e_.activation(o, a, AF.Square),
                                      tmax(t_ob, scrf.get(("xx", ct % 2)),
                                           scrf.get(("osq", ct % 2))))
                    else:
                        t_sq = t_ob
                    if wo_level >= 4:
                        t_ms = sch.pe(lambda e_, r_=obv, s0=(ct == 0),
                                      s1=(ct == NT - 1):
                                      e_.matmul(PS[4][0:1, 0:TC], ones_bf,
                                                r_, start=s0, stop=s1),
                                      tmax(t_ob, tok_ones,
                                           st["ps4"] if ct == 0 else ()),
                                      mark=True)
                        t_mq = sch.pe(lambda e_, r_=sqv, s0=(ct == 0),
                                      s1=(ct == NT - 1):
                                      e_.matmul(PS[5][0:1, 0:TC], ones_bf,
                                                r_, start=s0, stop=s1),
                                      tmax(t_sq, tok_ones,
                                           st["ps5"] if ct == 0 else ()),
                                      mark=True)
                        scrf[("obf", ct % 2)] = tmax(t_ms)
                        scrf[("osq", ct % 2)] = tmax(t_mq)
                        if ct == NT - 1:
                            tok_statmm, tok_statmq = t_ms, t_mq
                    else:
                        scrf[("obf", ct % 2)] = tmax(t_ob)
                        scrf[("osq", ct % 2)] = tmax(t_sq)
                        tok_statmm = tok_statmq = None
                slab_free[sb] = sch.pemark()
            ao_free = sch.pemark()

            if full_upto == "wo":
                mix_free = tmax(rg_mm_done, tok_x32)
                wkv_free = tmax(*[t for t in wkvr_toks if t])
                o32_free = tmax(*[t for t in o32_toks if t])
                if wo_level >= 4:
                    t_mu = sch.op("vector", lambda e_: e_.tensor_scalar(
                        row_mu, PS[4][0:1, 0:TC], 1.0 / D, None, ALU.mult),
                        tmax(tok_statmm, st["rows"]))
                    t_e2 = sch.op("vector", lambda e_: e_.tensor_scalar(
                        row_e2, PS[5][0:1, 0:TC], 1.0 / D, None, ALU.mult),
                        tmax(tok_statmq, st["rows"]))
                    st["ps4"] = tmax(t_mu)
                    st["ps5"] = tmax(t_e2)
                    st["rows"] = tmax(t_mu, t_e2)
                continue

            def ln_rows(tok_s, tok_q):
                t_mu = sch.op("vector", lambda e_: e_.tensor_scalar(
                    row_mu, PS[4][0:1, 0:TC], 1.0 / D, None, ALU.mult),
                    tmax(tok_s, st["rows"]))
                t_e2 = sch.op("vector", lambda e_: e_.tensor_scalar(
                    row_e2, PS[5][0:1, 0:TC], 1.0 / D, None, ALU.mult),
                    tmax(tok_q, st["rows"]))
                st["ps4"] = tmax(t_mu)
                st["ps5"] = tmax(t_e2)
                t_m2 = sch.op("vector", lambda e_: e_.tensor_tensor(
                    row_rs, row_mu, row_mu, ALU.mult), t_mu)
                t_var = sch.op("vector", lambda e_: e_.scalar_tensor_tensor(
                    row_rs, row_e2, EPS, row_rs, ALU.add, ALU.subtract),
                    tmax(t_e2, t_m2))
                t_ln = sch.op("scalar", lambda e_: e_.activation(
                    row_rs, row_rs, AF.Ln), t_var)
                t_rs = sch.op("scalar", lambda e_: e_.activation(
                    row_rs, row_rs, AF.Exp, scale=-0.5), t_ln)
                t_mr = sch.op("vector", lambda e_: e_.tensor_tensor(
                    row_mr, row_mu, row_rs, ALU.mult), tmax(t_rs, t_mu))
                return t_rs, t_mr, t_mu

            t_rs, t_mr, t_mu0 = ln_rows(tok_statmm, tok_statmq)
            t_p6 = bc_pair(6, row_rs, t_rs, st["bc"][0], 0)
            t_p7 = bc_pair(7, row_mr, t_mr, st["bc"][1], 1)

            bc_toks = [t_p6, t_p7]
            res_toks = [None] * NT
            rb_toks = []
            tts, tqs = [], []
            stat2s = stat2q = None
            tvs = []
            for ct in range(NT):
                prep_tick(1)
                uv = scr(4 + ct % 2)
                t_u = sch.op("scalar", lambda e_, o=uv, c=ct:
                             e_.mul(o, o32[:, c * TC:(c + 1) * TC],
                                    con["lng"][:, c:c + 1]),
                             tmax(o32_toks[ct], tok_con,
                                  scrf.get(("sa", ct % 2))))
                tv = scr(0 + ct % 2)
                t_t = sch.op("vector", lambda e_, o=tv, a=uv:
                             e_.tensor_tensor(o, a, PS[6][:], ALU.mult),
                             tmax(t_p6, t_u, scrf.get(("ek", ct % 2))))
                vv = scr(6 + ct % 2)
                t_v = sch.op("scalar", lambda e_, o=vv, c=ct:
                             e_.mul(o, PS[7][:], con["lng"][:, c:c + 1]),
                             tmax(t_p7, tok_con,
                                  scrf.get(("sb", ct % 2))))
                tvs.append(t_v)
                qv = scr(2 + ct % 2)
                t_q = sch.op("vector", lambda e_, o=qv, a=vv, c=ct:
                             e_.tensor_tensor(
                                 o, a, x32[:, c * TC:(c + 1) * TC],
                                 ALU.subtract),
                             tmax(t_v, tok_x32,
                                  scrf.get(("z", ct % 2))))
                rv = resf[:, ct * TC:(ct + 1) * TC]
                t_res = sch.op("vector", lambda e_, o=rv, a=tv, b_=qv, c=ct:
                               e_.scalar_tensor_tensor(
                                   o, a, con["lnb"][:, c:c + 1], b_,
                                   ALU.add, ALU.subtract),
                               tmax(t_t, t_q, tok_con))
                tts.append(t_t)
                tqs.append(t_q)
                scrf[("ek", ct % 2)] = tmax(t_res)
                scrf[("z", ct % 2)] = tmax(t_res)
                scrf[("sa", ct % 2)] = tmax(t_t)
                scrf[("sb", ct % 2)] = tmax(t_q)
                res_toks[ct] = t_res
                rbv = resbf[:, ct * TC:(ct + 1) * TC]
                t_rb = sch.op("vector", lambda e_, o=rbv, a=rv:
                              e_.tensor_copy(o, a),
                              tmax(t_res, ao_free if ct == 0 else ()))
                sqv = scr(10).bitcast(BF16)[:, (ct % 2) * TC:
                                            (ct % 2) * TC + TC]
                t_sq = sch.op("scalar", lambda e_, o=sqv, a=rbv:
                              e_.activation(o, a, AF.Square),
                              tmax(t_rb, scrf.get(("osq", ct % 2))))
                t_ms = sch.pe(lambda e_, r_=rbv, s0=(ct == 0),
                              s1=(ct == NT - 1):
                              e_.matmul(PS[4][0:1, 0:TC], ones_bf, r_,
                                        start=s0, stop=s1),
                              tmax(t_rb, tok_ones,
                                   st["ps4"] if ct == 0 else ()),
                              mark=True)
                t_mq = sch.pe(lambda e_, r_=sqv, s0=(ct == 0),
                              s1=(ct == NT - 1):
                              e_.matmul(PS[5][0:1, 0:TC], ones_bf, r_,
                                        start=s0, stop=s1),
                              tmax(t_sq, tok_ones,
                                   st["ps5"] if ct == 0 else ()),
                              mark=True)
                scrf[("osq", ct % 2)] = tmax(t_mq)
                rb_toks.append(t_ms)
                if ct == NT - 1:
                    stat2s, stat2q = t_ms, t_mq

            st["bc"][0] = tmax(*tts)
            st["bc"][1] = tmax(*tvs)

            if full_upto == "innerln":
                mix_free = tmax(rg_mm_done, *tqs)
                wkv_free = tmax(*res_toks)
                o32_free = tmax(*[t for t in o32_toks if t])
                ao_free = tmax(*rb_toks)
                st["rows"] = tmax(st["rows"], bc_toks)
                continue

            st["rows"] = tmax(st["rows"], bc_toks)
            t_rs, t_mr, _ = ln_rows(stat2s, stat2q)
            t_b6 = bc_pair(6, row_rs, t_rs, st["bc"][0], 0)
            t_b7 = bc_pair(7, row_mr, t_mr, st["bc"][1], 1)
            st["rows"] = tmax(st["rows"], t_b6, t_b7)
            res3 = resf.rearrange("p (c t) -> p c t", t=TC)
            o323 = o32.rearrange("p (c t) -> p c t", t=TC)
            ps6b = PS[6][:].rearrange("p (a t) -> p a t", a=1).broadcast_to(
                [128, NT, TC])
            ps7b = PS[7][:].rearrange("p (a t) -> p a t", a=1).broadcast_to(
                [128, NT, TC])
            t_t2 = sch.op("vector", lambda e_: e_.tensor_tensor(
                o323, res3, ps6b, ALU.mult),
                tmax(t_b6, *res_toks))
            t_xr = sch.op("vector", lambda e_: e_.tensor_tensor(
                res3, o323, ps7b, ALU.subtract), tmax(t_b7, t_t2))
            st["bc"][0] = tmax(t_t2)
            st["bc"][1] = tmax(t_xr)
            o32_free = tmax(t_xr)
            t_st = sch.dma("scalar", "xrst", lambda e_, o=fci: e_.dma_start(
                out=xres_d[:, o * TC:(o + 1) * TC].rearrange(
                    "(c p) t -> p c t", p=128), in_=res3),
                tmax(t_xr, st.get("xrst")))
            st["xrst"] = t_st
            xres_st.append(t_st)
            wkv_free = tmax(t_st, t_xr)
            ao_free = tmax(*rb_toks)
            mix_free = tmax(rg_mm_done, *tqs)

        tok_attn = tmax(wkv_free, o32_free, ao_free, mix_free, xbf_free,
                        tok_carry, *xres_st)


        if debug_stop == "attn":
            t_dbg = sch.dma("sync", "y", lambda e_: e_.dma_start(
                out=y_out[:, 0:M], in_=xres_d[:, 0:M]),
                tmax(tok_attn, st.get("rows"), st.get("ps4"), st.get("ps5")))
            sch.wait("sync", t_dbg)
            emit_all(block, sch, sems)
            return nc


        xresbf = A_MIX[:, 0:NT * TFF // 2].bitcast(BF16)
        h_ap = [A_WKV[:].bitcast(BF16), A_O32[:, 0:NT * TC].bitcast(BF16)]
        accq_ap = [A_XBF[:, 0:NT * 256], A_AO[:, 0:NT * 256],
                   A_SCR[:, 0:NT * 256], A_SCR[:, NT * 256:2 * NT * 256]]
        xq32s = [A_O32[:, 0:NT * 256], A_O32[:, NT * 256:2 * NT * 256]]
        resbf2 = A_WKV[:].bitcast(BF16)
        if sqq_extra:
            sqq = A_PREP[:, 2 * NT * 64 + NT * 96:].bitcast(BF16)
        else:
            sqq = A_PREP[:, 2 * NT * 64 + NT * 32:
                         2 * NT * 64 + NT * 96].bitcast(BF16)

        xrb_free = tmax(tok_attn)
        h_free = tmax(tok_attn)
        acc_free = [tmax(tok_attn), tmax(tok_attn),
                    tmax(tok_attn), tmax(tok_attn)]
        xq_free = [tmax(pb_free[0], pb_free[1], tok_attn),
                   tmax(pb_free[0], pb_free[1], tok_attn)]
        sq_free = {}
        y_toks = []

        for oc in range(n_out):
            t0 = oc * TFF
            xrb_toks = []
            for qi in range(NQ):
                qb = qi % 2
                src = xres_d[:, t0 + qi * 256:t0 + (qi + 1) * 256].rearrange(
                    "(c p) t -> p c t", p=128)
                tok_l = sch.dma("scalar", "xqa" if qb == 0 else "xqb",
                                lambda e_, s_=src, q=qb:
                                e_.dma_start(out=xq32s[q].rearrange(
                                    "p (c t) -> p c t", t=256), in_=s_),
                                tmax(xq_free[qb],
                                     xres_st[(t0 + qi * 256) // TC]))
                dstv = xresbf.rearrange("p (c t) -> p c t", t=TFF)[
                    :, :, qi * 256:(qi + 1) * 256]
                t_c = sch.op("vector", lambda e_, d=dstv, q=qb:
                             e_.tensor_copy(d, xq32s[q].rearrange(
                                 "p (c t) -> p c t", t=256)),
                             tmax(tok_l, xrb_free if qi == 0 else ()))
                xq_free[qb] = tmax(t_c)
                xrb_toks.append(t_c)
            tok_xrb = tmax(*xrb_toks)

            acc_toks = [None] * 4
            h_tok = {}
            HGOF = (NFG // 2) * TFF

            def mm1_group(g):
                for fts in range(0, NFG, 2):
                    ft0 = g * NFG + fts
                    q_run_to(n_p1 + g * NFG * 2 + (fts + 2) * 2 + 4)
                    sb = next_slab()
                    sdst = slabs[sb].rearrange("p (c n) -> p c n", n=256)
                    tok_sl = slab_load(
                        sb, sdst,
                        wq1[:, ft0 * 128:(ft0 + 2) * 128]
                        .rearrange("(c p) n -> p c n", p=128),
                        tmax(slab_free[sb],
                             *st["q1"][ft0 * 2:ft0 * 2 + 4]))
                    slab = slabs[sb]
                    for ftl in range(2):
                        ft = fts + ftl
                        ftg = ft0 + ftl
                        hxs = NFG // 2
                        for hf in range(NH):
                            bb = next_bank()
                            gm = proj_group(
                                lambda cj, l_=ftl: slab[
                                    :, cj * 256 + l_ * 128:
                                    cj * 256 + l_ * 128 + 128],
                                lambda cj, h=hf: xresbf[
                                    :, cj * TFF + h * 512:
                                    cj * TFF + h * 512 + 512],
                                bb, tmax(tok_sl, xrb_toks[2 * hf],
                                         xrb_toks[2 * hf + 1],
                                         bank_free[bb],
                                         h_free if (fts == 0 and ftl == 0
                                                    and hf == 0 and g == 0)
                                         else ()))
                            hx, hcol = divmod(ft, hxs)
                            hv = h_ap[hx][:, (g % 2) * HGOF + hcol * TFF
                                          + hf * 512:
                                          (g % 2) * HGOF + hcol * TFF
                                          + hf * 512 + 512]
                            t_h = sch.op("scalar",
                                         lambda e_, o=hv, p=PS[bb], f=ftg:
                                         e_.activation(o, p[:], AF.Gelu,
                                                       bias=b1s[:, f:f + 1]),
                                         tmax(gm, tok_con))
                            bank_free[bb] = tmax(t_h)
                            h_tok[(g, ft, hf)] = t_h
                    slab_free[sb] = sch.pemark()

            def mm2_group(g):
                for ct in range(NT):
                    if ct == 0:
                        q_run_to(n_p1 + (g + 1) * NFG * 2)
                    sb = next_slab()
                    sdst = slabs[sb][:, 0:NFG * 128].rearrange(
                        "p (a n) -> p a n", n=128)
                    tok_sl = slab_load(
                        sb, sdst,
                        wq2[ct, g * NFG * 128:(g + 1) * NFG * 128, :]
                        .rearrange("(a p) n -> p a n", p=128),
                        tmax(slab_free[sb], q_front()))
                    slab = slabs[sb]
                    for hf in range(NH):
                        bb = next_bank()
                        for ft in range(NFG):
                            hx, hcol = divmod(ft, NFG // 2)
                            lt = slab[:, ft * 128:(ft + 1) * 128]
                            rh = h_ap[hx][:, (g % 2) * HGOF + hcol * TFF
                                          + hf * 512:
                                          (g % 2) * HGOF + hcol * TFF
                                          + hf * 512 + 512]
                            sch.pe(lambda e_, p=PS[bb], l_=lt, r_=rh,
                                   s0=(ft == 0), s1=(ft == NFG - 1):
                                   e_.matmul(p[:], l_, r_, start=s0,
                                             stop=s1),
                                   tmax(h_tok[(g, ft, hf)],
                                        tmax(tok_sl, bank_free[bb])
                                        if ft == 0 else ()),
                                   mark=(ft == NFG - 1))
                        gmm = sch.pemark()
                        evs = []
                        for h2 in range(2):
                            qi = hf * 2 + h2
                            av = accq_ap[qi][:, ct * 256:(ct + 1) * 256]
                            pv = PS[bb][:, h2 * 256:h2 * 256 + 256]
                            if g == 0:
                                t_a = sch.op("vector",
                                             lambda e_, o=av, p=pv:
                                             e_.tensor_copy(o, p),
                                             tmax(gmm, acc_free[qi]
                                                  if ct == 0 else ()))
                            else:
                                t_a = sch.op("vector",
                                             lambda e_, o=av, p=pv:
                                             e_.tensor_tensor(o, o, p,
                                                              ALU.add),
                                             tmax(gmm, acc_toks[qi]))
                            acc_toks[qi] = tmax(t_a, acc_toks[qi])
                            evs.append(t_a)
                        bank_free[bb] = tmax(*evs)
                    slab_free[sb] = sch.pemark()

            q_run_to(n_p1 + 4 * NFG)
            mm1_group(0)
            for g in range(GF):
                if g + 1 < GF:
                    mm1_group(g + 1)
                mm2_group(g)
            h_free = sch.pemark()

            ep_toks = [[None] * NT for _ in range(4)]
            rb2_toks = [None] * 4
            for qi in range(NQ):
                qb = qi % 2
                src = xres_d[:, t0 + qi * 256:t0 + (qi + 1) * 256].rearrange(
                    "(c p) t -> p c t", p=128)
                tok_l = sch.dma("scalar", "xqa" if qb == 0 else "xqb",
                                lambda e_, s_=src, q=qb:
                                e_.dma_start(out=xq32s[q].rearrange(
                                    "p (c t) -> p c t", t=256), in_=s_),
                                tmax(xq_free[qb], h_free))
                qtoks = []
                for ct in range(NT):
                    av = accq_ap[qi][:, ct * 256:(ct + 1) * 256]
                    xv = xq32s[qb][:, ct * 256:(ct + 1) * 256]
                    t_ep = sch.op("vector",
                                  lambda e_, o=av, x_=xv, c=ct:
                                  e_.scalar_tensor_tensor(
                                      o, o, con["b2"][:, c:c + 1], x_,
                                      ALU.add, ALU.add),
                                  tmax(tok_l, acc_toks[qi], tok_con))
                    ep_toks[qi][ct] = t_ep
                    qtoks.append(t_ep)
                xq_free[qb] = tmax(*qtoks)
                dstv = resbf2[:, 0:NT * TFF].rearrange(
                    "p (c t) -> p c t", t=TFF)[:, :, qi * 256:(qi + 1) * 256]
                rb2_toks[qi] = sch.op("vector",
                                      lambda e_, d=dstv, s_=accq_ap[qi]
                                      .rearrange("p (c t) -> p c t", t=256):
                                      e_.tensor_copy(d, s_),
                                      tmax(qtoks, h_free))

            for hf in range(NH):
                s2s = s2q = None
                for ct in range(NT):
                    rbv = resbf2[:, ct * TFF + hf * 512:
                                 ct * TFF + hf * 512 + 512]
                    sqv = sqq[:, (ct % 2) * 512:(ct % 2) * 512 + 512]
                    t_sq = sch.op("scalar", lambda e_, o=sqv, a=rbv:
                                  e_.activation(o, a, AF.Square),
                                  tmax(rb2_toks[hf * 2],
                                       rb2_toks[hf * 2 + 1],
                                       sq_free.get(ct % 2), q_front()))
                    t_ms = sch.pe(lambda e_, r_=rbv, s0=(ct == 0),
                                  s1=(ct == NT - 1):
                                  e_.matmul(PS[4][0:1, 0:TC], ones_bf, r_,
                                            start=s0, stop=s1),
                                  tmax(rb2_toks[hf * 2],
                                       rb2_toks[hf * 2 + 1], tok_ones,
                                       st["ps4"] if ct == 0 else ()),
                                  mark=True)
                    t_mq = sch.pe(lambda e_, r_=sqv, s0=(ct == 0),
                                  s1=(ct == NT - 1):
                                  e_.matmul(PS[5][0:1, 0:TC], ones_bf, r_,
                                            start=s0, stop=s1),
                                  tmax(t_sq, tok_ones,
                                       st["ps5"] if ct == 0 else ()),
                                  mark=True)
                    sq_free[ct % 2] = tmax(t_mq)
                    if ct == NT - 1:
                        s2s, s2q = t_ms, t_mq
                t_rs, t_mr, _ = ln_rows(s2s, s2q)
                t_b6 = bc_pair(6, row_rs, t_rs, st["bc"][0], 0)
                t_b7 = bc_pair(7, row_mr, t_mr, st["bc"][1], 1)
                st["rows"] = tmax(st["rows"], t_b6, t_b7)
                for h2 in range(2):
                    qi = hf * 2 + h2
                    a3 = accq_ap[qi].rearrange("p (c t) -> p c t", t=256)
                    p6v = PS[6][:, h2 * 256:h2 * 256 + 256].rearrange(
                        "p (a t) -> p a t", a=1).broadcast_to([128, NT, 256])
                    p7v = PS[7][:, h2 * 256:h2 * 256 + 256].rearrange(
                        "p (a t) -> p a t", a=1).broadcast_to([128, NT, 256])
                    t_a1 = sch.op("vector", lambda e_, o=a3, p=p6v:
                                  e_.tensor_tensor(o, o, p, ALU.mult),
                                  tmax(t_b6, *ep_toks[qi], rb2_toks[qi]))
                    t_a2 = sch.op("vector", lambda e_, o=a3, p=p7v:
                                  e_.tensor_tensor(o, o, p, ALU.subtract),
                                  tmax(t_b7, t_a1))
                    st["bc"][0] = tmax(st["bc"][0], t_a1)
                    st["bc"][1] = tmax(st["bc"][1], t_a2)
                    t_y = sch.dma("scalar", f"y{qi}",
                                  lambda e_, q=qi, t_=t0:
                                  e_.dma_start(
                        out=y_out[:, t_ + q * 256:t_ + (q + 1) * 256]
                        .rearrange("(c p) t -> p c t", p=128),
                        in_=accq_ap[q].rearrange("p (c t) -> p c t", t=256)),
                        t_a2)
                    y_toks.append(t_y)
                    acc_free[qi] = tmax(t_y)
            h_free = tmax(h_free, sch.pemark())

        sch.wait("sync", tmax(*y_toks))
        emit_all(block, sch, sems)

    return nc


def emit_all(block, sch, sems):
    @block.sync
    def _(e_):
        sch.run("sync", sems, e_)

    @block.scalar
    def _(e_):
        sch.run("scalar", sems, e_)

    @block.vector
    def _(e_):
        sch.run("vector", sems, e_)

    @block.gpsimd
    def _(e_):
        sch.run("gpsimd", sems, e_)

    @block.tensor
    def _(e_):
        sch.run("tensor", sems, e_)



def _prep_inputs(x, td, tf, tmk, tmv, tmr, tmg, Wr, Wk, Wv, Wg, Wo,
                 ln_g, ln_b, w1, b1, w2, b2, D, T, F):
    NT = D // 128
    NF = F // 128
    B = x.shape[0]

    def vec128(v):
        return np.ascontiguousarray(
            np.asarray(v, np.float32).reshape(NT, 128).T)

    WkT = np.ascontiguousarray(np.asarray(Wk, np.float32).T)
    WvT = np.ascontiguousarray(np.asarray(Wv, np.float32).T)
    WrT = np.ascontiguousarray(np.asarray(Wr, np.float32).T)
    WgT = np.ascontiguousarray(np.asarray(Wg, np.float32).T)
    WoT = np.ascontiguousarray(np.asarray(Wo, np.float32).T)
    wkv_cat = np.empty((NT, D, 256), np.float32)
    wrg_cat = np.empty((NT, D, 256), np.float32)
    for ct in range(NT):
        wkv_cat[ct, :, 0:128] = WkT[:, ct * 128:(ct + 1) * 128]
        wkv_cat[ct, :, 128:256] = WvT[:, ct * 128:(ct + 1) * 128]
        wrg_cat[ct, :, 0:128] = WrT[:, ct * 128:(ct + 1) * 128]
        wrg_cat[ct, :, 128:256] = WgT[:, ct * 128:(ct + 1) * 128]

    common = {
        "wkv": wkv_cat, "wrg": wrg_cat, "wo": WoT,
        "w1t": np.ascontiguousarray(np.asarray(w1, np.float32).T),
        "w2t": np.ascontiguousarray(np.asarray(w2, np.float32).T),
        "tmk": vec128(tmk), "tmv": vec128(tmv), "tmr": vec128(tmr),
        "tmg": vec128(tmg), "dec": vec128(td), "tf": vec128(tf),
        "lng": vec128(ln_g), "lnb": vec128(ln_b), "b2": vec128(b2),
        "b1": np.ascontiguousarray(
            np.asarray(b1, np.float32).reshape(NF, 128).T),
        "lngrow": np.ascontiguousarray(
            np.asarray(ln_g, np.float32).reshape(1, D)),
    }

    xT = np.transpose(np.asarray(x, np.float32), (0, 2, 1))
    in_maps = []
    for core in range(2 * B):
        b = core // 2
        half = core % 2
        xb = xT[b]
        if half == 1:
            win = xb
        else:
            win = np.concatenate([xb[:, T // 2:], xb[:, :T // 2]], axis=1)
            win = np.ascontiguousarray(win)
            win[:, T // 2 - 1] = xb[:, 1]
        wext = np.concatenate([win[:, 1:2], win], axis=1)
        m = {"xw": np.ascontiguousarray(wext),
             "gate": np.full((128, 1), float(half), np.float32)}
        m.update(common)
        in_maps.append(m)
    return in_maps


def kernel(**inp):
    D, T, F = 2048, 4096, 8192
    B = 4
    x = np.asarray(inp["L0_attn_x"], np.float32)
    in_maps = _prep_inputs(
        x, inp["L0_attn_time_decay"], inp["L0_attn_time_first"],
        inp["L0_attn_tm_k"], inp["L0_attn_tm_v"], inp["L0_attn_tm_r"],
        inp["L0_attn_tm_g"], inp["L0_attn_Wr"], inp["L0_attn_Wk"],
        inp["L0_attn_Wv"], inp["L0_attn_Wg"], inp["L0_attn_Wo"],
        inp["L0_attn_ln_g"], inp["L0_attn_ln_b"], inp["L0_ffn_w1"],
        inp["L0_ffn_b1"], inp["L0_ffn_w2"], inp["L0_ffn_b2"], D, T, F)

    nc = build(D=D, T=T, F=F)
    res = run_bass_kernel_spmd(nc, in_maps, core_ids=list(range(8)))

    out = np.empty((B, T, D), np.float32)
    for core in range(8):
        b = core // 2
        half = core % 2
        yc = res.results[core]["y"]
        if half == 0:
            out[b, 0:T // 2, :] = yc.T
        else:
            out[b, T // 2:T, :] = yc.T
    return out
